# version 06
# baseline (speedup 1.0000x reference)
import sys
sys.path.insert(0, "/opt/trn_rl_repo")
import numpy as np
from contextlib import ExitStack
import concourse.bass as bass
import concourse.mybir as mybir

F32 = mybir.dt.float32
BF16 = mybir.dt.bfloat16
AF = mybir.ActivationFunctionType
ALU = mybir.AluOpType
AX = mybir.AxisListType

ENGINES = ("sync", "gpsimd", "scalar", "vector", "tensor")

NTOK = 4096
SEQ = 2048
DM = 1024


class Sem:
    __slots__ = ("name", "val", "h")

    def __init__(self, name):
        self.name = name
        self.val = 0
        self.h = None


class Prog:
    def __init__(self, nc, ctx):
        self.nc = nc
        self.ctx = ctx
        self.ops = {e: [] for e in ENGINES}
        self.sems = []
        self.waited = {e: {} for e in ENGINES}
        self.esem = {}
        for e in ("vector", "scalar"):
            self.esem[e] = self.sem(f"eo_{e}")

    @property
    def sv(self):
        return self.esem["vector"]

    @property
    def sa(self):
        return self.esem["scalar"]

    def sem(self, name):
        s = Sem(name)
        s.h = self.ctx.enter_context(self.nc.semaphore(name))
        self.sems.append(s)
        return s

    def sb(self, name, shape, dtype):
        st = self._scopes[-1] if getattr(self, "_scopes", None) else self.ctx
        return st.enter_context(self.nc.sbuf_tensor(name, shape, dtype))

    def ps(self, name, shape, dtype=F32):
        st = self._scopes[-1] if getattr(self, "_scopes", None) else self.ctx
        return st.enter_context(self.nc.psum_tensor(name, shape, dtype))

    def push_scope(self):
        if not hasattr(self, "_scopes"):
            self._scopes = []
        self._scopes.append(ExitStack())

    def pop_scope(self):
        self._scopes.pop().close()

    def join(self, *sems):
        return [(s, s.val) for s in sems]

    def soft_barrier(self):
        snaps = [(s, s.val) for s in self.sems if s.val > 0]
        for e in ENGINES:
            for s, v in snaps:
                if self.waited[e].get(s.name, -1) >= v:
                    continue
                self.waited[e][s.name] = v

                def run(engobj, _s=s, _v=v):
                    engobj.wait_ge(_s.h, _v)
                self.ops[e].append(run)

    def emit(self, eng, fn, waits=(), inc=None, n=1):
        if eng in self.esem:
            es = self.esem[eng]
            waits = list(waits) + [(es, es.val)]
            if inc is None:
                inc = es
            elif inc is not es:
                raise ValueError("vector/scalar ops must inc their engine sem")
        wl = []
        for w in waits:
            if isinstance(w, Sem):
                w = (w, w.val)
            s, v = w
            if v <= 0:
                continue
            if self.waited[eng].get(s.name, -1) >= v:
                continue
            self.waited[eng][s.name] = v
            wl.append((s, v))
        if inc is not None:
            inc.val += n
        ninc = n

        def run(engobj, _fn=fn, _wl=wl, _inc=inc, _n=ninc):
            for s, v in _wl:
                engobj.wait_ge(s.h, v)
            instr = _fn(engobj)
            if _inc is not None:
                instr.then_inc(_inc.h, _n)
            return instr

        self.ops[eng].append(run)

    def dma(self, queue, out, in_, waits=(), inc=None, slow=False):
        self.emit(queue, lambda e, o=out, i=in_, s=slow: e.dma_start(
            out=o, in_=i, allow_slow_non_contiguous=s),
            waits=waits, inc=inc, n=16)

    def barrier(self, name):
        s = self.sem(name)
        for e in ENGINES:
            self.emit(e, lambda g: g.engine_nop() if hasattr(g, "engine_nop") else g.nop(),
                      waits=(), inc=s, n=1)
        target = s.val
        for e in ENGINES:
            def run(engobj, _s=s, _v=target):
                engobj.wait_ge(_s.h, _v)
            self.ops[e].append(run)
            self.waited[e][s.name] = target

    def finalize(self, block):
        for e in ENGINES:
            ops = self.ops[e]
            if not ops:
                continue
            section = getattr(block, e if e != "sync" else "sync")

            def body(engobj, _ops=ops):
                for fn in _ops:
                    fn(engobj)
            section(body)



def emit_l0_ffn(P, rin_d, rout_d, wr_d, wk_d, wv_d, mur_d, muk_d, ident_d,
                name="f0"):
    nc = P.nc
    CH = 512
    NCH = NTOK // CH
    SUB = CH // 128

    sb, ps = P.sb, P.ps
    rin = [sb(f"{name}_rin{i}", [128, SUB, DM], F32) for i in range(2)]
    x2p = sb(f"{name}_x2p", [128, SUB, DM], BF16)
    xf = sb(f"{name}_xf", [128, 8, CH], BF16)
    xp = sb(f"{name}_xp", [128, 8, CH], BF16)
    carry = [sb(f"{name}_car{i}", [128, 8], BF16) for i in range(2)]
    xr = sb(f"{name}_xr", [128, 8, CH], BF16)
    xk = sb(f"{name}_xk", [128, 8, CH], BF16)
    t1 = sb(f"{name}_t1", [128, CH], BF16)
    ka = sb(f"{name}_ka", [128, 8, CH], BF16)
    rt = sb(f"{name}_rt", [128, SUB, 2, 512], BF16)
    sq = sb(f"{name}_sq", [128, DM], F32)
    stat = sb(f"{name}_stat", [128, 8], F32)
    krelu = sb(f"{name}_krelu", [128, 512], F32)
    vtmp = sb(f"{name}_vtmp", [128, 512], F32)
    wr = sb(f"{name}_wr", [128, 8, DM], BF16)
    wk = sb(f"{name}_wk", [128, 8, DM], BF16)
    wv = sb(f"{name}_wv", [128, 8, DM], BF16)
    mur = sb(f"{name}_mur", [128, 8], F32)
    murc = sb(f"{name}_murc", [128, 8], F32)
    muk = sb(f"{name}_muk", [128, 8], F32)
    mukc = sb(f"{name}_mukc", [128, 8], F32)
    idt = sb(f"{name}_idt", [128, 128], BF16)

    pt = [ps(f"{name}_pt{i}", [128, 128], BF16) for i in range(2)]
    pk = [ps(f"{name}_pk{i}", [128, 512]) for i in range(2)]
    pr = [ps(f"{name}_pr{i}", [128, 512]) for i in range(2)]

    s_ldw = P.sem(f"{name}_ldw")
    s_ldx = [P.sem(f"{name}_ldx{i}") for i in range(2)]
    s_st = [P.sem(f"{name}_st{i}") for i in range(2)]
    s_pe = P.sem(f"{name}_pe")
    s_act = P.sa
    s_dve = P.sv

    P.dma("gpsimd", wr[:], wr_d.rearrange("(c p) o -> p c o", p=128), inc=s_ldw)
    P.dma("gpsimd", wk[:], wk_d.rearrange("(c p) o -> p c o", p=128), inc=s_ldw)
    P.dma("gpsimd", wv[:], wv_d.rearrange("(c p) o -> p c o", p=128), inc=s_ldw)
    P.dma("gpsimd", mur[:], mur_d.rearrange("(c p) -> p c", p=128), inc=s_ldw,
          slow=True)
    P.dma("gpsimd", muk[:], muk_d.rearrange("(c p) -> p c", p=128), inc=s_ldw,
          slow=True)
    P.dma("gpsimd", idt[:], ident_d, inc=s_ldw)
    P.emit("vector", lambda e: e.tensor_scalar(murc[:], mur[:], -1.0, 1.0,
                                               op0=ALU.mult, op1=ALU.add),
           waits=[s_ldw], inc=s_dve)
    P.emit("vector", lambda e: e.tensor_scalar(mukc[:], muk[:], -1.0, 1.0,
                                               op0=ALU.mult, op1=ALU.add),
           waits=[s_ldw], inc=s_dve)
    P.emit("vector", lambda e: e.memset(carry[0][:], 0.0), inc=s_dve)
    P.emit("vector", lambda e: e.memset(carry[1][:], 0.0), inc=s_dve)
    eps = P.sb(f"{name}_eps", [128, 1], F32)
    P.emit("vector", lambda e: e.memset(eps[:], 1e-5), inc=s_dve)

    for ci in range(NCH):
        par = ci % 2
        r_c = rin[par]
        rview = rin_d[ci * CH:(ci + 1) * CH, :].rearrange("(s p) d -> p s d", p=128)
        P.dma("sync", r_c[:], rview,
              waits=[(s_st[par], s_st[par].val)], inc=s_ldx[par])
        ld_done = s_ldx[par].val

        for s in range(SUB):
            row = r_c[:, s, :]
            P.emit("vector", lambda e, row=row: e.reduce_sum(
                stat[:, 0:1], row, axis=AX.X),
                waits=[(s_ldx[par], ld_done)], inc=s_dve)
            P.emit("scalar", lambda e, row=row: e.activation(
                sq[:], row, AF.Square, accum_out=stat[:, 1:2]),
                waits=[(s_ldx[par], ld_done), (s_dve, s_dve.val)], inc=s_act)
            P.emit("vector", lambda e: e.tensor_scalar(
                stat[:, 2:3], stat[:, 0:1], 1.0 / DM, None, op0=ALU.mult),
                waits=[s_dve], inc=s_dve)
            P.emit("vector", lambda e: e.tensor_mul(
                stat[:, 3:4], stat[:, 2:3], stat[:, 2:3]), inc=s_dve)
            P.emit("vector", lambda e: e.tensor_scalar(
                stat[:, 4:5], stat[:, 1:2], 1.0 / DM, None, op0=ALU.mult),
                waits=[s_act], inc=s_dve)
            P.emit("vector", lambda e: e.tensor_sub(
                stat[:, 4:5], stat[:, 4:5], stat[:, 3:4]), inc=s_dve)
            P.emit("scalar", lambda e: e.activation(
                stat[:, 5:6], stat[:, 4:5], AF.Sqrt, bias=eps[:, 0:1]),
                waits=[s_dve], inc=s_act)
            P.emit("vector", lambda e: e.reciprocal(
                stat[:, 6:7], stat[:, 5:6]), waits=[s_act], inc=s_dve)
            P.emit("vector", lambda e, row=row, s=s: e.tensor_scalar(
                x2p[:, s, :], row, stat[:, 2:3], stat[:, 6:7],
                op0=ALU.subtract, op1=ALU.mult),
                waits=[(s_pe, s_pe.val)], inc=s_dve)
            x2_done = s_dve.val
            for c in range(8):
                tp = pt[c % 2]
                P.emit("tensor", lambda e, s=s, c=c, tp=tp: e.transpose(
                    tp[:], x2p[:, s, c * 128:(c + 1) * 128], idt[:]),
                    waits=[(s_dve, x2_done), (s_ldw, 96), (s_act, s_act.val)],
                    inc=s_pe)
                P.emit("scalar", lambda e, s=s, c=c, tp=tp: e.copy(
                    xf[:, c, s * 128:(s + 1) * 128], tp[:]),
                    waits=[s_pe], inc=s_act)
        tr_done = s_act.val

        P.emit("vector", lambda e: e.tensor_copy(xp[:, :, 1:CH], xf[:, :, 0:CH - 1]),
               waits=[(s_act, tr_done)], inc=s_dve)
        P.emit("vector", lambda e, par=par: e.tensor_copy(
            xp[:, :, 0:1], carry[par][:].rearrange("p c -> p c ()")),
            inc=s_dve)
        P.emit("vector", lambda e, par=par: e.tensor_copy(
            carry[par][:].rearrange("p c -> p c ()"), xf[:, :, CH - 1:CH]),
            inc=s_dve)
        if ci == NCH // 2 - 1:
            P.emit("vector", lambda e, par=par: e.memset(carry[(par + 1) % 2][:], 0.0),
                   inc=s_dve)

        for c in range(8):
            P.emit("vector", lambda e, c=c: e.tensor_scalar(
                t1[:], xf[:, c, :], mur[:, c:c + 1], None, op0=ALU.mult),
                inc=s_dve)
            P.emit("vector", lambda e, c=c: e.scalar_tensor_tensor(
                xr[:, c, :], xp[:, c, :], murc[:, c:c + 1], t1[:],
                op0=ALU.mult, op1=ALU.add), inc=s_dve)
            P.emit("vector", lambda e, c=c: e.tensor_scalar(
                t1[:], xf[:, c, :], muk[:, c:c + 1], None, op0=ALU.mult),
                inc=s_dve)
            P.emit("vector", lambda e, c=c: e.scalar_tensor_tensor(
                xk[:, c, :], xp[:, c, :], mukc[:, c:c + 1], t1[:],
                op0=ALU.mult, op1=ALU.add), inc=s_dve)
        mix_done = s_dve.val

        for o in range(8):
            po = pk[o % 2]
            for c in range(8):
                P.emit("tensor", lambda e, o=o, c=c, po=po: e.matmul(
                    po[:], wk[:, c, o * 128:(o + 1) * 128], xk[:, c, :],
                    start=(c == 0), stop=(c == 7)),
                    waits=[(s_dve, s_dve.val)] if c == 0 else (),
                    inc=s_pe if c == 7 else None)
            P.emit("vector", lambda e, po=po: e.tensor_scalar(
                krelu[:], po[:], 0.0, None, op0=ALU.max),
                waits=[s_pe, (s_act, s_act.val)], inc=s_dve)
            P.emit("scalar", lambda e, o=o: e.activation(
                ka[:, o, :], krelu[:], AF.Square), waits=[s_dve], inc=s_act)
        for s in range(SUB):
            for oc in range(2):
                pp = pr[(s * 2 + oc) % 2]
                for c in range(8):
                    P.emit("tensor", lambda e, s=s, oc=oc, c=c, pp=pp: e.matmul(
                        pp[:], xr[:, c, s * 128:(s + 1) * 128],
                        wr[:, c, oc * 512:(oc + 1) * 512],
                        start=(c == 0), stop=(c == 7)),
                        waits=[(s_act, s_act.val)] if c == 0 else (),
                        inc=s_pe if c == 7 else None)
                P.emit("scalar", lambda e, s=s, oc=oc, pp=pp: e.activation(
                    rt[:, s, oc, :], pp[:], AF.Sigmoid),
                    waits=[s_pe, (s_dve, s_dve.val)], inc=s_act)
        ka_done = s_act.val

        for s in range(SUB):
            for oc in range(2):
                pp = pk[(s * 2 + oc) % 2]
                for c in range(8):
                    P.emit("tensor", lambda e, s=s, oc=oc, c=c, pp=pp: e.matmul(
                        pp[:], ka[:, c, s * 128:(s + 1) * 128],
                        wv[:, c, oc * 512:(oc + 1) * 512],
                        start=(c == 0), stop=(c == 7)),
                        waits=[(s_act, ka_done), (s_dve, s_dve.val)] if c == 0 else (),
                        inc=s_pe if c == 7 else None)
                P.emit("vector", lambda e, s=s, oc=oc, pp=pp: e.tensor_mul(
                    vtmp[:], pp[:], rt[:, s, oc, :]),
                    waits=[s_pe, (s_act, s_act.val)], inc=s_dve)
                P.emit("vector", lambda e, s=s, oc=oc, r_c=r_c: e.tensor_add(
                    r_c[:, s, oc * 512:(oc + 1) * 512],
                    r_c[:, s, oc * 512:(oc + 1) * 512], vtmp[:]),
                    inc=s_dve)
        add_done = s_dve.val

        oview = rout_d[ci * CH:(ci + 1) * CH, :].rearrange("(s p) d -> p s d", p=128)
        P.dma("gpsimd", oview, r_c[:], waits=[(s_dve, add_done)], inc=s_st[par])

    fin = P.sb(f"{name}_fin", [1, 8], F32)
    P.emit("gpsimd", lambda e: e.memset(fin[:], 0.0),
           waits=[(s_st[0], s_st[0].val), (s_st[1], s_st[1].val)], inc=None)



def emit_l0_attn(P, rin_d, rout_d, w_inT_d, b_in_d, wconv_d, bconv_d, w_xT_d,
                 b_x_d, w_outT_d, b_out_d, alog_d, dvec_d, ident_d, ones_d,
                 onehot_d, name="a0"):
    nc = P.nc
    IT = 1024
    NIT = NTOK // IT
    TS = 512
    DI = 2048
    NDC = DI // 128
    NN = 16
    sb, ps = P.sb, P.ps
    s_act, s_dve = P.sa, P.sv

    ones1 = sb(f"{name}_ones", [1, 128], BF16)
    idt = sb(f"{name}_idt", [128, 128], BF16)
    oh_t = sb(f"{name}_oh", [16, NN, 128], BF16)
    at_t = sb(f"{name}_A", [128, NDC, NN], F32)
    wct = sb(f"{name}_wc", [128, NDC, 4], F32)
    bct = sb(f"{name}_bc", [128, NDC], F32)
    dvt = sb(f"{name}_dv", [128, NDC], F32)
    bint = sb(f"{name}_bin", [128, 32], F32)
    bxdt = sb(f"{name}_bxd", [128, NDC], F32)
    bxB = sb(f"{name}_bxB", [16, 1], F32)
    bxC = sb(f"{name}_bxC", [16, 1], F32)
    boutr = sb(f"{name}_bout", [1, DM], BF16)
    wxBC = sb(f"{name}_wxbc", [128, 16, 32], BF16)
    eps = sb(f"{name}_eps", [128, 1], F32)
    hcar = sb(f"{name}_hcar", [128, NDC * NN], F32)
    xh = sb(f"{name}_xh", [128, NDC, 4], F32)

    s_cst = P.sem(f"{name}_cst")
    s_ldx = [P.sem(f"{name}_ldx{i}") for i in range(2)]
    s_rst = [P.sem(f"{name}_rst{i}") for i in range(2)]
    s_win = [P.sem(f"{name}_win{i}") for i in range(2)]
    s_wx = [P.sem(f"{name}_wx{i}") for i in range(2)]
    s_wr = [P.sem(f"{name}_wr{i}") for i in range(2)]
    s_wo = P.sem(f"{name}_wo")
    s_wst = [P.sem(f"{name}_wst{i}") for i in range(2)]
    s_pe = P.sem(f"{name}_pe")
    s_pool = P.sem(f"{name}_pool")

    P.dma("gpsimd", ones1[:], ones_d, inc=s_cst)
    P.dma("gpsimd", idt[:], ident_d, inc=s_cst)
    P.dma("gpsimd", oh_t[:], onehot_d, inc=s_cst)
    P.dma("gpsimd", wct[:], wconv_d.rearrange("(c p) k -> p c k", p=128),
          inc=s_cst, slow=True)
    P.dma("gpsimd", bct[:], bconv_d.rearrange("(c p) -> p c", p=128),
          inc=s_cst, slow=True)
    P.dma("gpsimd", dvt[:], dvec_d.rearrange("(c p) -> p c", p=128),
          inc=s_cst, slow=True)
    P.dma("gpsimd", bint[:], b_in_d.rearrange("(c p) -> p c", p=128),
          inc=s_cst, slow=True)
    P.dma("gpsimd", bxdt[:], b_x_d[0:DI].rearrange("(c p) -> p c", p=128),
          inc=s_cst, slow=True)
    P.dma("gpsimd", bxB[:], b_x_d[DI:DI + 16].rearrange("(p) -> p ()"),
          inc=s_cst, slow=True)
    P.dma("gpsimd", bxC[:], b_x_d[DI + 16:DI + 32].rearrange("(p) -> p ()"),
          inc=s_cst, slow=True)
    P.dma("gpsimd", boutr[:], b_out_d.rearrange("(o d) -> o d", o=1), inc=s_cst)
    P.dma("gpsimd", wxBC[:], w_xT_d[:, DI:DI + 32].rearrange(
        "(c p) b -> p c b", p=128), inc=s_cst)
    P.push_scope()
    alt = sb(f"{name}_alt", [128, NDC, NN], F32)
    P.dma("gpsimd", alt[:], alog_d.rearrange("(c p) n -> p c n", p=128),
          inc=s_cst)
    cst_all = s_cst.val
    P.emit("scalar", lambda e: e.activation(at_t[:], alt[:], AF.Exp),
           waits=[(s_cst, cst_all)])
    P.emit("vector", lambda e: e.tensor_scalar(
        at_t[:], at_t[:], -1.0, None, op0=ALU.mult), waits=[s_act])
    a_scal_done = P.sa.val
    a_vec_done = P.sv.val
    P.pop_scope()
    P.emit("vector", lambda e: e.memset(eps[:], 1e-5))

    rin_t = [sb(f"{name}_rin{i}", [128, DM], F32) for i in range(2)]
    x1p = [sb(f"{name}_x1p{i}", [128, DM], BF16) for i in range(2)]
    xfm = sb(f"{name}_xfm", [128, 8, IT], BF16)
    stat = sb(f"{name}_stat", [128, 8], F32)
    win_t = [sb(f"{name}_win{i}", [128, 8, 128], BF16) for i in range(2)]
    wx_t = [sb(f"{name}_wxt{i}", [128, 16, 128], BF16) for i in range(2)]
    wr_t = [sb(f"{name}_wrt{i}", [128, 8, 128], BF16) for i in range(2)]
    wo_t = sb(f"{name}_wot", [128, 16, 512], BF16)
    xi_pre = sb(f"{name}_xipre", [128, 4 + IT], BF16)
    cacc = sb(f"{name}_cacc", [128, IT], F32)
    sq = cacc
    xic = sb(f"{name}_xic", [128, NDC, IT], BF16)
    sres_t = sb(f"{name}_srest", [128, TS], BF16)
    B_sb = sb(f"{name}_Bsb", [16, TS], BF16)
    C_sb = sb(f"{name}_Csb", [16, TS], BF16)
    cbc = sb(f"{name}_cbc", [128, NN, TS], BF16)
    bbc = sb(f"{name}_bbc", [128, NN, TS], BF16)
    d_t = [cacc[:, 0:TS], cacc[:, TS:2 * TS]]
    u_t = sb(f"{name}_ut", [128, TS], BF16)
    sp_t = sb(f"{name}_spt", [128, TS], F32)
    dA_t = [sb(f"{name}_dA{i}", [128, TS], F32) for i in range(2)]
    dbxa = sb(f"{name}_dbxa", [128, NN, TS], BF16)
    h_t = sb(f"{name}_h", [128, NN, TS], BF16)
    y_t = sp_t
    y2_t = sb(f"{name}_y2", [128, TS], F32)
    y3 = sb(f"{name}_y3", [128, NDC, TS], BF16)
    wt2 = [sb(f"{name}_wt2{i}", [128, 512], F32) for i in range(2)]

    pIN = [ps(f"{name}_pIN{i}", [128, 512]) for i in range(2)]
    pDX = [ps(f"{name}_pDX{i}", [128, 512]) for i in range(2)]
    pBB = [ps(f"{name}_pBB{i}", [128, TS]) for i in range(2)]
    pt = [ps(f"{name}_pt{i}", [128, 128], BF16) for i in range(2)]

    rin_free = [[(s_cst, cst_all), (s_act, a_scal_done), (s_dve, a_vec_done)]
                for _ in range(2)]
    for it in range(NIT):
        seq_start = (it % 2 == 0)
        t0 = it * IT
        if seq_start:
            P.emit("vector", lambda e: e.memset(hcar[:], 0.0))
            P.emit("vector", lambda e: e.memset(xh[:], 0.0))

        for s in range(8):
            par = s % 2
            r_c = rin_t[par]
            rview = rin_d[t0 + s * 128:t0 + (s + 1) * 128, :]
            P.dma("sync", r_c[:], rview,
                  waits=[(s_rst[par], s_rst[par].val)] + rin_free[par],
                  inc=s_ldx[par])
            ld_done = s_ldx[par].val
            oview = rout_d[t0 + s * 128:t0 + (s + 1) * 128, :]
            P.dma("gpsimd", oview, r_c[:], waits=[(s_ldx[par], ld_done)],
                  inc=s_rst[par])
            P.emit("vector", lambda e, r_c=r_c: e.reduce_sum(
                stat[:, 0:1], r_c[:], axis=AX.X), waits=[(s_ldx[par], ld_done)])
            P.emit("scalar", lambda e, r_c=r_c: e.activation(
                sq[:], r_c[:], AF.Square, accum_out=stat[:, 1:2]),
                waits=[(s_ldx[par], ld_done), (s_dve, s_dve.val)])
            P.emit("vector", lambda e: e.tensor_scalar(
                stat[:, 2:3], stat[:, 0:1], 1.0 / DM, None, op0=ALU.mult))
            P.emit("vector", lambda e: e.tensor_mul(
                stat[:, 3:4], stat[:, 2:3], stat[:, 2:3]))
            P.emit("vector", lambda e: e.tensor_scalar(
                stat[:, 4:5], stat[:, 1:2], 1.0 / DM, None, op0=ALU.mult),
                waits=[s_act])
            P.emit("vector", lambda e: e.tensor_sub(
                stat[:, 4:5], stat[:, 4:5], stat[:, 3:4]))
            P.emit("scalar", lambda e: e.activation(
                stat[:, 5:6], stat[:, 4:5], AF.Sqrt, bias=eps[:, 0:1]),
                waits=[s_dve])
            P.emit("vector", lambda e: e.reciprocal(
                stat[:, 6:7], stat[:, 5:6]), waits=[s_act])
            P.emit("vector", lambda e, r_c=r_c, par=par: e.tensor_scalar(
                x1p[par][:], r_c[:], stat[:, 2:3], stat[:, 6:7],
                op0=ALU.subtract, op1=ALU.mult),
                waits=[(s_pe, s_pe.val)])
            x1_done = s_dve.val
            rin_free[par] = [(s_dve, x1_done), (s_act, s_act.val)]
            for c in range(8):
                tp = pt[c % 2]
                P.emit("tensor", lambda e, par=par, c=c, tp=tp: e.transpose(
                    tp[:], x1p[par][:, c * 128:(c + 1) * 128], idt[:]),
                    waits=[(s_dve, x1_done), (s_cst, cst_all),
                           (s_act, s_act.val)],
                    inc=s_pe)
                P.emit("scalar", lambda e, s=s, c=c, tp=tp: e.copy(
                    xfm[:, c, s * 128:(s + 1) * 128], tp[:]),
                    waits=[s_pe])
        tr_done = s_act.val

        win_used = [0, 0]
        for dc in range(16):
            wpar = dc % 2
            P.dma("gpsimd", win_t[wpar][:],
                  w_inT_d[:, dc * 128:(dc + 1) * 128].rearrange(
                      "(c p) o -> p c o", p=128),
                  waits=[(s_pe, win_used[wpar])], inc=s_win[wpar])
            win_ok = s_win[wpar].val
            for th in range(2):
                pp = pIN[th]
                for c in range(8):
                    P.emit("tensor", lambda e, th=th, c=c, pp=pp, wpar=wpar:
                           e.matmul(pp[:], win_t[wpar][:, c, :],
                                    xfm[:, c, th * 512:(th + 1) * 512],
                                    start=(c == 0), stop=(c == 7)),
                           waits=[(s_win[wpar], win_ok), (s_act, s_act.val),
                                  (s_dve, s_dve.val)]
                           if c == 0 else (),
                           inc=s_pe if c == 7 else None)
            win_used[wpar] = s_pe.val
            P.emit("vector", lambda e, dc=dc: e.tensor_copy(
                xi_pre[:, 1:4], xh[:, dc, 1:4]), waits=[(s_act, s_act.val)])
            for th in range(2):
                P.emit("scalar", lambda e, th=th, dc=dc: e.activation(
                    xi_pre[:, 4 + th * 512:4 + (th + 1) * 512],
                    pIN[th][:], AF.Identity, bias=bint[:, dc:dc + 1]),
                    waits=[s_pe, (s_dve, s_dve.val)])
            P.emit("vector", lambda e, dc=dc: e.tensor_scalar(
                cacc[:], xi_pre[:, 1:1 + IT], wct[:, dc, 0:1], None,
                op0=ALU.mult), waits=[s_act])
            for k in range(1, 4):
                P.emit("vector", lambda e, dc=dc, k=k: e.scalar_tensor_tensor(
                    cacc[:], xi_pre[:, 1 + k:1 + k + IT],
                    wct[:, dc, k:k + 1], cacc[:],
                    op0=ALU.mult, op1=ALU.add))
            P.emit("scalar", lambda e, dc=dc: e.activation(
                xic[:, dc, :], cacc[:], AF.Silu, bias=bct[:, dc:dc + 1]),
                waits=[s_dve])
            P.emit("vector", lambda e, dc=dc: e.tensor_copy(
                xh[:, dc, 1:4], xi_pre[:, 1 + IT:4 + IT]))

        pdx_free = [[], []]
        pbb_free = [[], []]
        for su in range(2):
            ts0 = su * TS
            for c in range(16):
                P.emit("tensor", lambda e, c=c: e.matmul(
                    pBB[0][0:16, :], wxBC[:, c, 0:16], xic[:, c, ts0:ts0 + TS],
                    start=(c == 0), stop=(c == 15)),
                    waits=[(s_act, s_act.val)] + pbb_free[0]
                    if c == 0 else (),
                    inc=s_pe if c == 15 else None)
            for c in range(16):
                P.emit("tensor", lambda e, c=c: e.matmul(
                    pBB[1][0:16, :], wxBC[:, c, 16:32], xic[:, c, ts0:ts0 + TS],
                    start=(c == 0), stop=(c == 15)),
                    waits=pbb_free[1] if c == 0 else (),
                    inc=s_pe if c == 15 else None)
            P.emit("vector", lambda e: e.tensor_scalar(
                B_sb[:], pBB[0][0:16, :], bxB[:, 0:1], None, op0=ALU.add),
                waits=[s_pe])
            pbb_free[0] = [(s_dve, s_dve.val)]
            P.emit("vector", lambda e: e.tensor_scalar(
                C_sb[:], pBB[1][0:16, :], bxC[:, 0:1], None, op0=ALU.add))
            pbb_free[1] = [(s_dve, s_dve.val)]
            bc_done = s_dve.val
            for n in range(NN):
                pp = pBB[n % 2]
                P.emit("tensor", lambda e, n=n, pp=pp: e.matmul(
                    pp[:], oh_t[:, n, :], C_sb[:], start=True, stop=True),
                    waits=[(s_dve, bc_done)] + pbb_free[n % 2], inc=s_pe)
                P.emit("scalar", lambda e, n=n, pp=pp: e.copy(
                    cbc[:, n, :], pp[:]),
                    waits=[s_pe, (s_dve, s_dve.val), (s_pool, s_pool.val)])
                pbb_free[n % 2] = [(s_act, s_act.val)]
            for n in range(NN):
                pp = pBB[n % 2]
                P.emit("tensor", lambda e, n=n, pp=pp: e.matmul(
                    pp[:], oh_t[:, n, :], B_sb[:], start=True, stop=True),
                    waits=pbb_free[n % 2], inc=s_pe)
                P.emit("scalar", lambda e, n=n, pp=pp: e.copy(
                    bbc[:, n, :], pp[:]), waits=[s_pe, (s_dve, s_dve.val)])
                pbb_free[n % 2] = [(s_act, s_act.val)]
            cbc_done = s_act.val

            wx_used = [0, 0]
            wr_used = [0, 0]
            wx_ok = [0, 0]
            wr_ok = [0, 0]
            P.dma("gpsimd", wx_t[0][:],
                  w_xT_d[:, 0:128].rearrange("(c p) o -> p c o", p=128),
                  waits=[(s_pe, wx_used[0])], inc=s_wx[0])
            wx_ok[0] = s_wx[0].val
            P.dma("gpsimd", wr_t[0][:],
                  w_inT_d[:, 16 * 128:17 * 128].rearrange(
                      "(c p) o -> p c o", p=128),
                  waits=[(s_pe, wr_used[0])], inc=s_wr[0])
            wr_ok[0] = s_wr[0].val
            for dc in range(NDC):
                wpar = dc % 2
                if dc + 1 < NDC:
                    np_ = (dc + 1) % 2
                    P.dma("gpsimd", wx_t[np_][:],
                          w_xT_d[:, (dc + 1) * 128:(dc + 2) * 128].rearrange(
                              "(c p) o -> p c o", p=128),
                          waits=[(s_pe, wx_used[np_])], inc=s_wx[np_])
                    wx_ok[np_] = s_wx[np_].val
                    P.dma("gpsimd", wr_t[np_][:],
                          w_inT_d[:, (17 + dc) * 128:(18 + dc) * 128].rearrange(
                              "(c p) o -> p c o", p=128),
                          waits=[(s_pe, wr_used[np_])], inc=s_wr[np_])
                    wr_ok[np_] = s_wr[np_].val
                pp = pDX[wpar]
                for c in range(16):
                    P.emit("tensor", lambda e, c=c, pp=pp, wpar=wpar:
                           e.matmul(pp[:], wx_t[wpar][:, c, :],
                                    xic[:, c, ts0:ts0 + TS],
                                    start=(c == 0), stop=(c == 15)),
                           waits=[(s_wx[wpar], wx_ok[wpar])] + pdx_free[wpar]
                           if c == 0 else (),
                           inc=s_pe if c == 15 else None)
                wx_used[wpar] = s_pe.val
                dt_c = d_t[wpar]
                P.emit("vector", lambda e, dc=dc, pp=pp: e.tensor_scalar(
                    sp_t[:], pp[:], bxdt[:, dc:dc + 1], None,
                    op0=ALU.add), waits=[s_pe, (s_act, s_act.val)])
                pdx_free[wpar] = [(s_dve, s_dve.val)]
                P.emit("vector", lambda e: e.scalar_tensor_tensor(
                    y2_t[:], sp_t[:], -1.0, sp_t[:],
                    op0=ALU.mult, op1=ALU.max))
                P.emit("scalar", lambda e: e.activation(
                    y2_t[:], y2_t[:], AF.Exp, scale=-1.0),
                    waits=[s_dve])
                P.emit("scalar", lambda e: e.activation(
                    y2_t[:], y2_t[:], AF.Ln, bias=1.0))
                P.emit("vector", lambda e, dt_c=dt_c: e.scalar_tensor_tensor(
                    dt_c, sp_t[:], 0.0, y2_t[:],
                    op0=ALU.max, op1=ALU.add), waits=[s_act])
                P.emit("vector", lambda e, dc=dc, dt_c=dt_c:
                       e.tensor_mul(u_t[:], dt_c, xic[:, dc, ts0:ts0 + TS]))
                P.emit("vector", lambda e: e.tensor_mul(
                    dbxa[:], u_t[:, None, :].broadcast_to([128, NN, TS]),
                    bbc[:]), waits=[(s_act, cbc_done)])
                dbx_done = s_dve.val
                for n in range(NN):
                    npar = n % 2
                    P.emit("scalar", lambda e, n=n, dc=dc, dt_c=dt_c, npar=npar:
                           e.activation(dA_t[npar][:], dt_c, AF.Exp,
                                        scale=at_t[:, dc, n:n + 1]),
                           waits=[(s_dve, max(dbx_done, s_dve.val - 1))])
                    P.emit("vector", lambda e, n=n, npar=npar, dc=dc:
                           e.tensor_tensor_scan(
                               h_t[:, n, :], dA_t[npar][:], dbxa[:, n, :],
                               hcar[:, dc * NN + n:dc * NN + n + 1],
                               op0=ALU.mult, op1=ALU.add),
                           waits=[s_act, (s_pool, s_pool.val)])
                    P.emit("gpsimd", lambda e, n=n: e.tensor_mul(
                        dbxa[:, n, :], h_t[:, n, :], cbc[:, n, :]),
                        waits=[(s_dve, s_dve.val), (s_act, cbc_done),
                               (s_pool, s_pool.val)], inc=s_pool)
                P.emit("vector", lambda e, dc=dc: e.tensor_copy(
                    hcar[:, dc * NN:(dc + 1) * NN],
                    h_t[:, :, TS - 1]))
                P.emit("vector", lambda e: e.tensor_reduce(
                    y_t[:, :, None], dbxa[:].rearrange("p n t -> p t n"),
                    axis=AX.X, op=ALU.add),
                    waits=[(s_pool, s_pool.val)])
                P.emit("vector", lambda e, dc=dc, ts0=ts0: e.scalar_tensor_tensor(
                    y2_t[:], xic[:, dc, ts0:ts0 + TS], dvt[:, dc:dc + 1],
                    y_t[:], op0=ALU.mult, op1=ALU.add))
                pr2 = pIN[wpar]
                for c in range(8):
                    P.emit("tensor", lambda e, c=c, pr2=pr2, wpar=wpar, ts0=ts0:
                           e.matmul(pr2[:], wr_t[wpar][:, c, :],
                                    xfm[:, c, ts0:ts0 + TS],
                                    start=(c == 0), stop=(c == 7)),
                           waits=[(s_wr[wpar], wr_ok[wpar]), (s_act, s_act.val),
                                  (s_dve, s_dve.val)]
                           if c == 0 else (),
                           inc=s_pe if c == 7 else None)
                wr_used[wpar] = s_pe.val
                P.emit("scalar", lambda e, dc=dc, pr2=pr2: e.activation(
                    sres_t[:], pr2[:], AF.Silu, bias=bint[:, 16 + dc:17 + dc]),
                    waits=[s_pe, (s_dve, s_dve.val)])
                P.emit("vector", lambda e, dc=dc: e.tensor_mul(
                    y3[:, dc, :], y2_t[:], sres_t[:]),
                    waits=[(s_pe, s_pe.val), (s_act, s_act.val)])
            y3_done = s_dve.val

            for ocol in range(2):
                P.dma("gpsimd", wo_t[:],
                      w_outT_d[:, ocol * 512:(ocol + 1) * 512].rearrange(
                          "(c p) o -> p c o", p=128),
                      waits=[(s_pe, s_pe.val)], inc=s_wo)
                wo_ok = s_wo.val
                for tsu in range(4):
                    pp = pIN[tsu % 2]
                    for c in range(16):
                        P.emit("tensor",
                               lambda e, tsu=tsu, c=c, pp=pp:
                               e.matmul(pp[:], y3[:, c, tsu * 128:(tsu + 1) * 128],
                                        wo_t[:, c, :],
                                        start=(c == 0), stop=False),
                               waits=[(s_dve, y3_done), (s_wo, wo_ok),
                                      (s_act, s_act.val),
                                      (s_dve, s_dve.val)]
                               if c == 0 else ())
                    P.emit("tensor", lambda e, ocol=ocol, pp=pp: e.matmul(
                        pp[:], ones1[:], boutr[0:1, ocol * 512:(ocol + 1) * 512],
                        start=False, stop=True),
                        waits=[(s_cst, cst_all)], inc=s_pe)
                    wp = tsu % 2
                    P.emit("vector", lambda e, pp=pp, wp=wp: e.tensor_copy(
                        wt2[wp][:], pp[:]),
                        waits=[s_pe, (s_wst[wp], s_wst[wp].val)])
                    r0 = t0 + su * TS + tsu * 128
                    P.emit("gpsimd", lambda e, wp=wp, r0=r0, ocol=ocol:
                           e.dma_start(
                               out=rout_d[r0:r0 + 128,
                                          ocol * 512:(ocol + 1) * 512],
                               in_=wt2[wp][:], accum_op=ALU.add),
                           waits=[(s_dve, s_dve.val),
                                  (s_rst[0], s_rst[0].val),
                                  (s_rst[1], s_rst[1].val)],
                           inc=s_wst[wp], n=16)

    fin = sb(f"{name}_fin", [1, 8], F32)
    P.emit("gpsimd", lambda e: e.memset(fin[:], 0.0),
           waits=[(s_wst[0], s_wst[0].val), (s_wst[1], s_wst[1].val),
                  (s_rst[0], s_rst[0].val), (s_rst[1], s_rst[1].val)])


def build_l0_attn_program(dbg_stop=None):
    nc = bass.Bass("TRN2", target_bir_lowering=False)
    rin_d = nc.dram_tensor("rin", [NTOK, DM], F32, kind="ExternalInput").ap()
    w_inT_d = nc.dram_tensor("winT", [DM, 4096], F32, kind="ExternalInput").ap()
    b_in_d = nc.dram_tensor("bin", [4096], F32, kind="ExternalInput").ap()
    wconv_d = nc.dram_tensor("wconv", [2048, 4], F32, kind="ExternalInput").ap()
    bconv_d = nc.dram_tensor("bconv", [2048], F32, kind="ExternalInput").ap()
    w_xT_d = nc.dram_tensor("wxT", [2048, 2080], F32, kind="ExternalInput").ap()
    b_x_d = nc.dram_tensor("bx", [2080], F32, kind="ExternalInput").ap()
    w_outT_d = nc.dram_tensor("woutT", [2048, DM], F32, kind="ExternalInput").ap()
    b_out_d = nc.dram_tensor("bout", [DM], F32, kind="ExternalInput").ap()
    alog_d = nc.dram_tensor("alog", [2048, 16], F32, kind="ExternalInput").ap()
    dvec_d = nc.dram_tensor("dvec", [2048], F32, kind="ExternalInput").ap()
    ident_d = nc.dram_tensor("ident", [128, 128], F32, kind="ExternalInput").ap()
    ones_d = nc.dram_tensor("ones", [1, 128], F32, kind="ExternalInput").ap()
    onehot_d = nc.dram_tensor("onehot", [16, 16, 128], F32,
                              kind="ExternalInput").ap()
    rout_d = nc.dram_tensor("rout", [NTOK, DM], F32, kind="ExternalOutput").ap()

    with ExitStack() as ctx:
        P = Prog(nc, ctx)
        block = ctx.enter_context(nc.Block())
        emit_l0_attn(P, rin_d, rout_d, w_inT_d, b_in_d, wconv_d, bconv_d,
                     w_xT_d, b_x_d, w_outT_d, b_out_d, alog_d, dvec_d,
                     ident_d, ones_d, onehot_d)
        P.finalize(block)
    return nc



def poolsel_chunks(sb):
    out = {}

    def add(chunk, row_in_chunk, col, w, span):
        m = out.setdefault(chunk, np.zeros((128, 40), np.float32))
        m[row_in_chunk:row_in_chunk + span, col] = w
    for r in range(16):
        pl = 16 * sb - 8 + r
        if pl < 0:
            continue
        add(pl * 8 // 128, (pl * 8) % 128, r, 1 / 8, 8)
    for r in range(8):
        pl = 8 * (sb - 1) + r
        if pl < 0:
            continue
        add(pl * 16 // 128, (pl * 16) % 128, 16 + r, 1 / 16, 16)
    for r in range(8):
        pl = 8 * (sb // 2 - 1) + r
        if pl < 0:
            continue
        add(pl * 32 // 128, (pl * 32) % 128, 24 + r, 1 / 32, 32)
    for r in range(8):
        pl = 8 * (sb // 4 - 1) + r
        if pl < 0:
            continue
        c0 = pl * 64 // 128
        add(c0, (pl * 64) % 128, 32 + r, 1 / 64, 64)
    return out


def build_l1_attn_consts():
    slots = []
    for sb in range(16):
        slots.append(sorted(poolsel_chunks(sb).items()))
    nslot = max(len(s) for s in slots)
    psel = np.zeros((16, nslot, 128, 40), np.float32)
    for sb, items in enumerate(slots):
        for i, (cu, m) in enumerate(items):
            psel[sb, i] = m
    mask = np.full((128, 232), -1e30, np.float32)
    for q in range(128):
        mask[q, q + 1:q + 65] = 0.0
        if q < 64:
            mask[q, 192:200] = np.log(8.0)
        else:
            mask[q, 200:208] = np.log(8.0)
    mask[:, 208:216] = np.log(16.0)
    mask[:, 216:224] = np.log(32.0)
    mask[:, 224:232] = np.log(64.0)
    return psel, mask


def emit_l1_attn(P, rin_d, rout_d, watn_d, batn_d, wproj_d, bproj_d, ident_d,
                 ones_d, psel_d, mask_d, nslot, name="t1"):
    nc = P.nc
    sb_, ps = P.sb, P.ps
    s_act, s_dve = P.sa, P.sv
    DH = 64

    ones1 = sb_(f"{name}_ones", [1, 128], BF16)
    idt = sb_(f"{name}_idt", [128, 128], BF16)
    mask = sb_(f"{name}_mask", [128, 232], F32)
    psel = sb_(f"{name}_psel", [128, 16, nslot, 40], BF16)
    batn = sb_(f"{name}_batn", [128, 24], F32)
    batq = sb_(f"{name}_batq", [128, 8], F32)
    bprr = sb_(f"{name}_bpr", [1, DM], BF16)
    eps = sb_(f"{name}_eps", [128, 1], F32)

    qt = sb_(f"{name}_qt", [128, 8, SEQ], BF16)
    kt = sb_(f"{name}_kt", [128, 8, 64 + SEQ + 64], BF16)
    vtm = sb_(f"{name}_vtm", [128, 8, 16, 128], BF16)
    kp = sb_(f"{name}_kp", [128, 8, 512], BF16)
    otm = sb_(f"{name}_otm", [128, 16, DM], BF16)
    wat = [sb_(f"{name}_wat{i}", [128, 8, 128], BF16) for i in range(2)]
    wpj = sb_(f"{name}_wpj", [128, 8, DM], BF16)
    rin2 = [sb_(f"{name}_rin{i}", [128, DM], F32) for i in range(2)]
    x3p = [sb_(f"{name}_x3p{i}", [128, DM], BF16) for i in range(2)]
    x3fm = sb_(f"{name}_x3fm", [128, 8, 512], BF16)
    sq = sb_(f"{name}_sq", [128, DM], BF16)
    stat = sb_(f"{name}_stat", [128, 8], F32)
    vfm = sb_(f"{name}_vfm", [128, 512], BF16)
    prt = sb_(f"{name}_prt", [128, 256], F32)
    Sm = [sb_(f"{name}_Sm{i}", [128, 232], F32) for i in range(2)]
    Pp = [sb_(f"{name}_Pp{i}", [128, 232], BF16) for i in range(2)]
    pta = [sb_(f"{name}_pta{i}", [64, 128], BF16) for i in range(2)]
    ptb = [sb_(f"{name}_ptb{i}", [64, 128], BF16) for i in range(2)]
    ptc = [sb_(f"{name}_ptc{i}", [64, 128], BF16) for i in range(2)]
    pt3s = [sb_(f"{name}_pt3s{i}", [40, 128], BF16) for i in range(2)]
    vca = [sb_(f"{name}_vca{i}", [64, 64], BF16) for i in range(2)]
    vcb = [sb_(f"{name}_vcb{i}", [64, 64], BF16) for i in range(2)]
    vps = [sb_(f"{name}_vps{i}", [40, 64], BF16) for i in range(2)]
    ofm = sb_(f"{name}_ofm", [128, 8, 128], BF16)
    wt2 = [sb_(f"{name}_wt2{i}", [128, 512], F32) for i in range(2)]

    pS = [ps(f"{name}_pS{i}", [128, 512]) for i in range(2)]
    pO = [ps(f"{name}_pO{i}", [128, 512]) for i in range(2)]
    pPool = [ps(f"{name}_pPool{i}", [40, 64]) for i in range(2)]
    pT = [ps(f"{name}_pT{i}", [128, 512], BF16) for i in range(2)]

    s_cst = P.sem(f"{name}_cst")
    s_ldx = [P.sem(f"{name}_ldx{i}") for i in range(2)]
    s_rst = [P.sem(f"{name}_rst{i}") for i in range(2)]
    s_wa = [P.sem(f"{name}_wa{i}") for i in range(2)]
    s_wp = P.sem(f"{name}_wp")
    s_wst = [P.sem(f"{name}_wst{i}") for i in range(2)]
    s_pe = P.sem(f"{name}_pe")
    s_pool = P.sem(f"{name}_pool")

    P.dma("gpsimd", ones1[:], ones_d, inc=s_cst)
    P.dma("gpsimd", idt[:], ident_d, inc=s_cst)
    P.dma("gpsimd", mask[:], mask_d, inc=s_cst)
    P.dma("gpsimd", psel[:], psel_d, inc=s_cst)
    P.dma("gpsimd", batn[:], batn_d.rearrange("(c p) -> p c", p=128),
          inc=s_cst, slow=True)
    P.dma("gpsimd", bprr[:], bproj_d.rearrange("(o d) -> o d", o=1), inc=s_cst)
    cst_all = s_cst.val
    P.emit("vector", lambda e: e.memset(eps[:], 1e-5))
    P.emit("vector", lambda e: e.tensor_scalar(
        batq[:], batn[:, 0:8], 0.125, None, op0=ALU.mult),
        waits=[(s_cst, cst_all)])

    rin_free = [[], []]
    for sq_i in range(2):
        t0 = sq_i * SEQ
        P.emit("vector", lambda e: e.memset(kt[:, :, 0:64], 0.0),
               waits=[(s_pe, s_pe.val)])
        P.emit("vector", lambda e: e.memset(kt[:, :, 64 + SEQ:], 0.0))
        P.emit("vector", lambda e: e.memset(kp[:, :, 0:8], 0.0))
        P.emit("vector", lambda e: e.memset(kp[:, :, 264:272], 0.0))
        P.emit("vector", lambda e: e.memset(kp[:, :, 400:408], 0.0))
        P.emit("vector", lambda e: e.memset(kp[:, :, 472:480], 0.0))
        zero_done = s_dve.val

        for ch in range(4):
            c0 = t0 + ch * 512
            for s in range(4):
                par = s % 2
                r_c = rin2[par]
                P.dma("sync", r_c[:], rin_d[c0 + s * 128:c0 + (s + 1) * 128, :],
                      waits=[(s_rst[par], s_rst[par].val)] + rin_free[par],
                      inc=s_ldx[par])
                ld_done = s_ldx[par].val
                P.dma("gpsimd", rout_d[c0 + s * 128:c0 + (s + 1) * 128, :],
                      r_c[:], waits=[(s_ldx[par], ld_done)], inc=s_rst[par])
                P.emit("vector", lambda e, r_c=r_c: e.reduce_sum(
                    stat[:, 0:1], r_c[:], axis=AX.X),
                    waits=[(s_ldx[par], ld_done)])
                P.emit("scalar", lambda e, r_c=r_c: e.activation(
                    sq[:], r_c[:], AF.Square, accum_out=stat[:, 1:2]),
                    waits=[(s_ldx[par], ld_done), (s_dve, s_dve.val)])
                P.emit("vector", lambda e: e.tensor_scalar(
                    stat[:, 2:3], stat[:, 0:1], 1.0 / DM, None, op0=ALU.mult))
                P.emit("vector", lambda e: e.tensor_mul(
                    stat[:, 3:4], stat[:, 2:3], stat[:, 2:3]))
                P.emit("vector", lambda e: e.tensor_scalar(
                    stat[:, 4:5], stat[:, 1:2], 1.0 / DM, None, op0=ALU.mult),
                    waits=[s_act])
                P.emit("vector", lambda e: e.tensor_sub(
                    stat[:, 4:5], stat[:, 4:5], stat[:, 3:4]))
                P.emit("scalar", lambda e: e.activation(
                    stat[:, 5:6], stat[:, 4:5], AF.Sqrt, bias=eps[:, 0:1]),
                    waits=[s_dve])
                P.emit("vector", lambda e: e.reciprocal(
                    stat[:, 6:7], stat[:, 5:6]), waits=[s_act])
                P.emit("vector", lambda e, r_c=r_c, par=par: e.tensor_scalar(
                    x3p[par][:], r_c[:], stat[:, 2:3], stat[:, 6:7],
                    op0=ALU.subtract, op1=ALU.mult),
                    waits=[(s_pe, s_pe.val)])
                x3_done = s_dve.val
                rin_free[par] = [(s_dve, x3_done), (s_act, s_act.val)]
                for c in range(8):
                    tp_ = pT[c % 2]
                    P.emit("tensor", lambda e, par=par, c=c, tp_=tp_: e.transpose(
                        tp_[:, 0:128], x3p[par][:, c * 128:(c + 1) * 128], idt[:]),
                        waits=[(s_dve, x3_done), (s_cst, cst_all),
                               (s_act, s_act.val)],
                        inc=s_pe)
                    P.emit("scalar", lambda e, s=s, c=c, tp_=tp_: e.copy(
                        x3fm[:, c, s * 128:(s + 1) * 128], tp_[:, 0:128]),
                        waits=[s_pe])
            x3_ok = s_act.val
            wa_used = [0, 0]
            for o in range(24):
                wpar = o % 2
                P.dma("gpsimd", wat[wpar][:],
                      watn_d[:, o * 128:(o + 1) * 128].rearrange(
                          "(c p) o -> p c o", p=128),
                      waits=[(s_pe, wa_used[wpar])], inc=s_wa[wpar])
                wa_ok = s_wa[wpar].val
                pp = pS[o % 2]
                for c in range(8):
                    P.emit("tensor", lambda e, o=o, c=c, pp=pp, wpar=wpar:
                           e.matmul(pp[:], wat[wpar][:, c, :], x3fm[:, c, :],
                                    start=(c == 0), stop=(c == 7)),
                           waits=[(s_wa[wpar], wa_ok), (s_act, x3_ok),
                                  (s_act, s_act.val), (s_dve, s_dve.val)]
                           if c == 0 else (),
                           inc=s_pe if c == 7 else None)
                wa_used[wpar] = s_pe.val
                if o < 8:
                    P.emit("scalar", lambda e, o=o, ch=ch, pp=pp: e.activation(
                        qt[:, o, ch * 512:(ch + 1) * 512], pp[:], AF.Identity,
                        bias=batq[:, o:o + 1], scale=0.125),
                        waits=[s_pe, (s_dve, s_dve.val)])
                elif o < 16:
                    P.emit("scalar", lambda e, o=o, ch=ch, pp=pp: e.activation(
                        kt[:, o - 8, 64 + ch * 512:64 + (ch + 1) * 512], pp[:],
                        AF.Identity, bias=batn[:, o:o + 1]),
                        waits=[s_pe, (s_dve, zero_done), (s_dve, s_dve.val)])
                else:
                    hp = o - 16
                    P.emit("scalar", lambda e, o=o, pp=pp: e.activation(
                        vfm[:], pp[:], AF.Identity, bias=batn[:, o:o + 1]),
                        waits=[s_pe, (s_dve, s_dve.val)])
                    for b in range(4):
                        su = ch * 4 + b
                        tp_ = pT[b % 2]
                        P.emit("tensor", lambda e, b=b, tp_=tp_: e.transpose(
                            tp_[:, 0:128], vfm[:, b * 128:(b + 1) * 128], idt[:]),
                            waits=[(s_act, s_act.val), (s_dve, s_dve.val)],
                            inc=s_pe)
                        P.emit("scalar", lambda e, hp=hp, su=su, tp_=tp_: e.copy(
                            vtm[:, hp, su, :], tp_[:, 0:128]), waits=[s_pe])
        pk_done0 = s_dve.val
        for hp in range(8):
            kv = kt[:, hp, 64:64 + SEQ]
            P.emit("vector", lambda e, kv=kv: e.tensor_reduce(
                prt[:, 0:256, None], kv.rearrange("p (a b) -> p a b", b=8),
                axis=AX.X, op=ALU.add), waits=[(s_act, s_act.val)])
            P.emit("vector", lambda e, hp=hp: e.tensor_scalar(
                kp[:, hp, 8:264], prt[:, 0:256], 1 / 8.0, None,
                op0=ALU.mult))
            P.emit("vector", lambda e, hp=hp: e.tensor_reduce(
                prt[:, 0:128, None],
                kp[:, hp, 8:264].rearrange("p (a b) -> p a b", b=2),
                axis=AX.X, op=ALU.add))
            P.emit("vector", lambda e, hp=hp: e.tensor_scalar(
                kp[:, hp, 272:400], prt[:, 0:128], 0.5, None,
                op0=ALU.mult))
            P.emit("vector", lambda e, hp=hp: e.tensor_reduce(
                prt[:, 0:64, None],
                kp[:, hp, 272:400].rearrange("p (a b) -> p a b", b=2),
                axis=AX.X, op=ALU.add))
            P.emit("vector", lambda e, hp=hp: e.tensor_scalar(
                kp[:, hp, 408:472], prt[:, 0:64], 0.5, None,
                op0=ALU.mult))
            P.emit("vector", lambda e, hp=hp: e.tensor_reduce(
                prt[:, 0:32, None],
                kp[:, hp, 408:472].rearrange("p (a b) -> p a b", b=2),
                axis=AX.X, op=ALU.add))
            P.emit("vector", lambda e, hp=hp: e.tensor_scalar(
                kp[:, hp, 480:512], prt[:, 0:32], 0.5, None,
                op0=ALU.mult))
        kp_done = s_dve.val

        pS_free = [[], []]
        pO_free = [[], []]
        pPool_free = [[], []]
        pT_free = [[], []]
        vc_free = [[], []]
        sm_free = [[], []]
        for hp in range(8):
            for hd in range(2):
                hb = hd * 64
                for sb in range(16):
                    ip = sb % 2
                    pp = pS[ip]
                    q_l = qt[hb:hb + 64, hp, sb * 128:(sb + 1) * 128]
                    P.emit("tensor", lambda e, pp=pp, q_l=q_l, hp=hp, hb=hb,
                           sb=sb: e.matmul(
                               pp[:, 0:192], q_l,
                               kt[hb:hb + 64, hp, sb * 128:sb * 128 + 192],
                               start=True, stop=True),
                           waits=[(s_dve, kp_done)] + pS_free[ip])
                    P.emit("tensor", lambda e, pp=pp, q_l=q_l, hp=hp, hb=hb,
                           sb=sb: e.matmul(
                               pp[:, 192:208], q_l,
                               kp[hb:hb + 64, hp, 16 * sb:16 * sb + 16],
                               start=True, stop=True))
                    P.emit("tensor", lambda e, pp=pp, q_l=q_l, hp=hp, hb=hb,
                           sb=sb: e.matmul(
                               pp[:, 208:216], q_l,
                               kp[hb:hb + 64, hp, 264 + 8 * sb:272 + 8 * sb],
                               start=True, stop=True))
                    P.emit("tensor", lambda e, pp=pp, q_l=q_l, hp=hp, hb=hb,
                           sb=sb: e.matmul(
                               pp[:, 216:224], q_l,
                               kp[hb:hb + 64, hp,
                                  400 + 8 * (sb // 2):408 + 8 * (sb // 2)],
                               start=True, stop=True))
                    P.emit("tensor", lambda e, pp=pp, q_l=q_l, hp=hp, hb=hb,
                           sb=sb: e.matmul(
                               pp[:, 224:232], q_l,
                               kp[hb:hb + 64, hp,
                                  472 + 8 * (sb // 4):480 + 8 * (sb // 4)],
                               start=True, stop=True), inc=s_pe)
                    s_done = s_pe.val
                    items = sorted(poolsel_chunks(sb).items())
                    ni = len(items)
                    ppl = pPool[ip]
                    for i, (cu, _m) in enumerate(items):
                        P.emit("tensor", lambda e, sb=sb, i=i, cu=cu, hp=hp,
                               hb=hb, ni=ni, ppl=ppl: e.matmul(
                                   ppl[:], psel[:, sb, i, :],
                                   vtm[:, hp, cu, hb:hb + 64],
                                   start=(i == 0), stop=(i == ni - 1)),
                               waits=pPool_free[ip] if i == 0 else (),
                               inc=s_pe if i == ni - 1 else None)
                    P.emit("scalar", lambda e, ppl=ppl, ip=ip: e.copy(
                        vps[ip][:], ppl[:]), waits=[s_pe] + vc_free[ip])
                    pPool_free[ip] = [(s_act, s_act.val)]
                    smt = Sm[ip]
                    P.emit("vector", lambda e, smt=smt, pp=pp: e.tensor_add(
                        smt[:], pp[:, 0:232], mask[:]),
                        waits=[(s_pe, s_done)] + sm_free[ip])
                    pS_free[ip] = [(s_dve, s_dve.val)]
                    P.emit("vector", lambda e, smt=smt: e.reduce_max(
                        stat[:, 0:1], smt[:], axis=AX.X))
                    P.emit("vector", lambda e: e.tensor_scalar(
                        stat[:, 1:2], stat[:, 0:1], -1.0, None, op0=ALU.mult),
                        waits=[(s_act, s_act.val)])
                    P.emit("scalar", lambda e, smt=smt: e.activation(
                        smt[:], smt[:], AF.Exp, bias=stat[:, 1:2]),
                        waits=[s_dve])
                    P.emit("vector", lambda e, smt=smt: e.reduce_sum(
                        stat[:, 2:3], smt[:], axis=AX.X), waits=[s_act])
                    P.emit("vector", lambda e: e.reciprocal(
                        stat[:, 3:4], stat[:, 2:3]))
                    ppt = Pp[ip]
                    P.emit("vector", lambda e, smt=smt, ppt=ppt: e.tensor_scalar(
                        ppt[:], smt[:], stat[:, 3:4], None, op0=ALU.mult),
                        waits=pT_free[ip])
                    sm_free[ip] = [(s_dve, s_dve.val)]
                    p_done = s_dve.val
                    tp_ = pT[ip]
                    P.emit("tensor", lambda e, ppt=ppt, tp_=tp_: e.transpose(
                        tp_[0:64, 0:128], ppt[:, 0:64], idt[:]),
                        waits=[(s_dve, p_done)] + pT_free[ip], inc=s_pe)
                    P.emit("tensor", lambda e, ppt=ppt, tp_=tp_: e.transpose(
                        tp_[0:64, 128:256], ppt[:, 64:128], idt[:]), inc=s_pe)
                    P.emit("tensor", lambda e, ppt=ppt, tp_=tp_: e.transpose(
                        tp_[0:64, 256:384], ppt[:, 128:192], idt[:]), inc=s_pe)
                    P.emit("tensor", lambda e, ppt=ppt, tp_=tp_: e.transpose(
                        tp_[0:40, 384:512], ppt[:, 192:232], idt[:]), inc=s_pe)
                    tr_pe = s_pe.val
                    P.emit("scalar", lambda e, ip=ip, tp_=tp_: e.copy(
                        pta[ip][:], tp_[0:64, 0:128]),
                        waits=[(s_pe, tr_pe)])
                    P.emit("scalar", lambda e, ip=ip, tp_=tp_: e.copy(
                        ptb[ip][:], tp_[0:64, 128:256]))
                    P.emit("scalar", lambda e, ip=ip, tp_=tp_: e.copy(
                        ptc[ip][:], tp_[0:64, 256:384]))
                    P.emit("scalar", lambda e, ip=ip, tp_=tp_: e.copy(
                        pt3s[ip][:], tp_[0:40, 384:512]))
                    tr_ok = s_act.val
                    if sb > 0:
                        P.emit("vector", lambda e, hp=hp, hb=hb, sb=sb, ip=ip:
                               e.tensor_copy(vca[ip][:],
                                             vtm[64:128, hp, sb - 1,
                                                 hb:hb + 64]),
                               waits=pO_free[ip])
                    P.emit("vector", lambda e, hp=hp, hb=hb, sb=sb, ip=ip:
                           e.tensor_copy(vcb[ip][:], vtm[64:128, hp, sb,
                                                         hb:hb + 64]),
                           waits=pO_free[ip])
                    vc_ok = s_dve.val
                    vc_free[ip] = [(s_dve, vc_ok)]
                    po = pO[ip]
                    first = True
                    if sb > 0:
                        P.emit("tensor", lambda e, po=po, ip=ip:
                               e.matmul(po[:, 0:64], pta[ip][:], vca[ip][:],
                                        start=True, stop=False),
                               waits=[(s_act, tr_ok), (s_dve, vc_ok)] +
                               pO_free[ip])
                        first = False
                    P.emit("tensor", lambda e, po=po, hp=hp, hb=hb, sb=sb,
                           ip=ip, first=first: e.matmul(
                               po[:, 0:64], ptb[ip][:],
                               vtm[0:64, hp, sb, hb:hb + 64],
                               start=first, stop=False),
                           waits=[(s_act, tr_ok), (s_dve, vc_ok)] +
                           pO_free[ip] if first else ())
                    P.emit("tensor", lambda e, po=po, ip=ip:
                           e.matmul(po[:, 0:64], ptc[ip][:], vcb[ip][:],
                                    start=False, stop=False))
                    P.emit("tensor", lambda e, po=po, ip=ip: e.matmul(
                        po[:, 0:64], pt3s[ip][:], vps[ip][:],
                        start=False, stop=True),
                        waits=[(s_act, s_act.val)], inc=s_pe)
                    o_pe = s_pe.val
                    P.emit("scalar", lambda e, po=po, sb=sb, hp=hp, hd=hd:
                           e.copy(otm[:, sb, (hp * 2 + hd) * 64:
                                      (hp * 2 + hd + 1) * 64], po[:, 0:64]),
                           waits=[(s_pe, o_pe)])
                    pO_free[ip] = [(s_act, s_act.val)]
                    pT_free[ip] = [(s_act, tr_ok)]
        at_done = s_act.val

        P.dma("gpsimd", wpj[:], wproj_d.rearrange("(c p) o -> p c o", p=128),
              waits=[(s_pe, s_pe.val)], inc=s_wp)
        wp_ok = s_wp.val
        for tsu in range(16):
            for c in range(8):
                tp_ = pT[c % 2]
                P.emit("tensor", lambda e, tsu=tsu, c=c, tp_=tp_: e.transpose(
                    tp_[:, 0:128], otm[:, tsu, c * 128:(c + 1) * 128], idt[:]),
                    waits=[(s_act, at_done), (s_act, s_act.val)], inc=s_pe)
                P.emit("scalar", lambda e, c=c, tp_=tp_: e.copy(
                    ofm[:, c, :], tp_[:, 0:128]), waits=[s_pe])
            ofm_ok = s_act.val
            for ocol in range(2):
                pp = pS[ocol]
                for c in range(8):
                    P.emit("tensor", lambda e, c=c, ocol=ocol, pp=pp: e.matmul(
                        pp[:], ofm[:, c, :],
                        wpj[:, c, ocol * 512:(ocol + 1) * 512],
                        start=(c == 0), stop=False),
                        waits=[(s_act, ofm_ok), (s_wp, wp_ok),
                               (s_dve, s_dve.val)] if c == 0 else ())
                P.emit("tensor", lambda e, ocol=ocol, pp=pp: e.matmul(
                    pp[:], ones1[:], bprr[0:1, ocol * 512:(ocol + 1) * 512],
                    start=False, stop=True),
                    waits=[(s_cst, cst_all)], inc=s_pe)
                wp2 = ocol
                P.emit("vector", lambda e, pp=pp, wp2=wp2: e.tensor_copy(
                    wt2[wp2][:], pp[:]),
                    waits=[s_pe, (s_wst[wp2], s_wst[wp2].val)])
                r0 = t0 + tsu * 128
                P.emit("gpsimd", lambda e, wp2=wp2, r0=r0, ocol=ocol:
                       e.dma_start(
                           out=rout_d[r0:r0 + 128,
                                      ocol * 512:(ocol + 1) * 512],
                           in_=wt2[wp2][:], accum_op=ALU.add),
                       waits=[(s_dve, s_dve.val),
                              (s_rst[0], s_rst[0].val),
                              (s_rst[1], s_rst[1].val)],
                       inc=s_wst[wp2], n=16)

    fin = sb_(f"{name}_fin", [1, 8], F32)
    P.emit("gpsimd", lambda e: e.memset(fin[:], 0.0),
           waits=[(s_wst[0], s_wst[0].val), (s_wst[1], s_wst[1].val),
                  (s_rst[0], s_rst[0].val), (s_rst[1], s_rst[1].val)])


def build_l1_attn_program(nslot):
    nc = bass.Bass("TRN2", target_bir_lowering=False)
    rin_d = nc.dram_tensor("rin", [NTOK, DM], F32, kind="ExternalInput").ap()
    watn_d = nc.dram_tensor("watn", [DM, 3072], F32, kind="ExternalInput").ap()
    batn_d = nc.dram_tensor("batn", [3072], F32, kind="ExternalInput").ap()
    wproj_d = nc.dram_tensor("wproj", [DM, DM], F32, kind="ExternalInput").ap()
    bproj_d = nc.dram_tensor("bproj", [DM], F32, kind="ExternalInput").ap()
    ident_d = nc.dram_tensor("ident", [128, 128], F32, kind="ExternalInput").ap()
    ones_d = nc.dram_tensor("ones", [1, 128], F32, kind="ExternalInput").ap()
    psel_d = nc.dram_tensor("psel", [128, 16, nslot, 40], F32,
                            kind="ExternalInput").ap()
    mask_d = nc.dram_tensor("maskc", [128, 232], F32, kind="ExternalInput").ap()
    rout_d = nc.dram_tensor("rout", [NTOK, DM], F32, kind="ExternalOutput").ap()

    with ExitStack() as ctx:
        P = Prog(nc, ctx)
        block = ctx.enter_context(nc.Block())
        emit_l1_attn(P, rin_d, rout_d, watn_d, batn_d, wproj_d, bproj_d,
                     ident_d, ones_d, psel_d, mask_d, nslot)
        P.finalize(block)
    return nc



def emit_l1_ffn(P, rin_d, rout_d, x4fm_d, wgcat_d, brg_d, w1_d, b1_d, w2_d,
                aug2_d, ident_d, ones_d, name="m1"):
    nc = P.nc
    CH = 512
    NCH = NTOK // CH
    SUB = CH // 128
    NE = 9
    sb, ps = P.sb, P.ps
    s_act, s_dve = P.sa, P.sv

    cwp = sb(f"{name}_cwp", [128, 32, NE], F32)
    ones1 = sb(f"{name}_ones", [1, 128], BF16)

    s_ldw = P.sem(f"{name}_ldw")
    s_ldx = [P.sem(f"{name}_ldx{i}") for i in range(2)]
    s_rst = [P.sem(f"{name}_rst{i}") for i in range(2)]
    s_x4st = [P.sem(f"{name}_x4st{i}") for i in range(2)]
    s_pe = P.sem(f"{name}_pe")

    P.dma("gpsimd", ones1[:], ones_d, inc=s_ldw)

    P.push_scope()
    rin2 = [sb(f"{name}_rin{i}", [128, SUB, DM], F32) for i in range(2)]
    x2p = sb(f"{name}_x2p", [128, SUB, DM], BF16)
    xf = sb(f"{name}_xf", [128, 8, CH], BF16)
    sq = sb(f"{name}_sq", [128, DM], F32)
    stat = sb(f"{name}_stat", [128, 8], F32)
    stat2 = sb(f"{name}_stat2", [128, 16], F32)
    tmp10 = sb(f"{name}_tmp10", [128, 10], F32)
    gpt = sb(f"{name}_gpt", [128, 8], F32)
    t8a = sb(f"{name}_t8a", [128, 8], F32)
    t8b = sb(f"{name}_t8b", [128, 8], F32)
    wgt = sb(f"{name}_wgt", [128, 8, 10], BF16)
    brgt = sb(f"{name}_brgt", [1, 10], BF16)
    idt = sb(f"{name}_idt", [128, 128], BF16)
    eps = sb(f"{name}_eps", [128, 1], F32)
    pt = [ps(f"{name}_pt{i}", [128, 128], BF16) for i in range(2)]
    prt = ps(f"{name}_prt", [128, 16])

    P.dma("gpsimd", wgt[:], wgcat_d.rearrange("(c p) g -> p c g", p=128), inc=s_ldw)
    P.dma("gpsimd", brgt[:], brg_d.rearrange("(o g) -> o g", o=1), inc=s_ldw)
    P.dma("gpsimd", idt[:], ident_d, inc=s_ldw)
    ldw_all = s_ldw.val
    P.emit("vector", lambda e: e.memset(eps[:], 1e-5))

    rin_free = [[], []]
    for ci in range(NCH):
        par = ci % 2
        r_c = rin2[par]
        rview = rin_d[ci * CH:(ci + 1) * CH, :].rearrange("(s p) d -> p s d", p=128)
        P.dma("sync", r_c[:], rview,
              waits=[(s_rst[par], s_rst[par].val)] + rin_free[par],
              inc=s_ldx[par])
        ld_done = s_ldx[par].val
        oview = rout_d[ci * CH:(ci + 1) * CH, :].rearrange("(s p) d -> p s d", p=128)
        P.dma("gpsimd", oview, r_c[:], waits=[(s_ldx[par], ld_done)],
              inc=s_rst[par])
        for s in range(SUB):
            row = r_c[:, s, :]
            P.emit("vector", lambda e, row=row: e.reduce_sum(
                stat[:, 0:1], row, axis=AX.X), waits=[(s_ldx[par], ld_done)])
            P.emit("scalar", lambda e, row=row: e.activation(
                sq[:], row, AF.Square, accum_out=stat[:, 1:2]),
                waits=[(s_ldx[par], ld_done), (s_dve, s_dve.val)])
            P.emit("vector", lambda e: e.tensor_scalar(
                stat[:, 2:3], stat[:, 0:1], 1.0 / DM, None, op0=ALU.mult))
            P.emit("vector", lambda e: e.tensor_mul(
                stat[:, 3:4], stat[:, 2:3], stat[:, 2:3]))
            P.emit("vector", lambda e: e.tensor_scalar(
                stat[:, 4:5], stat[:, 1:2], 1.0 / DM, None, op0=ALU.mult),
                waits=[s_act])
            P.emit("vector", lambda e: e.tensor_sub(
                stat[:, 4:5], stat[:, 4:5], stat[:, 3:4]))
            P.emit("scalar", lambda e: e.activation(
                stat[:, 5:6], stat[:, 4:5], AF.Sqrt, bias=eps[:, 0:1]),
                waits=[s_dve])
            P.emit("vector", lambda e: e.reciprocal(
                stat[:, 6:7], stat[:, 5:6]), waits=[s_act])
            P.emit("vector", lambda e, row=row, s=s: e.tensor_scalar(
                x2p[:, s, :], row, stat[:, 2:3], stat[:, 6:7],
                op0=ALU.subtract, op1=ALU.mult),
                waits=[(s_pe, s_pe.val)])
            x2_done = s_dve.val
            for c in range(8):
                tp = pt[c % 2]
                P.emit("tensor", lambda e, s=s, c=c, tp=tp: e.transpose(
                    tp[:], x2p[:, s, c * 128:(c + 1) * 128], idt[:]),
                    waits=[(s_dve, x2_done), (s_ldw, ldw_all),
                           (s_act, s_act.val)],
                    inc=s_pe)
                P.emit("scalar", lambda e, s=s, c=c, tp=tp: e.copy(
                    xf[:, c, s * 128:(s + 1) * 128], tp[:]),
                    waits=[s_pe, (s_x4st[0], s_x4st[0].val),
                           (s_x4st[1], s_x4st[1].val)])
        tr_done = s_act.val
        rin_free[par] = [(s_dve, s_dve.val), (s_act, s_act.val)]
        for s in range(SUB):
            for c in range(8):
                P.emit("tensor", lambda e, s=s, c=c: e.matmul(
                    prt[:, 0:10], xf[:, c, s * 128:(s + 1) * 128],
                    wgt[:, c, :], start=(c == 0), stop=False),
                    waits=[(s_act, s_act.val), (s_dve, s_dve.val)] if c == 0 else ())
            P.emit("tensor", lambda e: e.matmul(
                prt[:, 0:10], ones1[:], brgt[:], start=False, stop=True),
                inc=s_pe)
            g = ci * SUB + s
            P.emit("vector", lambda e: e.reduce_max(
                stat2[:, 0:1], prt[:, 0:2], axis=AX.X), waits=[s_pe])
            P.emit("vector", lambda e: e.tensor_scalar(
                stat2[:, 1:2], stat2[:, 0:1], -1.0, None, op0=ALU.mult))
            P.emit("scalar", lambda e: e.activation(
                tmp10[:, 0:2], prt[:, 0:2], AF.Exp, bias=stat2[:, 1:2]),
                waits=[s_dve, s_pe])
            P.emit("vector", lambda e: e.reduce_sum(
                stat2[:, 2:3], tmp10[:, 0:2], axis=AX.X), waits=[s_act])
            P.emit("vector", lambda e: e.reciprocal(stat2[:, 3:4], stat2[:, 2:3]))
            P.emit("vector", lambda e: e.reduce_max(
                stat2[:, 4:5], prt[:, 2:10], axis=AX.X))
            P.emit("vector", lambda e: e.tensor_scalar(
                stat2[:, 5:6], stat2[:, 4:5], -1.0, None, op0=ALU.mult))
            P.emit("scalar", lambda e: e.activation(
                tmp10[:, 2:10], prt[:, 2:10], AF.Exp, bias=stat2[:, 5:6]),
                waits=[s_dve])
            P.emit("vector", lambda e: e.reduce_sum(
                stat2[:, 6:7], tmp10[:, 2:10], axis=AX.X), waits=[s_act])
            P.emit("vector", lambda e: e.reciprocal(stat2[:, 7:8], stat2[:, 6:7]))
            P.emit("vector", lambda e: e.tensor_scalar(
                gpt[:], tmp10[:, 2:10], stat2[:, 7:8], None, op0=ALU.mult))
            P.emit("vector", lambda e: e.reduce_max(
                stat2[:, 8:9], gpt[:], axis=AX.X))
            P.emit("vector", lambda e: e.tensor_scalar(
                t8a[:], gpt[:], stat2[:, 8:9], -1e30,
                op0=ALU.is_equal, op1=ALU.mult))
            P.emit("vector", lambda e: e.tensor_add(t8a[:], t8a[:], gpt[:]))
            P.emit("vector", lambda e: e.reduce_max(
                stat2[:, 9:10], t8a[:], axis=AX.X))
            P.emit("vector", lambda e: e.tensor_add(
                stat2[:, 10:11], stat2[:, 8:9], stat2[:, 9:10]))
            P.emit("vector", lambda e: e.reciprocal(
                stat2[:, 11:12], stat2[:, 10:11]))
            P.emit("vector", lambda e: e.tensor_scalar(
                t8a[:], gpt[:], stat2[:, 8:9], None, op0=ALU.is_equal))
            P.emit("vector", lambda e: e.tensor_scalar(
                t8b[:], gpt[:], stat2[:, 9:10], None, op0=ALU.is_equal))
            P.emit("vector", lambda e: e.tensor_add(t8a[:], t8a[:], t8b[:]))
            P.emit("vector", lambda e: e.tensor_mul(t8a[:], t8a[:], gpt[:]))
            P.emit("vector", lambda e: e.tensor_scalar(
                t8a[:], t8a[:], stat2[:, 11:12], None, op0=ALU.mult))
            P.emit("vector", lambda e: e.tensor_scalar(
                t8a[:], t8a[:], stat2[:, 3:4], None, op0=ALU.mult))
            P.emit("vector", lambda e, g=g: e.tensor_scalar(
                cwp[:, g, 0:8], t8a[:], tmp10[:, 0:1], None, op0=ALU.mult))
            P.emit("vector", lambda e, g=g: e.tensor_scalar(
                cwp[:, g, 8:9], tmp10[:, 1:2], stat2[:, 3:4], None,
                op0=ALU.mult))
        xview = x4fm_d[:, ci, :, :]
        P.dma("gpsimd", xview, xf[:],
              waits=[(s_act, tr_done), (s_pe, s_pe.val)], inc=s_x4st[par])
    P.pop_scope()

    w1t = sb(f"{name}_w1t", [128, 8, 4096], BF16)
    w2t = sb(f"{name}_w2t", [128, 32, DM], BF16)
    b1t = [sb(f"{name}_b1t{i}", [128, 32], F32) for i in range(2)]
    aug2t = [sb(f"{name}_aug{i}", [1, DM], BF16) for i in range(2)]
    xb = [sb(f"{name}_xb{i}", [128, 8, CH], BF16) for i in range(2)]
    h_blk = sb(f"{name}_hblk", [128, 32, CH], BF16)
    wtile = [sb(f"{name}_wt{i}", [128, 512], F32) for i in range(2)]
    psA = [ps(f"{name}_psA{i}", [128, 512]) for i in range(2)]
    psB = [ps(f"{name}_psB{i}", [128, 512]) for i in range(2)]
    s_w1q = [P.sem(f"{name}_w1q{i}") for i in range(4)]
    s_w2 = P.sem(f"{name}_w2")
    s_bld = P.sem(f"{name}_bld")
    s_xb = [P.sem(f"{name}_xb{i}") for i in range(2)]
    s_wst = [P.sem(f"{name}_wst{i}") for i in range(2)]

    boundary = P.join(P.sv, P.sa, s_pe, s_x4st[0], s_x4st[1],
                      s_rst[0], s_rst[1], s_ldx[0], s_ldx[1])
    peA_end = 0
    peB_end = 0
    xbA_done = [0, 0]
    peA_q = [0, 0, 0, 0]
    for e_i in range(NE):
        for q in range(4):
            P.dma("gpsimd", w1t[:, :, q * 1024:(q + 1) * 1024],
                  w1_d[e_i].rearrange("(c p) f -> p c f", p=128)[
                      :, :, q * 1024:(q + 1) * 1024],
                  waits=boundary + [(s_pe, peA_q[q])], inc=s_w1q[q])
        P.dma("gpsimd", w2t[:],
              w2_d[e_i].rearrange("(f p) d -> p f d", p=128),
              waits=boundary + [(s_pe, peB_end)], inc=s_w2)
        P.dma("gpsimd", b1t[e_i % 2][:],
              b1_d[e_i].rearrange("(f p) -> p f", p=128),
              waits=boundary + [(s_act, s_act.val)], inc=s_bld, slow=True)
        P.dma("gpsimd", aug2t[e_i % 2][:],
              aug2_d[e_i:e_i + 1, :],
              waits=boundary + [(s_pe, peB_end)], inc=s_bld)
        w1_ok = [s_w1q[q].val for q in range(4)]
        w2_ok = s_w2.val
        bld_ok = s_bld.val
        for tb in range(NCH):
            tp = tb % 2
            xview = x4fm_d[:, tb, :, :]
            P.dma("sync", xb[tp][:], xview,
                  waits=boundary + [(s_pe, xbA_done[tp])], inc=s_xb[tp])
            xb_ok = s_xb[tp].val
            for f in range(32):
                ph = psA[f % 2]
                for c in range(8):
                    P.emit("tensor", lambda e, f=f, c=c, ph=ph, tp=tp: e.matmul(
                        ph[:], w1t[:, c, f * 128:(f + 1) * 128],
                        xb[tp][:, c, :], start=(c == 0), stop=(c == 7)),
                        waits=[(s_xb[tp], xb_ok), (s_w1q[f // 8], w1_ok[f // 8]),
                               (s_act, s_act.val)] if c == 0 else (),
                        inc=s_pe if c == 7 else None)
                P.emit("scalar", lambda e, f=f, ph=ph, e_i=e_i: e.activation(
                    h_blk[:, f, :], ph[:], AF.Relu,
                    bias=b1t[e_i % 2][:, f:f + 1]),
                    waits=[s_pe, (s_bld, bld_ok)])
                if tb == NCH - 1 and f % 8 == 7:
                    peA_q[f // 8] = s_pe.val
            xbA_done[tp] = s_pe.val
            hB_ok = s_act.val
            for ts_i in range(SUB):
                for dc in range(2):
                    po = psB[(ts_i * 2 + dc) % 2]
                    for f in range(32):
                        P.emit("tensor",
                               lambda e, f=f, ts_i=ts_i, dc=dc, po=po: e.matmul(
                                   po[:], h_blk[:, f, ts_i * 128:(ts_i + 1) * 128],
                                   w2t[:, f, dc * 512:(dc + 1) * 512],
                                   start=(f == 0), stop=False),
                               waits=[(s_act, hB_ok), (s_w2, w2_ok),
                                      (s_dve, s_dve.val)] if f == 0 else ())
                    P.emit("tensor", lambda e, dc=dc, po=po, e_i=e_i: e.matmul(
                        po[:], ones1[:], aug2t[e_i % 2][0:1, dc * 512:(dc + 1) * 512],
                        start=False, stop=True),
                        waits=[(s_bld, bld_ok)], inc=s_pe)
                    wp = (ts_i * 2 + dc) % 2
                    g = tb * SUB + ts_i
                    P.emit("vector", lambda e, po=po, wp=wp, g=g, e_i=e_i:
                           e.tensor_scalar(wtile[wp][:], po[:],
                                           cwp[:, g, e_i:e_i + 1], None,
                                           op0=ALU.mult),
                           waits=[s_pe, (s_wst[wp], s_wst[wp].val)])
                    r0 = tb * CH + ts_i * 128
                    P.emit("gpsimd", lambda e, wp=wp, r0=r0, dc=dc: e.dma_start(
                        out=rout_d[r0:r0 + 128, dc * 512:(dc + 1) * 512],
                        in_=wtile[wp][:], accum_op=ALU.add),
                        waits=[(s_dve, s_dve.val)], inc=s_wst[wp], n=16)
            peB_end = s_pe.val

    fin = sb(f"{name}_fin", [1, 8], F32)
    P.emit("gpsimd", lambda e: e.memset(fin[:], 0.0),
           waits=[(s_wst[0], s_wst[0].val), (s_wst[1], s_wst[1].val),
                  (s_rst[0], s_rst[0].val), (s_rst[1], s_rst[1].val)])


def build_l1_ffn_program():
    nc = bass.Bass("TRN2", target_bir_lowering=False)
    rin_d = nc.dram_tensor("rin", [NTOK, DM], F32, kind="ExternalInput").ap()
    wgcat_d = nc.dram_tensor("wgcat", [DM, 10], F32, kind="ExternalInput").ap()
    brg_d = nc.dram_tensor("brg", [10], F32, kind="ExternalInput").ap()
    w1_d = nc.dram_tensor("w1", [9, DM, 4096], F32, kind="ExternalInput").ap()
    b1_d = nc.dram_tensor("b1", [9, 4096], F32, kind="ExternalInput").ap()
    w2_d = nc.dram_tensor("w2", [9, 4096, DM], F32, kind="ExternalInput").ap()
    aug2_d = nc.dram_tensor("aug2", [9, DM], F32, kind="ExternalInput").ap()
    ident_d = nc.dram_tensor("ident", [128, 128], F32, kind="ExternalInput").ap()
    ones_d = nc.dram_tensor("ones", [1, 128], F32, kind="ExternalInput").ap()
    rout_d = nc.dram_tensor("rout", [NTOK, DM], F32, kind="ExternalOutput").ap()
    x4fm_d = nc.dram_tensor("x4fm", [128, 8, 8, 512], BF16, kind="Internal").ap()

    with ExitStack() as ctx:
        P = Prog(nc, ctx)
        block = ctx.enter_context(nc.Block())
        emit_l1_ffn(P, rin_d, rout_d, x4fm_d, wgcat_d, brg_d, w1_d, b1_d,
                    w2_d, aug2_d, ident_d, ones_d)
        P.finalize(block)
    return nc


def build_l0_ffn_program():
    nc = bass.Bass("TRN2", target_bir_lowering=False)
    rin_d = nc.dram_tensor("rin", [NTOK, DM], F32, kind="ExternalInput").ap()
    wr_d = nc.dram_tensor("wrT", [DM, DM], F32, kind="ExternalInput").ap()
    wk_d = nc.dram_tensor("wkT", [DM, DM], F32, kind="ExternalInput").ap()
    wv_d = nc.dram_tensor("wvT", [DM, DM], F32, kind="ExternalInput").ap()
    mur_d = nc.dram_tensor("mur", [DM], F32, kind="ExternalInput").ap()
    muk_d = nc.dram_tensor("muk", [DM], F32, kind="ExternalInput").ap()
    ident_d = nc.dram_tensor("ident", [128, 128], F32, kind="ExternalInput").ap()
    rout_d = nc.dram_tensor("rout", [NTOK, DM], F32, kind="ExternalOutput").ap()

    with ExitStack() as ctx:
        P = Prog(nc, ctx)
        block = ctx.enter_context(nc.Block())
        emit_l0_ffn(P, rin_d, rout_d, wr_d, wk_d, wv_d, mur_d, muk_d, ident_d)
        P.finalize(block)
    return nc


def build_fused_program(nslot):
    nc = bass.Bass("TRN2", target_bir_lowering=False)
    F = F32
    xin = nc.dram_tensor("xin", [NTOK, DM], F, kind="ExternalInput").ap()
    out = nc.dram_tensor("out", [NTOK, DM], F, kind="ExternalOutput").ap()
    r1 = nc.dram_tensor("r1", [NTOK, DM], F, kind="Internal").ap()
    r2 = nc.dram_tensor("r2", [NTOK, DM], F, kind="Internal").ap()
    r3 = nc.dram_tensor("r3", [NTOK, DM], F, kind="Internal").ap()
    x4fm_d = nc.dram_tensor("x4fm", [128, 8, 8, 512], BF16, kind="Internal").ap()

    winT = nc.dram_tensor("winT", [DM, 4096], F, kind="ExternalInput").ap()
    bin_ = nc.dram_tensor("bin", [4096], F, kind="ExternalInput").ap()
    wconv = nc.dram_tensor("wconv", [2048, 4], F, kind="ExternalInput").ap()
    bconv = nc.dram_tensor("bconv", [2048], F, kind="ExternalInput").ap()
    wxT = nc.dram_tensor("wxT", [2048, 2080], F, kind="ExternalInput").ap()
    bx = nc.dram_tensor("bx", [2080], F, kind="ExternalInput").ap()
    woutT = nc.dram_tensor("woutT", [2048, DM], F, kind="ExternalInput").ap()
    bout = nc.dram_tensor("bout", [DM], F, kind="ExternalInput").ap()
    alog = nc.dram_tensor("alog", [2048, 16], F, kind="ExternalInput").ap()
    dvec = nc.dram_tensor("dvec", [2048], F, kind="ExternalInput").ap()
    wrT = nc.dram_tensor("wrT", [DM, DM], F, kind="ExternalInput").ap()
    wkT = nc.dram_tensor("wkT", [DM, DM], F, kind="ExternalInput").ap()
    wvT = nc.dram_tensor("wvT", [DM, DM], F, kind="ExternalInput").ap()
    mur = nc.dram_tensor("mur", [DM], F, kind="ExternalInput").ap()
    muk = nc.dram_tensor("muk", [DM], F, kind="ExternalInput").ap()
    watn = nc.dram_tensor("watn", [DM, 3072], F, kind="ExternalInput").ap()
    batn = nc.dram_tensor("batn", [3072], F, kind="ExternalInput").ap()
    wproj = nc.dram_tensor("wproj", [DM, DM], F, kind="ExternalInput").ap()
    bproj = nc.dram_tensor("bproj", [DM], F, kind="ExternalInput").ap()
    wgcat = nc.dram_tensor("wgcat", [DM, 10], F, kind="ExternalInput").ap()
    brg = nc.dram_tensor("brg", [10], F, kind="ExternalInput").ap()
    w1 = nc.dram_tensor("w1", [9, DM, 4096], F, kind="ExternalInput").ap()
    b1 = nc.dram_tensor("b1", [9, 4096], F, kind="ExternalInput").ap()
    w2 = nc.dram_tensor("w2", [9, 4096, DM], F, kind="ExternalInput").ap()
    aug2 = nc.dram_tensor("aug2", [9, DM], F, kind="ExternalInput").ap()
    ident = nc.dram_tensor("ident", [128, 128], F, kind="ExternalInput").ap()
    ones = nc.dram_tensor("ones", [1, 128], F, kind="ExternalInput").ap()
    onehot = nc.dram_tensor("onehot", [16, 16, 128], F, kind="ExternalInput").ap()
    psel = nc.dram_tensor("psel", [128, 16, nslot, 40], F,
                          kind="ExternalInput").ap()
    maskc = nc.dram_tensor("maskc", [128, 232], F, kind="ExternalInput").ap()

    with ExitStack() as ctx:
        P = Prog(nc, ctx)
        block = ctx.enter_context(nc.Block())
        P.push_scope()
        emit_l0_attn(P, xin, r1, winT, bin_, wconv, bconv, wxT, bx, woutT,
                     bout, alog, dvec, ident, ones, onehot)
        P.pop_scope()
        P.soft_barrier()
        P.push_scope()
        emit_l0_ffn(P, r1, r2, wrT, wkT, wvT, mur, muk, ident)
        P.pop_scope()
        P.soft_barrier()
        P.push_scope()
        emit_l1_attn(P, r2, r3, watn, batn, wproj, bproj, ident, ones,
                     psel, maskc, nslot)
        P.pop_scope()
        P.soft_barrier()
        P.push_scope()
        emit_l1_ffn(P, r3, out, x4fm_d, wgcat, brg, w1, b1, w2, aug2,
                    ident, ones)
        P.pop_scope()
        P.finalize(block)
    return nc



_CACHE = {}


def _program():
    if "prog" not in _CACHE:
        psel, maskc = build_l1_attn_consts()
        nslot = psel.shape[1]
        _CACHE["consts"] = (psel, maskc, nslot)
        _CACHE["prog"] = build_fused_program(nslot)
    return _CACHE["prog"], _CACHE["consts"]


def kernel(**ins):
    from concourse.bass_utils import run_bass_kernel_spmd
    prog, (psel, maskc, nslot) = _program()
    ncores = 8

    ident = np.eye(128, dtype=np.float32)
    ones = np.ones((1, 128), np.float32)
    onehot = np.zeros((16, 16, 128), np.float32)
    for n in range(16):
        onehot[n, n, :] = 1.0

    w = {
        "winT": np.ascontiguousarray(ins["L0_attn_W_in"].T),
        "bin": ins["L0_attn_b_in"],
        "wconv": np.ascontiguousarray(ins["L0_attn_W_conv"].reshape(2048, 4)),
        "bconv": ins["L0_attn_b_conv"],
        "wxT": np.ascontiguousarray(ins["L0_attn_W_x"].T),
        "bx": ins["L0_attn_b_x"],
        "woutT": np.ascontiguousarray(ins["L0_attn_W_out"].T),
        "bout": ins["L0_attn_b_out"],
        "alog": ins["L0_attn_A_log"],
        "dvec": ins["L0_attn_D"],
        "wrT": np.ascontiguousarray(ins["L0_ffn_W_r"].T),
        "wkT": np.ascontiguousarray(ins["L0_ffn_W_k"].T),
        "wvT": np.ascontiguousarray(ins["L0_ffn_W_v"].T),
        "mur": ins["L0_ffn_mu_r"], "muk": ins["L0_ffn_mu_k"],
        "watn": ins["L1_attn_W_attn"], "batn": ins["L1_attn_b_attn"],
        "wproj": ins["L1_attn_W_proj"], "bproj": ins["L1_attn_b_proj"],
        "wgcat": np.ascontiguousarray(
            np.concatenate([ins["L1_ffn_Wr"], ins["L1_ffn_Wg"]], axis=1)),
        "brg": np.concatenate([ins["L1_ffn_br"], ins["L1_ffn_bg"]]),
        "w1": np.ascontiguousarray(
            np.concatenate([ins["L1_ffn_W1"], ins["L1_ffn_D1"][None]], axis=0)),
        "b1": np.ascontiguousarray(
            np.concatenate([ins["L1_ffn_b1"], ins["L1_ffn_d1b"][None]], axis=0)),
        "w2": np.ascontiguousarray(
            np.concatenate([ins["L1_ffn_W2"], ins["L1_ffn_D2"][None]], axis=0)),
        "aug2": np.ascontiguousarray(
            np.concatenate([ins["L1_ffn_b2"], ins["L1_ffn_d2b"][None]], axis=0)),
        "ident": ident, "ones": ones, "onehot": onehot,
        "psel": np.ascontiguousarray(psel.transpose(2, 0, 1, 3)),
        "maskc": maskc,
    }
    x = ins["L0_attn_x"]
    in_maps = [dict(w, xin=np.ascontiguousarray(
        x[2 * c:2 * c + 2].reshape(NTOK, DM))) for c in range(ncores)]
    res = run_bass_kernel_spmd(prog, in_maps, core_ids=list(range(ncores)))
    out = np.stack([r["out"].reshape(2, SEQ, DM) for r in res.results])
    return out.reshape(16, SEQ, DM).astype(np.float32)


# version 07
# speedup vs baseline: 1.0226x; 1.0226x over previous
import sys
sys.path.insert(0, "/opt/trn_rl_repo")
import numpy as np
from contextlib import ExitStack
import concourse.bass as bass
import concourse.mybir as mybir

F32 = mybir.dt.float32
BF16 = mybir.dt.bfloat16
AF = mybir.ActivationFunctionType
ALU = mybir.AluOpType
AX = mybir.AxisListType

ENGINES = ("sync", "gpsimd", "scalar", "vector", "tensor")

NTOK = 4096
SEQ = 2048
DM = 1024


class Sem:
    __slots__ = ("name", "val", "h")

    def __init__(self, name):
        self.name = name
        self.val = 0
        self.h = None


class Prog:
    def __init__(self, nc, ctx):
        self.nc = nc
        self.ctx = ctx
        self.ops = {e: [] for e in ENGINES}
        self.sems = []
        self.waited = {e: {} for e in ENGINES}
        self.esem = {}
        for e in ("vector", "scalar"):
            self.esem[e] = self.sem(f"eo_{e}")

    @property
    def sv(self):
        return self.esem["vector"]

    @property
    def sa(self):
        return self.esem["scalar"]

    def sem(self, name):
        s = Sem(name)
        s.h = self.ctx.enter_context(self.nc.semaphore(name))
        self.sems.append(s)
        return s

    def sb(self, name, shape, dtype):
        st = self._scopes[-1] if getattr(self, "_scopes", None) else self.ctx
        return st.enter_context(self.nc.sbuf_tensor(name, shape, dtype))

    def ps(self, name, shape, dtype=F32):
        st = self._scopes[-1] if getattr(self, "_scopes", None) else self.ctx
        return st.enter_context(self.nc.psum_tensor(name, shape, dtype))

    def push_scope(self):
        if not hasattr(self, "_scopes"):
            self._scopes = []
        self._scopes.append(ExitStack())

    def pop_scope(self):
        self._scopes.pop().close()

    def join(self, *sems):
        return [(s, s.val) for s in sems]

    def soft_barrier(self):
        snaps = [(s, s.val) for s in self.sems if s.val > 0]
        for e in ENGINES:
            for s, v in snaps:
                if self.waited[e].get(s.name, -1) >= v:
                    continue
                self.waited[e][s.name] = v

                def run(engobj, _s=s, _v=v):
                    engobj.wait_ge(_s.h, _v)
                self.ops[e].append(run)

    def emit(self, eng, fn, waits=(), inc=None, n=1):
        if eng in self.esem:
            es = self.esem[eng]
            waits = list(waits) + [(es, es.val)]
            if inc is None:
                inc = es
            elif inc is not es:
                raise ValueError("vector/scalar ops must inc their engine sem")
        wl = []
        for w in waits:
            if isinstance(w, Sem):
                w = (w, w.val)
            s, v = w
            if v <= 0:
                continue
            if self.waited[eng].get(s.name, -1) >= v:
                continue
            self.waited[eng][s.name] = v
            wl.append((s, v))
        if inc is not None:
            inc.val += n
        ninc = n

        def run(engobj, _fn=fn, _wl=wl, _inc=inc, _n=ninc):
            for s, v in _wl:
                engobj.wait_ge(s.h, v)
            instr = _fn(engobj)
            if _inc is not None:
                instr.then_inc(_inc.h, _n)
            return instr

        self.ops[eng].append(run)

    def dma(self, queue, out, in_, waits=(), inc=None, slow=False):
        self.emit(queue, lambda e, o=out, i=in_, s=slow: e.dma_start(
            out=o, in_=i, allow_slow_non_contiguous=s),
            waits=waits, inc=inc, n=16)

    def barrier(self, name):
        s = self.sem(name)
        for e in ENGINES:
            self.emit(e, lambda g: g.engine_nop() if hasattr(g, "engine_nop") else g.nop(),
                      waits=(), inc=s, n=1)
        target = s.val
        for e in ENGINES:
            def run(engobj, _s=s, _v=target):
                engobj.wait_ge(_s.h, _v)
            self.ops[e].append(run)
            self.waited[e][s.name] = target

    def finalize(self, block):
        for e in ENGINES:
            ops = self.ops[e]
            if not ops:
                continue
            section = getattr(block, e if e != "sync" else "sync")

            def body(engobj, _ops=ops):
                for fn in _ops:
                    fn(engobj)
            section(body)



def emit_l0_ffn(P, rin_d, rout_d, wr_d, wk_d, wv_d, mur_d, muk_d, ident_d,
                name="f0"):
    nc = P.nc
    CH = 512
    NCH = NTOK // CH
    SUB = CH // 128

    sb, ps = P.sb, P.ps
    rin = [sb(f"{name}_rin{i}", [128, SUB, DM], F32) for i in range(2)]
    x2p = sb(f"{name}_x2p", [128, SUB, DM], BF16)
    xf = sb(f"{name}_xf", [128, 8, CH], BF16)
    xp = sb(f"{name}_xp", [128, 8, CH], BF16)
    carry = [sb(f"{name}_car{i}", [128, 8], BF16) for i in range(2)]
    xr = sb(f"{name}_xr", [128, 8, CH], BF16)
    xk = sb(f"{name}_xk", [128, 8, CH], BF16)
    t1 = sb(f"{name}_t1", [128, CH], BF16)
    ka = sb(f"{name}_ka", [128, 8, CH], BF16)
    rt = sb(f"{name}_rt", [128, SUB, 2, 512], BF16)
    sq = sb(f"{name}_sq", [128, DM], F32)
    stat = sb(f"{name}_stat", [128, 8], F32)
    krelu = sb(f"{name}_krelu", [128, 512], F32)
    vtmp = sb(f"{name}_vtmp", [128, 512], F32)
    wr = sb(f"{name}_wr", [128, 8, DM], BF16)
    wk = sb(f"{name}_wk", [128, 8, DM], BF16)
    wv = sb(f"{name}_wv", [128, 8, DM], BF16)
    mur = sb(f"{name}_mur", [128, 8], F32)
    murc = sb(f"{name}_murc", [128, 8], F32)
    muk = sb(f"{name}_muk", [128, 8], F32)
    mukc = sb(f"{name}_mukc", [128, 8], F32)
    idt = sb(f"{name}_idt", [128, 128], BF16)

    pt = [ps(f"{name}_pt{i}", [128, 128], BF16) for i in range(2)]
    pk = [ps(f"{name}_pk{i}", [128, 512]) for i in range(2)]
    pr = [ps(f"{name}_pr{i}", [128, 512]) for i in range(2)]

    s_ldw = P.sem(f"{name}_ldw")
    s_ldx = [P.sem(f"{name}_ldx{i}") for i in range(2)]
    s_st = [P.sem(f"{name}_st{i}") for i in range(2)]
    s_pe = P.sem(f"{name}_pe")
    s_act = P.sa
    s_dve = P.sv

    P.dma("gpsimd", wr[:], wr_d.rearrange("(c p) o -> p c o", p=128), inc=s_ldw)
    P.dma("gpsimd", wk[:], wk_d.rearrange("(c p) o -> p c o", p=128), inc=s_ldw)
    P.dma("gpsimd", wv[:], wv_d.rearrange("(c p) o -> p c o", p=128), inc=s_ldw)
    P.dma("gpsimd", mur[:], mur_d.rearrange("(c p) -> p c", p=128), inc=s_ldw,
          slow=True)
    P.dma("gpsimd", muk[:], muk_d.rearrange("(c p) -> p c", p=128), inc=s_ldw,
          slow=True)
    P.dma("gpsimd", idt[:], ident_d, inc=s_ldw)
    P.emit("vector", lambda e: e.tensor_scalar(murc[:], mur[:], -1.0, 1.0,
                                               op0=ALU.mult, op1=ALU.add),
           waits=[s_ldw], inc=s_dve)
    P.emit("vector", lambda e: e.tensor_scalar(mukc[:], muk[:], -1.0, 1.0,
                                               op0=ALU.mult, op1=ALU.add),
           waits=[s_ldw], inc=s_dve)
    P.emit("vector", lambda e: e.memset(carry[0][:], 0.0), inc=s_dve)
    P.emit("vector", lambda e: e.memset(carry[1][:], 0.0), inc=s_dve)
    eps = P.sb(f"{name}_eps", [128, 1], F32)
    P.emit("vector", lambda e: e.memset(eps[:], 1e-5), inc=s_dve)

    for ci in range(NCH):
        par = ci % 2
        r_c = rin[par]
        rview = rin_d[ci * CH:(ci + 1) * CH, :].rearrange("(s p) d -> p s d", p=128)
        P.dma("sync", r_c[:], rview,
              waits=[(s_st[par], s_st[par].val)], inc=s_ldx[par])
        ld_done = s_ldx[par].val

        for s in range(SUB):
            row = r_c[:, s, :]
            P.emit("vector", lambda e, row=row: e.reduce_sum(
                stat[:, 0:1], row, axis=AX.X),
                waits=[(s_ldx[par], ld_done)], inc=s_dve)
            P.emit("scalar", lambda e, row=row: e.activation(
                sq[:], row, AF.Square, accum_out=stat[:, 1:2]),
                waits=[(s_ldx[par], ld_done), (s_dve, s_dve.val)], inc=s_act)
            P.emit("vector", lambda e: e.tensor_scalar(
                stat[:, 2:3], stat[:, 0:1], 1.0 / DM, None, op0=ALU.mult),
                waits=[s_dve], inc=s_dve)
            P.emit("vector", lambda e: e.tensor_mul(
                stat[:, 3:4], stat[:, 2:3], stat[:, 2:3]), inc=s_dve)
            P.emit("vector", lambda e: e.tensor_scalar(
                stat[:, 4:5], stat[:, 1:2], 1.0 / DM, None, op0=ALU.mult),
                waits=[s_act], inc=s_dve)
            P.emit("vector", lambda e: e.tensor_sub(
                stat[:, 4:5], stat[:, 4:5], stat[:, 3:4]), inc=s_dve)
            P.emit("scalar", lambda e: e.activation(
                stat[:, 5:6], stat[:, 4:5], AF.Sqrt, bias=eps[:, 0:1]),
                waits=[s_dve], inc=s_act)
            P.emit("vector", lambda e: e.reciprocal(
                stat[:, 6:7], stat[:, 5:6]), waits=[s_act], inc=s_dve)
            P.emit("vector", lambda e, row=row, s=s: e.tensor_scalar(
                x2p[:, s, :], row, stat[:, 2:3], stat[:, 6:7],
                op0=ALU.subtract, op1=ALU.mult),
                waits=[(s_pe, s_pe.val)], inc=s_dve)
            x2_done = s_dve.val
            for c in range(8):
                tp = pt[c % 2]
                P.emit("tensor", lambda e, s=s, c=c, tp=tp: e.transpose(
                    tp[:], x2p[:, s, c * 128:(c + 1) * 128], idt[:]),
                    waits=[(s_dve, x2_done), (s_ldw, 96), (s_act, s_act.val)],
                    inc=s_pe)
                P.emit("scalar", lambda e, s=s, c=c, tp=tp: e.copy(
                    xf[:, c, s * 128:(s + 1) * 128], tp[:]),
                    waits=[s_pe], inc=s_act)
        tr_done = s_act.val

        P.emit("vector", lambda e: e.tensor_copy(xp[:, :, 1:CH], xf[:, :, 0:CH - 1]),
               waits=[(s_act, tr_done)], inc=s_dve)
        P.emit("vector", lambda e, par=par: e.tensor_copy(
            xp[:, :, 0:1], carry[par][:].rearrange("p c -> p c ()")),
            inc=s_dve)
        P.emit("vector", lambda e, par=par: e.tensor_copy(
            carry[par][:].rearrange("p c -> p c ()"), xf[:, :, CH - 1:CH]),
            inc=s_dve)
        if ci == NCH // 2 - 1:
            P.emit("vector", lambda e, par=par: e.memset(carry[(par + 1) % 2][:], 0.0),
                   inc=s_dve)

        for c in range(8):
            P.emit("vector", lambda e, c=c: e.tensor_scalar(
                t1[:], xf[:, c, :], mur[:, c:c + 1], None, op0=ALU.mult),
                inc=s_dve)
            P.emit("vector", lambda e, c=c: e.scalar_tensor_tensor(
                xr[:, c, :], xp[:, c, :], murc[:, c:c + 1], t1[:],
                op0=ALU.mult, op1=ALU.add), inc=s_dve)
            P.emit("vector", lambda e, c=c: e.tensor_scalar(
                t1[:], xf[:, c, :], muk[:, c:c + 1], None, op0=ALU.mult),
                inc=s_dve)
            P.emit("vector", lambda e, c=c: e.scalar_tensor_tensor(
                xk[:, c, :], xp[:, c, :], mukc[:, c:c + 1], t1[:],
                op0=ALU.mult, op1=ALU.add), inc=s_dve)
        mix_done = s_dve.val

        for o in range(8):
            po = pk[o % 2]
            for c in range(8):
                P.emit("tensor", lambda e, o=o, c=c, po=po: e.matmul(
                    po[:], wk[:, c, o * 128:(o + 1) * 128], xk[:, c, :],
                    start=(c == 0), stop=(c == 7)),
                    waits=[(s_dve, s_dve.val)] if c == 0 else (),
                    inc=s_pe if c == 7 else None)
            P.emit("vector", lambda e, po=po: e.tensor_scalar(
                krelu[:], po[:], 0.0, None, op0=ALU.max),
                waits=[s_pe, (s_act, s_act.val)], inc=s_dve)
            P.emit("scalar", lambda e, o=o: e.activation(
                ka[:, o, :], krelu[:], AF.Square), waits=[s_dve], inc=s_act)
        for s in range(SUB):
            for oc in range(2):
                pp = pr[(s * 2 + oc) % 2]
                for c in range(8):
                    P.emit("tensor", lambda e, s=s, oc=oc, c=c, pp=pp: e.matmul(
                        pp[:], xr[:, c, s * 128:(s + 1) * 128],
                        wr[:, c, oc * 512:(oc + 1) * 512],
                        start=(c == 0), stop=(c == 7)),
                        waits=[(s_act, s_act.val)] if c == 0 else (),
                        inc=s_pe if c == 7 else None)
                P.emit("scalar", lambda e, s=s, oc=oc, pp=pp: e.activation(
                    rt[:, s, oc, :], pp[:], AF.Sigmoid),
                    waits=[s_pe, (s_dve, s_dve.val)], inc=s_act)
        ka_done = s_act.val

        for s in range(SUB):
            for oc in range(2):
                pp = pk[(s * 2 + oc) % 2]
                for c in range(8):
                    P.emit("tensor", lambda e, s=s, oc=oc, c=c, pp=pp: e.matmul(
                        pp[:], ka[:, c, s * 128:(s + 1) * 128],
                        wv[:, c, oc * 512:(oc + 1) * 512],
                        start=(c == 0), stop=(c == 7)),
                        waits=[(s_act, ka_done), (s_dve, s_dve.val)] if c == 0 else (),
                        inc=s_pe if c == 7 else None)
                P.emit("vector", lambda e, s=s, oc=oc, pp=pp: e.tensor_mul(
                    vtmp[:], pp[:], rt[:, s, oc, :]),
                    waits=[s_pe, (s_act, s_act.val)], inc=s_dve)
                P.emit("vector", lambda e, s=s, oc=oc, r_c=r_c: e.tensor_add(
                    r_c[:, s, oc * 512:(oc + 1) * 512],
                    r_c[:, s, oc * 512:(oc + 1) * 512], vtmp[:]),
                    inc=s_dve)
        add_done = s_dve.val

        oview = rout_d[ci * CH:(ci + 1) * CH, :].rearrange("(s p) d -> p s d", p=128)
        P.dma("gpsimd", oview, r_c[:], waits=[(s_dve, add_done)], inc=s_st[par])

    fin = P.sb(f"{name}_fin", [1, 8], F32)
    P.emit("gpsimd", lambda e: e.memset(fin[:], 0.0),
           waits=[(s_st[0], s_st[0].val), (s_st[1], s_st[1].val)], inc=None)



def emit_l0_attn(P, rin_d, rout_d, w_inT_d, b_in_d, wconv_d, bconv_d, w_xT_d,
                 b_x_d, w_outT_d, b_out_d, alog_d, dvec_d, ident_d, ones_d,
                 onehot_d, name="a0"):
    nc = P.nc
    IT = 1024
    NIT = NTOK // IT
    TS = 512
    DI = 2048
    NDC = DI // 128
    NN = 16
    sb, ps = P.sb, P.ps
    s_act, s_dve = P.sa, P.sv

    ones1 = sb(f"{name}_ones", [1, 128], BF16)
    idt = sb(f"{name}_idt", [128, 128], BF16)
    oh_t = sb(f"{name}_oh", [16, NN, 128], BF16)
    at_t = sb(f"{name}_A", [128, NDC, NN], F32)
    wct = sb(f"{name}_wc", [128, NDC, 4], F32)
    bct = sb(f"{name}_bc", [128, NDC], F32)
    dvt = sb(f"{name}_dv", [128, NDC], F32)
    bint = sb(f"{name}_bin", [128, 32], F32)
    bxdt = sb(f"{name}_bxd", [128, NDC], F32)
    bxB = sb(f"{name}_bxB", [16, 1], F32)
    bxC = sb(f"{name}_bxC", [16, 1], F32)
    boutr = sb(f"{name}_bout", [1, DM], BF16)
    wxBC = sb(f"{name}_wxbc", [128, 16, 32], BF16)
    eps = sb(f"{name}_eps", [128, 1], F32)
    hcar = sb(f"{name}_hcar", [128, NDC * NN], F32)
    xh = sb(f"{name}_xh", [128, NDC, 4], F32)

    s_cst = P.sem(f"{name}_cst")
    s_ldx = [P.sem(f"{name}_ldx{i}") for i in range(2)]
    s_rst = [P.sem(f"{name}_rst{i}") for i in range(2)]
    s_win = [P.sem(f"{name}_win{i}") for i in range(2)]
    s_wx = [P.sem(f"{name}_wx{i}") for i in range(2)]
    s_wr = [P.sem(f"{name}_wr{i}") for i in range(2)]
    s_wo = P.sem(f"{name}_wo")
    s_wst = [P.sem(f"{name}_wst{i}") for i in range(2)]
    s_pe = P.sem(f"{name}_pe")
    s_pool = P.sem(f"{name}_pool")

    P.dma("gpsimd", ones1[:], ones_d, inc=s_cst)
    P.dma("gpsimd", idt[:], ident_d, inc=s_cst)
    P.dma("gpsimd", oh_t[:], onehot_d, inc=s_cst)
    P.dma("gpsimd", wct[:], wconv_d.rearrange("(c p) k -> p c k", p=128),
          inc=s_cst, slow=True)
    P.dma("gpsimd", bct[:], bconv_d.rearrange("(c p) -> p c", p=128),
          inc=s_cst, slow=True)
    P.dma("gpsimd", dvt[:], dvec_d.rearrange("(c p) -> p c", p=128),
          inc=s_cst, slow=True)
    P.dma("gpsimd", bint[:], b_in_d.rearrange("(c p) -> p c", p=128),
          inc=s_cst, slow=True)
    P.dma("gpsimd", bxdt[:], b_x_d[0:DI].rearrange("(c p) -> p c", p=128),
          inc=s_cst, slow=True)
    P.dma("gpsimd", bxB[:], b_x_d[DI:DI + 16].rearrange("(p) -> p ()"),
          inc=s_cst, slow=True)
    P.dma("gpsimd", bxC[:], b_x_d[DI + 16:DI + 32].rearrange("(p) -> p ()"),
          inc=s_cst, slow=True)
    P.dma("gpsimd", boutr[:], b_out_d.rearrange("(o d) -> o d", o=1), inc=s_cst)
    P.dma("gpsimd", wxBC[:], w_xT_d[:, DI:DI + 32].rearrange(
        "(c p) b -> p c b", p=128), inc=s_cst)
    P.push_scope()
    alt = sb(f"{name}_alt", [128, NDC, NN], F32)
    P.dma("gpsimd", alt[:], alog_d.rearrange("(c p) n -> p c n", p=128),
          inc=s_cst)
    cst_all = s_cst.val
    P.emit("scalar", lambda e: e.activation(at_t[:], alt[:], AF.Exp),
           waits=[(s_cst, cst_all)])
    P.emit("vector", lambda e: e.tensor_scalar(
        at_t[:], at_t[:], -1.0, None, op0=ALU.mult), waits=[s_act])
    a_scal_done = P.sa.val
    a_vec_done = P.sv.val
    P.pop_scope()
    P.emit("vector", lambda e: e.memset(eps[:], 1e-5))

    rin_t = [sb(f"{name}_rin{i}", [128, DM], F32) for i in range(2)]
    x1p = [sb(f"{name}_x1p{i}", [128, DM], BF16) for i in range(2)]
    xfm = sb(f"{name}_xfm", [128, 8, IT], BF16)
    stat = sb(f"{name}_stat", [128, 8], F32)
    win_t = [sb(f"{name}_win{i}", [128, 8, 128], BF16) for i in range(2)]
    wx_t = [sb(f"{name}_wxt{i}", [128, 16, 128], BF16) for i in range(2)]
    wr_t = [sb(f"{name}_wrt{i}", [128, 8, 128], BF16) for i in range(2)]
    wo_t = sb(f"{name}_wot", [128, 16, 512], BF16)
    xi_pre = sb(f"{name}_xipre", [128, 4 + IT], BF16)
    cacc = sb(f"{name}_cacc", [128, IT], F32)
    sq = cacc
    xic = sb(f"{name}_xic", [128, NDC, IT], BF16)
    sres_t = sb(f"{name}_srest", [128, TS], BF16)
    B_sb = sb(f"{name}_Bsb", [16, TS], BF16)
    C_sb = sb(f"{name}_Csb", [16, TS], BF16)
    cbc = sb(f"{name}_cbc", [128, NN, TS], BF16)
    bbc = sb(f"{name}_bbc", [128, NN, TS], BF16)
    d_t = [cacc[:, 0:TS], cacc[:, TS:2 * TS]]
    u_t = sb(f"{name}_ut", [128, TS], BF16)
    sp_t = sb(f"{name}_spt", [128, TS], F32)
    dA_t = [sb(f"{name}_dA{i}", [128, TS], F32) for i in range(2)]
    dbxa = sb(f"{name}_dbxa", [128, NN, TS], BF16)
    h_t = sb(f"{name}_h", [128, NN, TS], BF16)
    y_t = sp_t
    y2_t = sb(f"{name}_y2", [128, TS], F32)
    y3 = sb(f"{name}_y3", [128, NDC, TS], BF16)
    wt2 = [sb(f"{name}_wt2{i}", [128, 512], F32) for i in range(2)]

    pIN = [ps(f"{name}_pIN{i}", [128, 512]) for i in range(2)]
    pDX = [ps(f"{name}_pDX{i}", [128, 512]) for i in range(2)]
    pBB = [ps(f"{name}_pBB{i}", [128, TS]) for i in range(2)]
    pt = [ps(f"{name}_pt{i}", [128, 128], BF16) for i in range(2)]

    rin_free = [[(s_cst, cst_all), (s_act, a_scal_done), (s_dve, a_vec_done)]
                for _ in range(2)]
    for it in range(NIT):
        seq_start = (it % 2 == 0)
        t0 = it * IT
        if seq_start:
            P.emit("vector", lambda e: e.memset(hcar[:], 0.0))
            P.emit("vector", lambda e: e.memset(xh[:], 0.0))

        for s in range(8):
            par = s % 2
            r_c = rin_t[par]
            rview = rin_d[t0 + s * 128:t0 + (s + 1) * 128, :]
            P.dma("sync", r_c[:], rview,
                  waits=[(s_rst[par], s_rst[par].val)] + rin_free[par],
                  inc=s_ldx[par])
            ld_done = s_ldx[par].val
            oview = rout_d[t0 + s * 128:t0 + (s + 1) * 128, :]
            P.dma("gpsimd", oview, r_c[:], waits=[(s_ldx[par], ld_done)],
                  inc=s_rst[par])
            P.emit("vector", lambda e, r_c=r_c: e.reduce_sum(
                stat[:, 0:1], r_c[:], axis=AX.X), waits=[(s_ldx[par], ld_done)])
            P.emit("scalar", lambda e, r_c=r_c: e.activation(
                sq[:], r_c[:], AF.Square, accum_out=stat[:, 1:2]),
                waits=[(s_ldx[par], ld_done), (s_dve, s_dve.val)])
            P.emit("vector", lambda e: e.tensor_scalar(
                stat[:, 2:3], stat[:, 0:1], 1.0 / DM, None, op0=ALU.mult))
            P.emit("vector", lambda e: e.tensor_mul(
                stat[:, 3:4], stat[:, 2:3], stat[:, 2:3]))
            P.emit("vector", lambda e: e.tensor_scalar(
                stat[:, 4:5], stat[:, 1:2], 1.0 / DM, None, op0=ALU.mult),
                waits=[s_act])
            P.emit("vector", lambda e: e.tensor_sub(
                stat[:, 4:5], stat[:, 4:5], stat[:, 3:4]))
            P.emit("scalar", lambda e: e.activation(
                stat[:, 5:6], stat[:, 4:5], AF.Sqrt, bias=eps[:, 0:1]),
                waits=[s_dve])
            P.emit("vector", lambda e: e.reciprocal(
                stat[:, 6:7], stat[:, 5:6]), waits=[s_act])
            P.emit("vector", lambda e, r_c=r_c, par=par: e.tensor_scalar(
                x1p[par][:], r_c[:], stat[:, 2:3], stat[:, 6:7],
                op0=ALU.subtract, op1=ALU.mult),
                waits=[(s_pe, s_pe.val)])
            x1_done = s_dve.val
            rin_free[par] = [(s_dve, x1_done), (s_act, s_act.val)]
            for c in range(8):
                tp = pt[c % 2]
                P.emit("tensor", lambda e, par=par, c=c, tp=tp: e.transpose(
                    tp[:], x1p[par][:, c * 128:(c + 1) * 128], idt[:]),
                    waits=[(s_dve, x1_done), (s_cst, cst_all),
                           (s_act, s_act.val)],
                    inc=s_pe)
                P.emit("scalar", lambda e, s=s, c=c, tp=tp: e.copy(
                    xfm[:, c, s * 128:(s + 1) * 128], tp[:]),
                    waits=[s_pe])
        tr_done = s_act.val

        win_used = [0, 0]
        for dc in range(16):
            wpar = dc % 2
            P.dma("gpsimd", win_t[wpar][:],
                  w_inT_d[:, dc * 128:(dc + 1) * 128].rearrange(
                      "(c p) o -> p c o", p=128),
                  waits=[(s_pe, win_used[wpar])], inc=s_win[wpar])
            win_ok = s_win[wpar].val
            for th in range(2):
                pp = pIN[th]
                for c in range(8):
                    P.emit("tensor", lambda e, th=th, c=c, pp=pp, wpar=wpar:
                           e.matmul(pp[:], win_t[wpar][:, c, :],
                                    xfm[:, c, th * 512:(th + 1) * 512],
                                    start=(c == 0), stop=(c == 7)),
                           waits=[(s_win[wpar], win_ok), (s_act, s_act.val),
                                  (s_dve, s_dve.val)]
                           if c == 0 else (),
                           inc=s_pe if c == 7 else None)
            win_used[wpar] = s_pe.val
            P.emit("vector", lambda e, dc=dc: e.tensor_copy(
                xi_pre[:, 1:4], xh[:, dc, 1:4]), waits=[(s_act, s_act.val)])
            for th in range(2):
                P.emit("scalar", lambda e, th=th, dc=dc: e.activation(
                    xi_pre[:, 4 + th * 512:4 + (th + 1) * 512],
                    pIN[th][:], AF.Identity, bias=bint[:, dc:dc + 1]),
                    waits=[s_pe, (s_dve, s_dve.val)])
            P.emit("vector", lambda e, dc=dc: e.tensor_scalar(
                cacc[:], xi_pre[:, 1:1 + IT], wct[:, dc, 0:1], None,
                op0=ALU.mult), waits=[s_act])
            for k in range(1, 4):
                P.emit("vector", lambda e, dc=dc, k=k: e.scalar_tensor_tensor(
                    cacc[:], xi_pre[:, 1 + k:1 + k + IT],
                    wct[:, dc, k:k + 1], cacc[:],
                    op0=ALU.mult, op1=ALU.add))
            P.emit("scalar", lambda e, dc=dc: e.activation(
                xic[:, dc, :], cacc[:], AF.Silu, bias=bct[:, dc:dc + 1]),
                waits=[s_dve])
            P.emit("vector", lambda e, dc=dc: e.tensor_copy(
                xh[:, dc, 1:4], xi_pre[:, 1 + IT:4 + IT]))

        pdx_free = [[], []]
        pbb_free = [[], []]
        for su in range(2):
            ts0 = su * TS
            for c in range(16):
                P.emit("tensor", lambda e, c=c: e.matmul(
                    pBB[0][0:16, :], wxBC[:, c, 0:16], xic[:, c, ts0:ts0 + TS],
                    start=(c == 0), stop=(c == 15)),
                    waits=[(s_act, s_act.val)] + pbb_free[0]
                    if c == 0 else (),
                    inc=s_pe if c == 15 else None)
            for c in range(16):
                P.emit("tensor", lambda e, c=c: e.matmul(
                    pBB[1][0:16, :], wxBC[:, c, 16:32], xic[:, c, ts0:ts0 + TS],
                    start=(c == 0), stop=(c == 15)),
                    waits=pbb_free[1] if c == 0 else (),
                    inc=s_pe if c == 15 else None)
            P.emit("vector", lambda e: e.tensor_scalar(
                B_sb[:], pBB[0][0:16, :], bxB[:, 0:1], None, op0=ALU.add),
                waits=[s_pe])
            pbb_free[0] = [(s_dve, s_dve.val)]
            P.emit("vector", lambda e: e.tensor_scalar(
                C_sb[:], pBB[1][0:16, :], bxC[:, 0:1], None, op0=ALU.add))
            pbb_free[1] = [(s_dve, s_dve.val)]
            bc_done = s_dve.val
            for n in range(NN):
                pp = pBB[n % 2]
                P.emit("tensor", lambda e, n=n, pp=pp: e.matmul(
                    pp[:], oh_t[:, n, :], C_sb[:], start=True, stop=True),
                    waits=[(s_dve, bc_done)] + pbb_free[n % 2], inc=s_pe)
                P.emit("scalar", lambda e, n=n, pp=pp: e.copy(
                    cbc[:, n, :], pp[:]),
                    waits=[s_pe, (s_dve, s_dve.val), (s_pool, s_pool.val)])
                pbb_free[n % 2] = [(s_act, s_act.val)]
            for n in range(NN):
                pp = pBB[n % 2]
                P.emit("tensor", lambda e, n=n, pp=pp: e.matmul(
                    pp[:], oh_t[:, n, :], B_sb[:], start=True, stop=True),
                    waits=pbb_free[n % 2], inc=s_pe)
                P.emit("scalar", lambda e, n=n, pp=pp: e.copy(
                    bbc[:, n, :], pp[:]), waits=[s_pe, (s_dve, s_dve.val)])
                pbb_free[n % 2] = [(s_act, s_act.val)]
            cbc_done = s_act.val

            wx_used = [0, 0]
            wr_used = [0, 0]
            wx_ok = [0, 0]
            wr_ok = [0, 0]
            zc_prev = 0
            P.dma("gpsimd", wx_t[0][:],
                  w_xT_d[:, 0:128].rearrange("(c p) o -> p c o", p=128),
                  waits=[(s_pe, wx_used[0])], inc=s_wx[0])
            wx_ok[0] = s_wx[0].val
            P.dma("gpsimd", wr_t[0][:],
                  w_inT_d[:, 16 * 128:17 * 128].rearrange(
                      "(c p) o -> p c o", p=128),
                  waits=[(s_pe, wr_used[0])], inc=s_wr[0])
            wr_ok[0] = s_wr[0].val
            for dc in range(NDC):
                wpar = dc % 2
                pp = pDX[wpar]
                for c in range(16):
                    P.emit("tensor", lambda e, c=c, pp=pp, wpar=wpar:
                           e.matmul(pp[:], wx_t[wpar][:, c, :],
                                    xic[:, c, ts0:ts0 + TS],
                                    start=(c == 0), stop=(c == 15)),
                           waits=[(s_wx[wpar], wx_ok[wpar])] + pdx_free[wpar]
                           if c == 0 else (),
                           inc=s_pe if c == 15 else None)
                wx_used[wpar] = s_pe.val
                dt_c = d_t[wpar]
                P.emit("vector", lambda e, dc=dc, pp=pp: e.tensor_scalar(
                    sp_t[:], pp[:], bxdt[:, dc:dc + 1], None,
                    op0=ALU.add), waits=[s_pe, (s_act, s_act.val)])
                pdx_free[wpar] = [(s_dve, s_dve.val)]
                P.emit("vector", lambda e: e.scalar_tensor_tensor(
                    y2_t[:], sp_t[:], -1.0, sp_t[:],
                    op0=ALU.mult, op1=ALU.max))
                P.emit("scalar", lambda e: e.activation(
                    y2_t[:], y2_t[:], AF.Exp, scale=-1.0),
                    waits=[s_dve])
                P.emit("scalar", lambda e: e.activation(
                    y2_t[:], y2_t[:], AF.Ln, bias=1.0))
                P.emit("vector", lambda e, dt_c=dt_c: e.scalar_tensor_tensor(
                    dt_c, sp_t[:], 0.0, y2_t[:],
                    op0=ALU.max, op1=ALU.add), waits=[s_act])
                P.emit("vector", lambda e, dc=dc, dt_c=dt_c:
                       e.tensor_mul(u_t[:], dt_c, xic[:, dc, ts0:ts0 + TS]))
                P.emit("vector", lambda e: e.tensor_mul(
                    dbxa[:], u_t[:, None, :].broadcast_to([128, NN, TS]),
                    bbc[:]), waits=[(s_act, cbc_done)])
                dbx_done = s_dve.val
                for n in range(NN):
                    npar = n % 2
                    if n == 3 and dc + 1 < NDC:
                        np_ = (dc + 1) % 2
                        P.dma("gpsimd", wx_t[np_][:],
                              w_xT_d[:, (dc + 1) * 128:(dc + 2) * 128].rearrange(
                                  "(c p) o -> p c o", p=128),
                              waits=[(s_pe, wx_used[np_])], inc=s_wx[np_])
                        wx_ok[np_] = s_wx[np_].val
                        P.dma("gpsimd", wr_t[np_][:],
                              w_inT_d[:, (17 + dc) * 128:(18 + dc) * 128].rearrange(
                                  "(c p) o -> p c o", p=128),
                              waits=[(s_pe, wr_used[np_])], inc=s_wr[np_])
                        wr_ok[np_] = s_wr[np_].val
                    P.emit("scalar", lambda e, n=n, dc=dc, dt_c=dt_c, npar=npar:
                           e.activation(dA_t[npar][:], dt_c, AF.Exp,
                                        scale=at_t[:, dc, n:n + 1]),
                           waits=[(s_dve, max(dbx_done, s_dve.val - 1))])
                    P.emit("vector", lambda e, n=n, npar=npar, dc=dc:
                           e.tensor_tensor_scan(
                               h_t[:, n, :], dA_t[npar][:], dbxa[:, n, :],
                               hcar[:, dc * NN + n:dc * NN + n + 1],
                               op0=ALU.mult, op1=ALU.add),
                           waits=[s_act, (s_pool, zc_prev)])
                    P.emit("gpsimd", lambda e, n=n: e.tensor_mul(
                        dbxa[:, n, :], h_t[:, n, :], cbc[:, n, :]),
                        waits=[(s_dve, s_dve.val), (s_act, cbc_done),
                               (s_pool, s_pool.val)], inc=s_pool)
                zc_prev = s_pool.val
                P.emit("vector", lambda e, dc=dc: e.tensor_copy(
                    hcar[:, dc * NN:(dc + 1) * NN],
                    h_t[:, :, TS - 1]))
                P.emit("vector", lambda e: e.tensor_reduce(
                    y_t[:, :, None], dbxa[:].rearrange("p n t -> p t n"),
                    axis=AX.X, op=ALU.add),
                    waits=[(s_pool, s_pool.val)])
                P.emit("vector", lambda e, dc=dc, ts0=ts0: e.scalar_tensor_tensor(
                    y2_t[:], xic[:, dc, ts0:ts0 + TS], dvt[:, dc:dc + 1],
                    y_t[:], op0=ALU.mult, op1=ALU.add))
                pr2 = pIN[wpar]
                for c in range(8):
                    P.emit("tensor", lambda e, c=c, pr2=pr2, wpar=wpar, ts0=ts0:
                           e.matmul(pr2[:], wr_t[wpar][:, c, :],
                                    xfm[:, c, ts0:ts0 + TS],
                                    start=(c == 0), stop=(c == 7)),
                           waits=[(s_wr[wpar], wr_ok[wpar]), (s_act, s_act.val),
                                  (s_dve, s_dve.val)]
                           if c == 0 else (),
                           inc=s_pe if c == 7 else None)
                wr_used[wpar] = s_pe.val
                P.emit("scalar", lambda e, dc=dc, pr2=pr2: e.activation(
                    sres_t[:], pr2[:], AF.Silu, bias=bint[:, 16 + dc:17 + dc]),
                    waits=[s_pe, (s_dve, s_dve.val)])
                P.emit("vector", lambda e, dc=dc: e.tensor_mul(
                    y3[:, dc, :], y2_t[:], sres_t[:]),
                    waits=[(s_pe, s_pe.val), (s_act, s_act.val)])
            y3_done = s_dve.val

            for ocol in range(2):
                P.dma("gpsimd", wo_t[:],
                      w_outT_d[:, ocol * 512:(ocol + 1) * 512].rearrange(
                          "(c p) o -> p c o", p=128),
                      waits=[(s_pe, s_pe.val)], inc=s_wo)
                wo_ok = s_wo.val
                for tsu in range(4):
                    pp = pIN[tsu % 2]
                    for c in range(16):
                        P.emit("tensor",
                               lambda e, tsu=tsu, c=c, pp=pp:
                               e.matmul(pp[:], y3[:, c, tsu * 128:(tsu + 1) * 128],
                                        wo_t[:, c, :],
                                        start=(c == 0), stop=False),
                               waits=[(s_dve, y3_done), (s_wo, wo_ok),
                                      (s_act, s_act.val),
                                      (s_dve, s_dve.val)]
                               if c == 0 else ())
                    P.emit("tensor", lambda e, ocol=ocol, pp=pp: e.matmul(
                        pp[:], ones1[:], boutr[0:1, ocol * 512:(ocol + 1) * 512],
                        start=False, stop=True),
                        waits=[(s_cst, cst_all)], inc=s_pe)
                    wp = tsu % 2
                    P.emit("vector", lambda e, pp=pp, wp=wp: e.tensor_copy(
                        wt2[wp][:], pp[:]),
                        waits=[s_pe, (s_wst[wp], s_wst[wp].val)])
                    r0 = t0 + su * TS + tsu * 128
                    P.emit("gpsimd", lambda e, wp=wp, r0=r0, ocol=ocol:
                           e.dma_start(
                               out=rout_d[r0:r0 + 128,
                                          ocol * 512:(ocol + 1) * 512],
                               in_=wt2[wp][:], accum_op=ALU.add),
                           waits=[(s_dve, s_dve.val),
                                  (s_rst[0], s_rst[0].val),
                                  (s_rst[1], s_rst[1].val)],
                           inc=s_wst[wp], n=16)

    fin = sb(f"{name}_fin", [1, 8], F32)
    P.emit("gpsimd", lambda e: e.memset(fin[:], 0.0),
           waits=[(s_wst[0], s_wst[0].val), (s_wst[1], s_wst[1].val),
                  (s_rst[0], s_rst[0].val), (s_rst[1], s_rst[1].val)])


def build_l0_attn_program(dbg_stop=None):
    nc = bass.Bass("TRN2", target_bir_lowering=False)
    rin_d = nc.dram_tensor("rin", [NTOK, DM], F32, kind="ExternalInput").ap()
    w_inT_d = nc.dram_tensor("winT", [DM, 4096], F32, kind="ExternalInput").ap()
    b_in_d = nc.dram_tensor("bin", [4096], F32, kind="ExternalInput").ap()
    wconv_d = nc.dram_tensor("wconv", [2048, 4], F32, kind="ExternalInput").ap()
    bconv_d = nc.dram_tensor("bconv", [2048], F32, kind="ExternalInput").ap()
    w_xT_d = nc.dram_tensor("wxT", [2048, 2080], F32, kind="ExternalInput").ap()
    b_x_d = nc.dram_tensor("bx", [2080], F32, kind="ExternalInput").ap()
    w_outT_d = nc.dram_tensor("woutT", [2048, DM], F32, kind="ExternalInput").ap()
    b_out_d = nc.dram_tensor("bout", [DM], F32, kind="ExternalInput").ap()
    alog_d = nc.dram_tensor("alog", [2048, 16], F32, kind="ExternalInput").ap()
    dvec_d = nc.dram_tensor("dvec", [2048], F32, kind="ExternalInput").ap()
    ident_d = nc.dram_tensor("ident", [128, 128], F32, kind="ExternalInput").ap()
    ones_d = nc.dram_tensor("ones", [1, 128], F32, kind="ExternalInput").ap()
    onehot_d = nc.dram_tensor("onehot", [16, 16, 128], F32,
                              kind="ExternalInput").ap()
    rout_d = nc.dram_tensor("rout", [NTOK, DM], F32, kind="ExternalOutput").ap()

    with ExitStack() as ctx:
        P = Prog(nc, ctx)
        block = ctx.enter_context(nc.Block())
        emit_l0_attn(P, rin_d, rout_d, w_inT_d, b_in_d, wconv_d, bconv_d,
                     w_xT_d, b_x_d, w_outT_d, b_out_d, alog_d, dvec_d,
                     ident_d, ones_d, onehot_d)
        P.finalize(block)
    return nc



def poolsel_chunks(sb):
    out = {}

    def add(chunk, row_in_chunk, col, w, span):
        m = out.setdefault(chunk, np.zeros((128, 40), np.float32))
        m[row_in_chunk:row_in_chunk + span, col] = w
    for r in range(16):
        pl = 16 * sb - 8 + r
        if pl < 0:
            continue
        add(pl * 8 // 128, (pl * 8) % 128, r, 1 / 8, 8)
    for r in range(8):
        pl = 8 * (sb - 1) + r
        if pl < 0:
            continue
        add(pl * 16 // 128, (pl * 16) % 128, 16 + r, 1 / 16, 16)
    for r in range(8):
        pl = 8 * (sb // 2 - 1) + r
        if pl < 0:
            continue
        add(pl * 32 // 128, (pl * 32) % 128, 24 + r, 1 / 32, 32)
    for r in range(8):
        pl = 8 * (sb // 4 - 1) + r
        if pl < 0:
            continue
        c0 = pl * 64 // 128
        add(c0, (pl * 64) % 128, 32 + r, 1 / 64, 64)
    return out


def build_l1_attn_consts():
    slots = []
    for sb in range(16):
        slots.append(sorted(poolsel_chunks(sb).items()))
    nslot = max(len(s) for s in slots)
    psel = np.zeros((16, nslot, 128, 40), np.float32)
    for sb, items in enumerate(slots):
        for i, (cu, m) in enumerate(items):
            psel[sb, i] = m
    mask = np.full((128, 232), -1e30, np.float32)
    for q in range(128):
        mask[q, q + 1:q + 65] = 0.0
        if q < 64:
            mask[q, 192:200] = np.log(8.0)
        else:
            mask[q, 200:208] = np.log(8.0)
    mask[:, 208:216] = np.log(16.0)
    mask[:, 216:224] = np.log(32.0)
    mask[:, 224:232] = np.log(64.0)
    return psel, mask


def emit_l1_attn(P, rin_d, rout_d, watn_d, batn_d, wproj_d, bproj_d, ident_d,
                 ones_d, psel_d, mask_d, nslot, name="t1"):
    nc = P.nc
    sb_, ps = P.sb, P.ps
    s_act, s_dve = P.sa, P.sv
    DH = 64

    ones1 = sb_(f"{name}_ones", [1, 128], BF16)
    idt = sb_(f"{name}_idt", [128, 128], BF16)
    mask = sb_(f"{name}_mask", [128, 232], F32)
    psel = sb_(f"{name}_psel", [128, 16, nslot, 40], BF16)
    batn = sb_(f"{name}_batn", [128, 24], F32)
    batq = sb_(f"{name}_batq", [128, 8], F32)
    bprr = sb_(f"{name}_bpr", [1, DM], BF16)
    eps = sb_(f"{name}_eps", [128, 1], F32)

    qt = sb_(f"{name}_qt", [128, 8, SEQ], BF16)
    kt = sb_(f"{name}_kt", [128, 8, 64 + SEQ + 64], BF16)
    vtm = sb_(f"{name}_vtm", [128, 8, 16, 128], BF16)
    kp = sb_(f"{name}_kp", [128, 8, 512], BF16)
    otm = sb_(f"{name}_otm", [128, 16, DM], BF16)
    wat = [sb_(f"{name}_wat{i}", [128, 8, 128], BF16) for i in range(2)]
    wpj = sb_(f"{name}_wpj", [128, 8, DM], BF16)
    rin2 = [sb_(f"{name}_rin{i}", [128, DM], F32) for i in range(2)]
    x3p = [sb_(f"{name}_x3p{i}", [128, DM], BF16) for i in range(2)]
    x3fm = sb_(f"{name}_x3fm", [128, 8, 512], BF16)
    sq = sb_(f"{name}_sq", [128, DM], BF16)
    stat = sb_(f"{name}_stat", [128, 8], F32)
    vfm = sb_(f"{name}_vfm", [128, 512], BF16)
    prt = sb_(f"{name}_prt", [128, 256], F32)
    Sm = [sb_(f"{name}_Sm{i}", [128, 232], F32) for i in range(2)]
    Pp = [sb_(f"{name}_Pp{i}", [128, 232], BF16) for i in range(2)]
    pta = [sb_(f"{name}_pta{i}", [64, 128], BF16) for i in range(2)]
    ptb = [sb_(f"{name}_ptb{i}", [64, 128], BF16) for i in range(2)]
    ptc = [sb_(f"{name}_ptc{i}", [64, 128], BF16) for i in range(2)]
    pt3s = [sb_(f"{name}_pt3s{i}", [40, 128], BF16) for i in range(2)]
    vca = [sb_(f"{name}_vca{i}", [64, 64], BF16) for i in range(2)]
    vcb = [sb_(f"{name}_vcb{i}", [64, 64], BF16) for i in range(2)]
    vps = [sb_(f"{name}_vps{i}", [40, 64], BF16) for i in range(2)]
    ofm = sb_(f"{name}_ofm", [128, 8, 128], BF16)
    wt2 = [sb_(f"{name}_wt2{i}", [128, 512], F32) for i in range(2)]

    pS = [ps(f"{name}_pS{i}", [128, 512]) for i in range(2)]
    pO = [ps(f"{name}_pO{i}", [128, 512]) for i in range(2)]
    pPool = [ps(f"{name}_pPool{i}", [40, 64]) for i in range(2)]
    pT = [ps(f"{name}_pT{i}", [128, 512], BF16) for i in range(2)]

    s_cst = P.sem(f"{name}_cst")
    s_ldx = [P.sem(f"{name}_ldx{i}") for i in range(2)]
    s_rst = [P.sem(f"{name}_rst{i}") for i in range(2)]
    s_wa = [P.sem(f"{name}_wa{i}") for i in range(2)]
    s_wp = P.sem(f"{name}_wp")
    s_wst = [P.sem(f"{name}_wst{i}") for i in range(2)]
    s_pe = P.sem(f"{name}_pe")
    s_pool = P.sem(f"{name}_pool")

    P.dma("gpsimd", ones1[:], ones_d, inc=s_cst)
    P.dma("gpsimd", idt[:], ident_d, inc=s_cst)
    P.dma("gpsimd", mask[:], mask_d, inc=s_cst)
    P.dma("gpsimd", psel[:], psel_d, inc=s_cst)
    P.dma("gpsimd", batn[:], batn_d.rearrange("(c p) -> p c", p=128),
          inc=s_cst, slow=True)
    P.dma("gpsimd", bprr[:], bproj_d.rearrange("(o d) -> o d", o=1), inc=s_cst)
    cst_all = s_cst.val
    P.emit("vector", lambda e: e.memset(eps[:], 1e-5))
    P.emit("vector", lambda e: e.tensor_scalar(
        batq[:], batn[:, 0:8], 0.125, None, op0=ALU.mult),
        waits=[(s_cst, cst_all)])

    rin_free = [[], []]
    for sq_i in range(2):
        t0 = sq_i * SEQ
        P.emit("vector", lambda e: e.memset(kt[:, :, 0:64], 0.0),
               waits=[(s_pe, s_pe.val)])
        P.emit("vector", lambda e: e.memset(kt[:, :, 64 + SEQ:], 0.0))
        P.emit("vector", lambda e: e.memset(kp[:, :, 0:8], 0.0))
        P.emit("vector", lambda e: e.memset(kp[:, :, 264:272], 0.0))
        P.emit("vector", lambda e: e.memset(kp[:, :, 400:408], 0.0))
        P.emit("vector", lambda e: e.memset(kp[:, :, 472:480], 0.0))
        zero_done = s_dve.val

        for ch in range(4):
            c0 = t0 + ch * 512
            for s in range(4):
                par = s % 2
                r_c = rin2[par]
                P.dma("sync", r_c[:], rin_d[c0 + s * 128:c0 + (s + 1) * 128, :],
                      waits=[(s_rst[par], s_rst[par].val)] + rin_free[par],
                      inc=s_ldx[par])
                ld_done = s_ldx[par].val
                P.dma("gpsimd", rout_d[c0 + s * 128:c0 + (s + 1) * 128, :],
                      r_c[:], waits=[(s_ldx[par], ld_done)], inc=s_rst[par])
                P.emit("vector", lambda e, r_c=r_c: e.reduce_sum(
                    stat[:, 0:1], r_c[:], axis=AX.X),
                    waits=[(s_ldx[par], ld_done)])
                P.emit("scalar", lambda e, r_c=r_c: e.activation(
                    sq[:], r_c[:], AF.Square, accum_out=stat[:, 1:2]),
                    waits=[(s_ldx[par], ld_done), (s_dve, s_dve.val)])
                P.emit("vector", lambda e: e.tensor_scalar(
                    stat[:, 2:3], stat[:, 0:1], 1.0 / DM, None, op0=ALU.mult))
                P.emit("vector", lambda e: e.tensor_mul(
                    stat[:, 3:4], stat[:, 2:3], stat[:, 2:3]))
                P.emit("vector", lambda e: e.tensor_scalar(
                    stat[:, 4:5], stat[:, 1:2], 1.0 / DM, None, op0=ALU.mult),
                    waits=[s_act])
                P.emit("vector", lambda e: e.tensor_sub(
                    stat[:, 4:5], stat[:, 4:5], stat[:, 3:4]))
                P.emit("scalar", lambda e: e.activation(
                    stat[:, 5:6], stat[:, 4:5], AF.Sqrt, bias=eps[:, 0:1]),
                    waits=[s_dve])
                P.emit("vector", lambda e: e.reciprocal(
                    stat[:, 6:7], stat[:, 5:6]), waits=[s_act])
                P.emit("vector", lambda e, r_c=r_c, par=par: e.tensor_scalar(
                    x3p[par][:], r_c[:], stat[:, 2:3], stat[:, 6:7],
                    op0=ALU.subtract, op1=ALU.mult),
                    waits=[(s_pe, s_pe.val)])
                x3_done = s_dve.val
                rin_free[par] = [(s_dve, x3_done), (s_act, s_act.val)]
                for c in range(8):
                    tp_ = pT[c % 2]
                    P.emit("tensor", lambda e, par=par, c=c, tp_=tp_: e.transpose(
                        tp_[:, 0:128], x3p[par][:, c * 128:(c + 1) * 128], idt[:]),
                        waits=[(s_dve, x3_done), (s_cst, cst_all),
                               (s_act, s_act.val)],
                        inc=s_pe)
                    P.emit("scalar", lambda e, s=s, c=c, tp_=tp_: e.copy(
                        x3fm[:, c, s * 128:(s + 1) * 128], tp_[:, 0:128]),
                        waits=[s_pe])
            x3_ok = s_act.val
            wa_used = [0, 0]
            for o in range(24):
                wpar = o % 2
                P.dma("gpsimd", wat[wpar][:],
                      watn_d[:, o * 128:(o + 1) * 128].rearrange(
                          "(c p) o -> p c o", p=128),
                      waits=[(s_pe, wa_used[wpar])], inc=s_wa[wpar])
                wa_ok = s_wa[wpar].val
                pp = pS[o % 2]
                for c in range(8):
                    P.emit("tensor", lambda e, o=o, c=c, pp=pp, wpar=wpar:
                           e.matmul(pp[:], wat[wpar][:, c, :], x3fm[:, c, :],
                                    start=(c == 0), stop=(c == 7)),
                           waits=[(s_wa[wpar], wa_ok), (s_act, x3_ok),
                                  (s_act, s_act.val), (s_dve, s_dve.val)]
                           if c == 0 else (),
                           inc=s_pe if c == 7 else None)
                wa_used[wpar] = s_pe.val
                if o < 8:
                    P.emit("scalar", lambda e, o=o, ch=ch, pp=pp: e.activation(
                        qt[:, o, ch * 512:(ch + 1) * 512], pp[:], AF.Identity,
                        bias=batq[:, o:o + 1], scale=0.125),
                        waits=[s_pe, (s_dve, s_dve.val)])
                elif o < 16:
                    P.emit("scalar", lambda e, o=o, ch=ch, pp=pp: e.activation(
                        kt[:, o - 8, 64 + ch * 512:64 + (ch + 1) * 512], pp[:],
                        AF.Identity, bias=batn[:, o:o + 1]),
                        waits=[s_pe, (s_dve, zero_done), (s_dve, s_dve.val)])
                else:
                    hp = o - 16
                    P.emit("scalar", lambda e, o=o, pp=pp: e.activation(
                        vfm[:], pp[:], AF.Identity, bias=batn[:, o:o + 1]),
                        waits=[s_pe, (s_dve, s_dve.val)])
                    for b in range(4):
                        su = ch * 4 + b
                        tp_ = pT[b % 2]
                        P.emit("tensor", lambda e, b=b, tp_=tp_: e.transpose(
                            tp_[:, 0:128], vfm[:, b * 128:(b + 1) * 128], idt[:]),
                            waits=[(s_act, s_act.val), (s_dve, s_dve.val)],
                            inc=s_pe)
                        P.emit("scalar", lambda e, hp=hp, su=su, tp_=tp_: e.copy(
                            vtm[:, hp, su, :], tp_[:, 0:128]), waits=[s_pe])
        pk_done0 = s_dve.val
        for hp in range(8):
            kv = kt[:, hp, 64:64 + SEQ]
            P.emit("vector", lambda e, kv=kv: e.tensor_reduce(
                prt[:, 0:256, None], kv.rearrange("p (a b) -> p a b", b=8),
                axis=AX.X, op=ALU.add), waits=[(s_act, s_act.val)])
            P.emit("vector", lambda e, hp=hp: e.tensor_scalar(
                kp[:, hp, 8:264], prt[:, 0:256], 1 / 8.0, None,
                op0=ALU.mult))
            P.emit("vector", lambda e, hp=hp: e.tensor_reduce(
                prt[:, 0:128, None],
                kp[:, hp, 8:264].rearrange("p (a b) -> p a b", b=2),
                axis=AX.X, op=ALU.add))
            P.emit("vector", lambda e, hp=hp: e.tensor_scalar(
                kp[:, hp, 272:400], prt[:, 0:128], 0.5, None,
                op0=ALU.mult))
            P.emit("vector", lambda e, hp=hp: e.tensor_reduce(
                prt[:, 0:64, None],
                kp[:, hp, 272:400].rearrange("p (a b) -> p a b", b=2),
                axis=AX.X, op=ALU.add))
            P.emit("vector", lambda e, hp=hp: e.tensor_scalar(
                kp[:, hp, 408:472], prt[:, 0:64], 0.5, None,
                op0=ALU.mult))
            P.emit("vector", lambda e, hp=hp: e.tensor_reduce(
                prt[:, 0:32, None],
                kp[:, hp, 408:472].rearrange("p (a b) -> p a b", b=2),
                axis=AX.X, op=ALU.add))
            P.emit("vector", lambda e, hp=hp: e.tensor_scalar(
                kp[:, hp, 480:512], prt[:, 0:32], 0.5, None,
                op0=ALU.mult))
        kp_done = s_dve.val

        pS_free = [[], []]
        pO_free = [[], []]
        pPool_free = [[], []]
        pT_free = [[], []]
        vc_free = [[], []]
        sm_free = [[], []]
        for hp in range(8):
            for hd in range(2):
                hb = hd * 64
                for sb in range(16):
                    ip = sb % 2
                    pp = pS[ip]
                    q_l = qt[hb:hb + 64, hp, sb * 128:(sb + 1) * 128]
                    P.emit("tensor", lambda e, pp=pp, q_l=q_l, hp=hp, hb=hb,
                           sb=sb: e.matmul(
                               pp[:, 0:192], q_l,
                               kt[hb:hb + 64, hp, sb * 128:sb * 128 + 192],
                               start=True, stop=True),
                           waits=[(s_dve, kp_done)] + pS_free[ip])
                    P.emit("tensor", lambda e, pp=pp, q_l=q_l, hp=hp, hb=hb,
                           sb=sb: e.matmul(
                               pp[:, 192:208], q_l,
                               kp[hb:hb + 64, hp, 16 * sb:16 * sb + 16],
                               start=True, stop=True))
                    P.emit("tensor", lambda e, pp=pp, q_l=q_l, hp=hp, hb=hb,
                           sb=sb: e.matmul(
                               pp[:, 208:216], q_l,
                               kp[hb:hb + 64, hp, 264 + 8 * sb:272 + 8 * sb],
                               start=True, stop=True))
                    P.emit("tensor", lambda e, pp=pp, q_l=q_l, hp=hp, hb=hb,
                           sb=sb: e.matmul(
                               pp[:, 216:224], q_l,
                               kp[hb:hb + 64, hp,
                                  400 + 8 * (sb // 2):408 + 8 * (sb // 2)],
                               start=True, stop=True))
                    P.emit("tensor", lambda e, pp=pp, q_l=q_l, hp=hp, hb=hb,
                           sb=sb: e.matmul(
                               pp[:, 224:232], q_l,
                               kp[hb:hb + 64, hp,
                                  472 + 8 * (sb // 4):480 + 8 * (sb // 4)],
                               start=True, stop=True), inc=s_pe)
                    s_done = s_pe.val
                    items = sorted(poolsel_chunks(sb).items())
                    ni = len(items)
                    ppl = pPool[ip]
                    for i, (cu, _m) in enumerate(items):
                        P.emit("tensor", lambda e, sb=sb, i=i, cu=cu, hp=hp,
                               hb=hb, ni=ni, ppl=ppl: e.matmul(
                                   ppl[:], psel[:, sb, i, :],
                                   vtm[:, hp, cu, hb:hb + 64],
                                   start=(i == 0), stop=(i == ni - 1)),
                               waits=pPool_free[ip] if i == 0 else (),
                               inc=s_pe if i == ni - 1 else None)
                    P.emit("scalar", lambda e, ppl=ppl, ip=ip: e.copy(
                        vps[ip][:], ppl[:]), waits=[s_pe] + vc_free[ip])
                    pPool_free[ip] = [(s_act, s_act.val)]
                    smt = Sm[ip]
                    P.emit("vector", lambda e, smt=smt, pp=pp: e.tensor_add(
                        smt[:], pp[:, 0:232], mask[:]),
                        waits=[(s_pe, s_done)] + sm_free[ip])
                    pS_free[ip] = [(s_dve, s_dve.val)]
                    P.emit("vector", lambda e, smt=smt: e.reduce_max(
                        stat[:, 0:1], smt[:], axis=AX.X))
                    P.emit("vector", lambda e: e.tensor_scalar(
                        stat[:, 1:2], stat[:, 0:1], -1.0, None, op0=ALU.mult),
                        waits=[(s_act, s_act.val)])
                    P.emit("scalar", lambda e, smt=smt: e.activation(
                        smt[:], smt[:], AF.Exp, bias=stat[:, 1:2]),
                        waits=[s_dve])
                    P.emit("vector", lambda e, smt=smt: e.reduce_sum(
                        stat[:, 2:3], smt[:], axis=AX.X), waits=[s_act])
                    P.emit("vector", lambda e: e.reciprocal(
                        stat[:, 3:4], stat[:, 2:3]))
                    ppt = Pp[ip]
                    P.emit("vector", lambda e, smt=smt, ppt=ppt: e.tensor_scalar(
                        ppt[:], smt[:], stat[:, 3:4], None, op0=ALU.mult),
                        waits=pT_free[ip])
                    sm_free[ip] = [(s_dve, s_dve.val)]
                    p_done = s_dve.val
                    tp_ = pT[ip]
                    P.emit("tensor", lambda e, ppt=ppt, tp_=tp_: e.transpose(
                        tp_[0:64, 0:128], ppt[:, 0:64], idt[:]),
                        waits=[(s_dve, p_done)] + pT_free[ip], inc=s_pe)
                    P.emit("tensor", lambda e, ppt=ppt, tp_=tp_: e.transpose(
                        tp_[0:64, 128:256], ppt[:, 64:128], idt[:]), inc=s_pe)
                    P.emit("tensor", lambda e, ppt=ppt, tp_=tp_: e.transpose(
                        tp_[0:64, 256:384], ppt[:, 128:192], idt[:]), inc=s_pe)
                    P.emit("tensor", lambda e, ppt=ppt, tp_=tp_: e.transpose(
                        tp_[0:40, 384:512], ppt[:, 192:232], idt[:]), inc=s_pe)
                    tr_pe = s_pe.val
                    P.emit("scalar", lambda e, ip=ip, tp_=tp_: e.copy(
                        pta[ip][:], tp_[0:64, 0:128]),
                        waits=[(s_pe, tr_pe)])
                    P.emit("scalar", lambda e, ip=ip, tp_=tp_: e.copy(
                        ptb[ip][:], tp_[0:64, 128:256]))
                    P.emit("scalar", lambda e, ip=ip, tp_=tp_: e.copy(
                        ptc[ip][:], tp_[0:64, 256:384]))
                    P.emit("scalar", lambda e, ip=ip, tp_=tp_: e.copy(
                        pt3s[ip][:], tp_[0:40, 384:512]))
                    tr_ok = s_act.val
                    if sb > 0:
                        P.emit("vector", lambda e, hp=hp, hb=hb, sb=sb, ip=ip:
                               e.tensor_copy(vca[ip][:],
                                             vtm[64:128, hp, sb - 1,
                                                 hb:hb + 64]),
                               waits=pO_free[ip])
                    P.emit("vector", lambda e, hp=hp, hb=hb, sb=sb, ip=ip:
                           e.tensor_copy(vcb[ip][:], vtm[64:128, hp, sb,
                                                         hb:hb + 64]),
                           waits=pO_free[ip])
                    vc_ok = s_dve.val
                    vc_free[ip] = [(s_dve, vc_ok)]
                    po = pO[ip]
                    first = True
                    if sb > 0:
                        P.emit("tensor", lambda e, po=po, ip=ip:
                               e.matmul(po[:, 0:64], pta[ip][:], vca[ip][:],
                                        start=True, stop=False),
                               waits=[(s_act, tr_ok), (s_dve, vc_ok)] +
                               pO_free[ip])
                        first = False
                    P.emit("tensor", lambda e, po=po, hp=hp, hb=hb, sb=sb,
                           ip=ip, first=first: e.matmul(
                               po[:, 0:64], ptb[ip][:],
                               vtm[0:64, hp, sb, hb:hb + 64],
                               start=first, stop=False),
                           waits=[(s_act, tr_ok), (s_dve, vc_ok)] +
                           pO_free[ip] if first else ())
                    P.emit("tensor", lambda e, po=po, ip=ip:
                           e.matmul(po[:, 0:64], ptc[ip][:], vcb[ip][:],
                                    start=False, stop=False))
                    P.emit("tensor", lambda e, po=po, ip=ip: e.matmul(
                        po[:, 0:64], pt3s[ip][:], vps[ip][:],
                        start=False, stop=True),
                        waits=[(s_act, s_act.val)], inc=s_pe)
                    o_pe = s_pe.val
                    P.emit("scalar", lambda e, po=po, sb=sb, hp=hp, hd=hd:
                           e.copy(otm[:, sb, (hp * 2 + hd) * 64:
                                      (hp * 2 + hd + 1) * 64], po[:, 0:64]),
                           waits=[(s_pe, o_pe)])
                    pO_free[ip] = [(s_act, s_act.val)]
                    pT_free[ip] = [(s_act, tr_ok)]
        at_done = s_act.val

        P.dma("gpsimd", wpj[:], wproj_d.rearrange("(c p) o -> p c o", p=128),
              waits=[(s_pe, s_pe.val)], inc=s_wp)
        wp_ok = s_wp.val
        for tsu in range(16):
            for c in range(8):
                tp_ = pT[c % 2]
                P.emit("tensor", lambda e, tsu=tsu, c=c, tp_=tp_: e.transpose(
                    tp_[:, 0:128], otm[:, tsu, c * 128:(c + 1) * 128], idt[:]),
                    waits=[(s_act, at_done), (s_act, s_act.val)], inc=s_pe)
                P.emit("scalar", lambda e, c=c, tp_=tp_: e.copy(
                    ofm[:, c, :], tp_[:, 0:128]), waits=[s_pe])
            ofm_ok = s_act.val
            for ocol in range(2):
                pp = pS[ocol]
                for c in range(8):
                    P.emit("tensor", lambda e, c=c, ocol=ocol, pp=pp: e.matmul(
                        pp[:], ofm[:, c, :],
                        wpj[:, c, ocol * 512:(ocol + 1) * 512],
                        start=(c == 0), stop=False),
                        waits=[(s_act, ofm_ok), (s_wp, wp_ok),
                               (s_dve, s_dve.val)] if c == 0 else ())
                P.emit("tensor", lambda e, ocol=ocol, pp=pp: e.matmul(
                    pp[:], ones1[:], bprr[0:1, ocol * 512:(ocol + 1) * 512],
                    start=False, stop=True),
                    waits=[(s_cst, cst_all)], inc=s_pe)
                wp2 = ocol
                P.emit("vector", lambda e, pp=pp, wp2=wp2: e.tensor_copy(
                    wt2[wp2][:], pp[:]),
                    waits=[s_pe, (s_wst[wp2], s_wst[wp2].val)])
                r0 = t0 + tsu * 128
                P.emit("gpsimd", lambda e, wp2=wp2, r0=r0, ocol=ocol:
                       e.dma_start(
                           out=rout_d[r0:r0 + 128,
                                      ocol * 512:(ocol + 1) * 512],
                           in_=wt2[wp2][:], accum_op=ALU.add),
                       waits=[(s_dve, s_dve.val),
                              (s_rst[0], s_rst[0].val),
                              (s_rst[1], s_rst[1].val)],
                       inc=s_wst[wp2], n=16)

    fin = sb_(f"{name}_fin", [1, 8], F32)
    P.emit("gpsimd", lambda e: e.memset(fin[:], 0.0),
           waits=[(s_wst[0], s_wst[0].val), (s_wst[1], s_wst[1].val),
                  (s_rst[0], s_rst[0].val), (s_rst[1], s_rst[1].val)])


def build_l1_attn_program(nslot):
    nc = bass.Bass("TRN2", target_bir_lowering=False)
    rin_d = nc.dram_tensor("rin", [NTOK, DM], F32, kind="ExternalInput").ap()
    watn_d = nc.dram_tensor("watn", [DM, 3072], F32, kind="ExternalInput").ap()
    batn_d = nc.dram_tensor("batn", [3072], F32, kind="ExternalInput").ap()
    wproj_d = nc.dram_tensor("wproj", [DM, DM], F32, kind="ExternalInput").ap()
    bproj_d = nc.dram_tensor("bproj", [DM], F32, kind="ExternalInput").ap()
    ident_d = nc.dram_tensor("ident", [128, 128], F32, kind="ExternalInput").ap()
    ones_d = nc.dram_tensor("ones", [1, 128], F32, kind="ExternalInput").ap()
    psel_d = nc.dram_tensor("psel", [128, 16, nslot, 40], F32,
                            kind="ExternalInput").ap()
    mask_d = nc.dram_tensor("maskc", [128, 232], F32, kind="ExternalInput").ap()
    rout_d = nc.dram_tensor("rout", [NTOK, DM], F32, kind="ExternalOutput").ap()

    with ExitStack() as ctx:
        P = Prog(nc, ctx)
        block = ctx.enter_context(nc.Block())
        emit_l1_attn(P, rin_d, rout_d, watn_d, batn_d, wproj_d, bproj_d,
                     ident_d, ones_d, psel_d, mask_d, nslot)
        P.finalize(block)
    return nc



def emit_l1_ffn(P, rin_d, rout_d, x4fm_d, wgcat_d, brg_d, w1_d, b1_d, w2_d,
                aug2_d, ident_d, ones_d, name="m1"):
    nc = P.nc
    CH = 512
    NCH = NTOK // CH
    SUB = CH // 128
    NE = 9
    sb, ps = P.sb, P.ps
    s_act, s_dve = P.sa, P.sv

    cwp = sb(f"{name}_cwp", [128, 32, NE], F32)
    ones1 = sb(f"{name}_ones", [1, 128], BF16)

    s_ldw = P.sem(f"{name}_ldw")
    s_ldx = [P.sem(f"{name}_ldx{i}") for i in range(2)]
    s_rst = [P.sem(f"{name}_rst{i}") for i in range(2)]
    s_x4st = [P.sem(f"{name}_x4st{i}") for i in range(2)]
    s_pe = P.sem(f"{name}_pe")

    P.dma("gpsimd", ones1[:], ones_d, inc=s_ldw)

    P.push_scope()
    rin2 = [sb(f"{name}_rin{i}", [128, SUB, DM], F32) for i in range(2)]
    x2p = sb(f"{name}_x2p", [128, SUB, DM], BF16)
    xf = sb(f"{name}_xf", [128, 8, CH], BF16)
    sq = sb(f"{name}_sq", [128, DM], F32)
    stat = sb(f"{name}_stat", [128, 8], F32)
    stat2 = sb(f"{name}_stat2", [128, 16], F32)
    tmp10 = sb(f"{name}_tmp10", [128, 10], F32)
    gpt = sb(f"{name}_gpt", [128, 8], F32)
    t8a = sb(f"{name}_t8a", [128, 8], F32)
    t8b = sb(f"{name}_t8b", [128, 8], F32)
    wgt = sb(f"{name}_wgt", [128, 8, 10], BF16)
    brgt = sb(f"{name}_brgt", [1, 10], BF16)
    idt = sb(f"{name}_idt", [128, 128], BF16)
    eps = sb(f"{name}_eps", [128, 1], F32)
    pt = [ps(f"{name}_pt{i}", [128, 128], BF16) for i in range(2)]
    prt = ps(f"{name}_prt", [128, 16])

    P.dma("gpsimd", wgt[:], wgcat_d.rearrange("(c p) g -> p c g", p=128), inc=s_ldw)
    P.dma("gpsimd", brgt[:], brg_d.rearrange("(o g) -> o g", o=1), inc=s_ldw)
    P.dma("gpsimd", idt[:], ident_d, inc=s_ldw)
    ldw_all = s_ldw.val
    P.emit("vector", lambda e: e.memset(eps[:], 1e-5))

    rin_free = [[], []]
    for ci in range(NCH):
        par = ci % 2
        r_c = rin2[par]
        rview = rin_d[ci * CH:(ci + 1) * CH, :].rearrange("(s p) d -> p s d", p=128)
        P.dma("sync", r_c[:], rview,
              waits=[(s_rst[par], s_rst[par].val)] + rin_free[par],
              inc=s_ldx[par])
        ld_done = s_ldx[par].val
        oview = rout_d[ci * CH:(ci + 1) * CH, :].rearrange("(s p) d -> p s d", p=128)
        P.dma("gpsimd", oview, r_c[:], waits=[(s_ldx[par], ld_done)],
              inc=s_rst[par])
        for s in range(SUB):
            row = r_c[:, s, :]
            P.emit("vector", lambda e, row=row: e.reduce_sum(
                stat[:, 0:1], row, axis=AX.X), waits=[(s_ldx[par], ld_done)])
            P.emit("scalar", lambda e, row=row: e.activation(
                sq[:], row, AF.Square, accum_out=stat[:, 1:2]),
                waits=[(s_ldx[par], ld_done), (s_dve, s_dve.val)])
            P.emit("vector", lambda e: e.tensor_scalar(
                stat[:, 2:3], stat[:, 0:1], 1.0 / DM, None, op0=ALU.mult))
            P.emit("vector", lambda e: e.tensor_mul(
                stat[:, 3:4], stat[:, 2:3], stat[:, 2:3]))
            P.emit("vector", lambda e: e.tensor_scalar(
                stat[:, 4:5], stat[:, 1:2], 1.0 / DM, None, op0=ALU.mult),
                waits=[s_act])
            P.emit("vector", lambda e: e.tensor_sub(
                stat[:, 4:5], stat[:, 4:5], stat[:, 3:4]))
            P.emit("scalar", lambda e: e.activation(
                stat[:, 5:6], stat[:, 4:5], AF.Sqrt, bias=eps[:, 0:1]),
                waits=[s_dve])
            P.emit("vector", lambda e: e.reciprocal(
                stat[:, 6:7], stat[:, 5:6]), waits=[s_act])
            P.emit("vector", lambda e, row=row, s=s: e.tensor_scalar(
                x2p[:, s, :], row, stat[:, 2:3], stat[:, 6:7],
                op0=ALU.subtract, op1=ALU.mult),
                waits=[(s_pe, s_pe.val)])
            x2_done = s_dve.val
            for c in range(8):
                tp = pt[c % 2]
                P.emit("tensor", lambda e, s=s, c=c, tp=tp: e.transpose(
                    tp[:], x2p[:, s, c * 128:(c + 1) * 128], idt[:]),
                    waits=[(s_dve, x2_done), (s_ldw, ldw_all),
                           (s_act, s_act.val)],
                    inc=s_pe)
                P.emit("scalar", lambda e, s=s, c=c, tp=tp: e.copy(
                    xf[:, c, s * 128:(s + 1) * 128], tp[:]),
                    waits=[s_pe, (s_x4st[0], s_x4st[0].val),
                           (s_x4st[1], s_x4st[1].val)])
        tr_done = s_act.val
        rin_free[par] = [(s_dve, s_dve.val), (s_act, s_act.val)]
        for s in range(SUB):
            for c in range(8):
                P.emit("tensor", lambda e, s=s, c=c: e.matmul(
                    prt[:, 0:10], xf[:, c, s * 128:(s + 1) * 128],
                    wgt[:, c, :], start=(c == 0), stop=False),
                    waits=[(s_act, s_act.val), (s_dve, s_dve.val)] if c == 0 else ())
            P.emit("tensor", lambda e: e.matmul(
                prt[:, 0:10], ones1[:], brgt[:], start=False, stop=True),
                inc=s_pe)
            g = ci * SUB + s
            P.emit("vector", lambda e: e.reduce_max(
                stat2[:, 0:1], prt[:, 0:2], axis=AX.X), waits=[s_pe])
            P.emit("vector", lambda e: e.tensor_scalar(
                stat2[:, 1:2], stat2[:, 0:1], -1.0, None, op0=ALU.mult))
            P.emit("scalar", lambda e: e.activation(
                tmp10[:, 0:2], prt[:, 0:2], AF.Exp, bias=stat2[:, 1:2]),
                waits=[s_dve, s_pe])
            P.emit("vector", lambda e: e.reduce_sum(
                stat2[:, 2:3], tmp10[:, 0:2], axis=AX.X), waits=[s_act])
            P.emit("vector", lambda e: e.reciprocal(stat2[:, 3:4], stat2[:, 2:3]))
            P.emit("vector", lambda e: e.reduce_max(
                stat2[:, 4:5], prt[:, 2:10], axis=AX.X))
            P.emit("vector", lambda e: e.tensor_scalar(
                stat2[:, 5:6], stat2[:, 4:5], -1.0, None, op0=ALU.mult))
            P.emit("scalar", lambda e: e.activation(
                tmp10[:, 2:10], prt[:, 2:10], AF.Exp, bias=stat2[:, 5:6]),
                waits=[s_dve])
            P.emit("vector", lambda e: e.reduce_sum(
                stat2[:, 6:7], tmp10[:, 2:10], axis=AX.X), waits=[s_act])
            P.emit("vector", lambda e: e.reciprocal(stat2[:, 7:8], stat2[:, 6:7]))
            P.emit("vector", lambda e: e.tensor_scalar(
                gpt[:], tmp10[:, 2:10], stat2[:, 7:8], None, op0=ALU.mult))
            P.emit("vector", lambda e: e.reduce_max(
                stat2[:, 8:9], gpt[:], axis=AX.X))
            P.emit("vector", lambda e: e.tensor_scalar(
                t8a[:], gpt[:], stat2[:, 8:9], -1e30,
                op0=ALU.is_equal, op1=ALU.mult))
            P.emit("vector", lambda e: e.tensor_add(t8a[:], t8a[:], gpt[:]))
            P.emit("vector", lambda e: e.reduce_max(
                stat2[:, 9:10], t8a[:], axis=AX.X))
            P.emit("vector", lambda e: e.tensor_add(
                stat2[:, 10:11], stat2[:, 8:9], stat2[:, 9:10]))
            P.emit("vector", lambda e: e.reciprocal(
                stat2[:, 11:12], stat2[:, 10:11]))
            P.emit("vector", lambda e: e.tensor_scalar(
                t8a[:], gpt[:], stat2[:, 8:9], None, op0=ALU.is_equal))
            P.emit("vector", lambda e: e.tensor_scalar(
                t8b[:], gpt[:], stat2[:, 9:10], None, op0=ALU.is_equal))
            P.emit("vector", lambda e: e.tensor_add(t8a[:], t8a[:], t8b[:]))
            P.emit("vector", lambda e: e.tensor_mul(t8a[:], t8a[:], gpt[:]))
            P.emit("vector", lambda e: e.tensor_scalar(
                t8a[:], t8a[:], stat2[:, 11:12], None, op0=ALU.mult))
            P.emit("vector", lambda e: e.tensor_scalar(
                t8a[:], t8a[:], stat2[:, 3:4], None, op0=ALU.mult))
            P.emit("vector", lambda e, g=g: e.tensor_scalar(
                cwp[:, g, 0:8], t8a[:], tmp10[:, 0:1], None, op0=ALU.mult))
            P.emit("vector", lambda e, g=g: e.tensor_scalar(
                cwp[:, g, 8:9], tmp10[:, 1:2], stat2[:, 3:4], None,
                op0=ALU.mult))
        xview = x4fm_d[:, ci, :, :]
        P.dma("gpsimd", xview, xf[:],
              waits=[(s_act, tr_done), (s_pe, s_pe.val)], inc=s_x4st[par])
    P.pop_scope()

    w1t = sb(f"{name}_w1t", [128, 8, 4096], BF16)
    w2t = sb(f"{name}_w2t", [128, 32, DM], BF16)
    b1t = [sb(f"{name}_b1t{i}", [128, 32], F32) for i in range(2)]
    aug2t = [sb(f"{name}_aug{i}", [1, DM], BF16) for i in range(2)]
    xb = [sb(f"{name}_xb{i}", [128, 8, CH], BF16) for i in range(2)]
    h_blk = sb(f"{name}_hblk", [128, 32, CH], BF16)
    wtile = [sb(f"{name}_wt{i}", [128, 512], F32) for i in range(2)]
    psA = [ps(f"{name}_psA{i}", [128, 512]) for i in range(2)]
    psB = [ps(f"{name}_psB{i}", [128, 512]) for i in range(2)]
    s_w1q = [P.sem(f"{name}_w1q{i}") for i in range(4)]
    s_w2 = P.sem(f"{name}_w2")
    s_bld = P.sem(f"{name}_bld")
    s_xb = [P.sem(f"{name}_xb{i}") for i in range(2)]
    s_wst = [P.sem(f"{name}_wst{i}") for i in range(2)]

    boundary = P.join(P.sv, P.sa, s_pe, s_x4st[0], s_x4st[1],
                      s_rst[0], s_rst[1], s_ldx[0], s_ldx[1])
    peA_end = 0
    peB_end = 0
    xbA_done = [0, 0]
    peA_q = [0, 0, 0, 0]
    for e_i in range(NE):
        for q in range(4):
            P.dma("gpsimd", w1t[:, :, q * 1024:(q + 1) * 1024],
                  w1_d[e_i].rearrange("(c p) f -> p c f", p=128)[
                      :, :, q * 1024:(q + 1) * 1024],
                  waits=boundary + [(s_pe, peA_q[q])], inc=s_w1q[q])
        P.dma("gpsimd", w2t[:],
              w2_d[e_i].rearrange("(f p) d -> p f d", p=128),
              waits=boundary + [(s_pe, peB_end)], inc=s_w2)
        P.dma("gpsimd", b1t[e_i % 2][:],
              b1_d[e_i].rearrange("(f p) -> p f", p=128),
              waits=boundary + [(s_act, s_act.val)], inc=s_bld, slow=True)
        P.dma("gpsimd", aug2t[e_i % 2][:],
              aug2_d[e_i:e_i + 1, :],
              waits=boundary + [(s_pe, peB_end)], inc=s_bld)
        w1_ok = [s_w1q[q].val for q in range(4)]
        w2_ok = s_w2.val
        bld_ok = s_bld.val
        for tb in range(NCH):
            tp = tb % 2
            xview = x4fm_d[:, tb, :, :]
            P.dma("sync", xb[tp][:], xview,
                  waits=boundary + [(s_pe, xbA_done[tp])], inc=s_xb[tp])
            xb_ok = s_xb[tp].val
            for f in range(32):
                ph = psA[f % 2]
                for c in range(8):
                    P.emit("tensor", lambda e, f=f, c=c, ph=ph, tp=tp: e.matmul(
                        ph[:], w1t[:, c, f * 128:(f + 1) * 128],
                        xb[tp][:, c, :], start=(c == 0), stop=(c == 7)),
                        waits=[(s_xb[tp], xb_ok), (s_w1q[f // 8], w1_ok[f // 8]),
                               (s_act, s_act.val)] if c == 0 else (),
                        inc=s_pe if c == 7 else None)
                P.emit("scalar", lambda e, f=f, ph=ph, e_i=e_i: e.activation(
                    h_blk[:, f, :], ph[:], AF.Relu,
                    bias=b1t[e_i % 2][:, f:f + 1]),
                    waits=[s_pe, (s_bld, bld_ok)])
                if tb == NCH - 1 and f % 8 == 7:
                    peA_q[f // 8] = s_pe.val
            xbA_done[tp] = s_pe.val
            hB_ok = s_act.val
            for ts_i in range(SUB):
                for dc in range(2):
                    po = psB[(ts_i * 2 + dc) % 2]
                    for f in range(32):
                        P.emit("tensor",
                               lambda e, f=f, ts_i=ts_i, dc=dc, po=po: e.matmul(
                                   po[:], h_blk[:, f, ts_i * 128:(ts_i + 1) * 128],
                                   w2t[:, f, dc * 512:(dc + 1) * 512],
                                   start=(f == 0), stop=False),
                               waits=[(s_act, hB_ok), (s_w2, w2_ok),
                                      (s_dve, s_dve.val)] if f == 0 else ())
                    P.emit("tensor", lambda e, dc=dc, po=po, e_i=e_i: e.matmul(
                        po[:], ones1[:], aug2t[e_i % 2][0:1, dc * 512:(dc + 1) * 512],
                        start=False, stop=True),
                        waits=[(s_bld, bld_ok)], inc=s_pe)
                    wp = (ts_i * 2 + dc) % 2
                    g = tb * SUB + ts_i
                    P.emit("vector", lambda e, po=po, wp=wp, g=g, e_i=e_i:
                           e.tensor_scalar(wtile[wp][:], po[:],
                                           cwp[:, g, e_i:e_i + 1], None,
                                           op0=ALU.mult),
                           waits=[s_pe, (s_wst[wp], s_wst[wp].val)])
                    r0 = tb * CH + ts_i * 128
                    P.emit("gpsimd", lambda e, wp=wp, r0=r0, dc=dc: e.dma_start(
                        out=rout_d[r0:r0 + 128, dc * 512:(dc + 1) * 512],
                        in_=wtile[wp][:], accum_op=ALU.add),
                        waits=[(s_dve, s_dve.val)], inc=s_wst[wp], n=16)
            peB_end = s_pe.val

    fin = sb(f"{name}_fin", [1, 8], F32)
    P.emit("gpsimd", lambda e: e.memset(fin[:], 0.0),
           waits=[(s_wst[0], s_wst[0].val), (s_wst[1], s_wst[1].val),
                  (s_rst[0], s_rst[0].val), (s_rst[1], s_rst[1].val)])


def build_l1_ffn_program():
    nc = bass.Bass("TRN2", target_bir_lowering=False)
    rin_d = nc.dram_tensor("rin", [NTOK, DM], F32, kind="ExternalInput").ap()
    wgcat_d = nc.dram_tensor("wgcat", [DM, 10], F32, kind="ExternalInput").ap()
    brg_d = nc.dram_tensor("brg", [10], F32, kind="ExternalInput").ap()
    w1_d = nc.dram_tensor("w1", [9, DM, 4096], F32, kind="ExternalInput").ap()
    b1_d = nc.dram_tensor("b1", [9, 4096], F32, kind="ExternalInput").ap()
    w2_d = nc.dram_tensor("w2", [9, 4096, DM], F32, kind="ExternalInput").ap()
    aug2_d = nc.dram_tensor("aug2", [9, DM], F32, kind="ExternalInput").ap()
    ident_d = nc.dram_tensor("ident", [128, 128], F32, kind="ExternalInput").ap()
    ones_d = nc.dram_tensor("ones", [1, 128], F32, kind="ExternalInput").ap()
    rout_d = nc.dram_tensor("rout", [NTOK, DM], F32, kind="ExternalOutput").ap()
    x4fm_d = nc.dram_tensor("x4fm", [128, 8, 8, 512], BF16, kind="Internal").ap()

    with ExitStack() as ctx:
        P = Prog(nc, ctx)
        block = ctx.enter_context(nc.Block())
        emit_l1_ffn(P, rin_d, rout_d, x4fm_d, wgcat_d, brg_d, w1_d, b1_d,
                    w2_d, aug2_d, ident_d, ones_d)
        P.finalize(block)
    return nc


def build_l0_ffn_program():
    nc = bass.Bass("TRN2", target_bir_lowering=False)
    rin_d = nc.dram_tensor("rin", [NTOK, DM], F32, kind="ExternalInput").ap()
    wr_d = nc.dram_tensor("wrT", [DM, DM], F32, kind="ExternalInput").ap()
    wk_d = nc.dram_tensor("wkT", [DM, DM], F32, kind="ExternalInput").ap()
    wv_d = nc.dram_tensor("wvT", [DM, DM], F32, kind="ExternalInput").ap()
    mur_d = nc.dram_tensor("mur", [DM], F32, kind="ExternalInput").ap()
    muk_d = nc.dram_tensor("muk", [DM], F32, kind="ExternalInput").ap()
    ident_d = nc.dram_tensor("ident", [128, 128], F32, kind="ExternalInput").ap()
    rout_d = nc.dram_tensor("rout", [NTOK, DM], F32, kind="ExternalOutput").ap()

    with ExitStack() as ctx:
        P = Prog(nc, ctx)
        block = ctx.enter_context(nc.Block())
        emit_l0_ffn(P, rin_d, rout_d, wr_d, wk_d, wv_d, mur_d, muk_d, ident_d)
        P.finalize(block)
    return nc


def build_fused_program(nslot):
    nc = bass.Bass("TRN2", target_bir_lowering=False)
    F = F32
    xin = nc.dram_tensor("xin", [NTOK, DM], F, kind="ExternalInput").ap()
    out = nc.dram_tensor("out", [NTOK, DM], F, kind="ExternalOutput").ap()
    r1 = nc.dram_tensor("r1", [NTOK, DM], F, kind="Internal").ap()
    r2 = nc.dram_tensor("r2", [NTOK, DM], F, kind="Internal").ap()
    r3 = nc.dram_tensor("r3", [NTOK, DM], F, kind="Internal").ap()
    x4fm_d = nc.dram_tensor("x4fm", [128, 8, 8, 512], BF16, kind="Internal").ap()

    winT = nc.dram_tensor("winT", [DM, 4096], F, kind="ExternalInput").ap()
    bin_ = nc.dram_tensor("bin", [4096], F, kind="ExternalInput").ap()
    wconv = nc.dram_tensor("wconv", [2048, 4], F, kind="ExternalInput").ap()
    bconv = nc.dram_tensor("bconv", [2048], F, kind="ExternalInput").ap()
    wxT = nc.dram_tensor("wxT", [2048, 2080], F, kind="ExternalInput").ap()
    bx = nc.dram_tensor("bx", [2080], F, kind="ExternalInput").ap()
    woutT = nc.dram_tensor("woutT", [2048, DM], F, kind="ExternalInput").ap()
    bout = nc.dram_tensor("bout", [DM], F, kind="ExternalInput").ap()
    alog = nc.dram_tensor("alog", [2048, 16], F, kind="ExternalInput").ap()
    dvec = nc.dram_tensor("dvec", [2048], F, kind="ExternalInput").ap()
    wrT = nc.dram_tensor("wrT", [DM, DM], F, kind="ExternalInput").ap()
    wkT = nc.dram_tensor("wkT", [DM, DM], F, kind="ExternalInput").ap()
    wvT = nc.dram_tensor("wvT", [DM, DM], F, kind="ExternalInput").ap()
    mur = nc.dram_tensor("mur", [DM], F, kind="ExternalInput").ap()
    muk = nc.dram_tensor("muk", [DM], F, kind="ExternalInput").ap()
    watn = nc.dram_tensor("watn", [DM, 3072], F, kind="ExternalInput").ap()
    batn = nc.dram_tensor("batn", [3072], F, kind="ExternalInput").ap()
    wproj = nc.dram_tensor("wproj", [DM, DM], F, kind="ExternalInput").ap()
    bproj = nc.dram_tensor("bproj", [DM], F, kind="ExternalInput").ap()
    wgcat = nc.dram_tensor("wgcat", [DM, 10], F, kind="ExternalInput").ap()
    brg = nc.dram_tensor("brg", [10], F, kind="ExternalInput").ap()
    w1 = nc.dram_tensor("w1", [9, DM, 4096], F, kind="ExternalInput").ap()
    b1 = nc.dram_tensor("b1", [9, 4096], F, kind="ExternalInput").ap()
    w2 = nc.dram_tensor("w2", [9, 4096, DM], F, kind="ExternalInput").ap()
    aug2 = nc.dram_tensor("aug2", [9, DM], F, kind="ExternalInput").ap()
    ident = nc.dram_tensor("ident", [128, 128], F, kind="ExternalInput").ap()
    ones = nc.dram_tensor("ones", [1, 128], F, kind="ExternalInput").ap()
    onehot = nc.dram_tensor("onehot", [16, 16, 128], F, kind="ExternalInput").ap()
    psel = nc.dram_tensor("psel", [128, 16, nslot, 40], F,
                          kind="ExternalInput").ap()
    maskc = nc.dram_tensor("maskc", [128, 232], F, kind="ExternalInput").ap()

    with ExitStack() as ctx:
        P = Prog(nc, ctx)
        block = ctx.enter_context(nc.Block())
        P.push_scope()
        emit_l0_attn(P, xin, r1, winT, bin_, wconv, bconv, wxT, bx, woutT,
                     bout, alog, dvec, ident, ones, onehot)
        P.pop_scope()
        P.soft_barrier()
        P.push_scope()
        emit_l0_ffn(P, r1, r2, wrT, wkT, wvT, mur, muk, ident)
        P.pop_scope()
        P.soft_barrier()
        P.push_scope()
        emit_l1_attn(P, r2, r3, watn, batn, wproj, bproj, ident, ones,
                     psel, maskc, nslot)
        P.pop_scope()
        P.soft_barrier()
        P.push_scope()
        emit_l1_ffn(P, r3, out, x4fm_d, wgcat, brg, w1, b1, w2, aug2,
                    ident, ones)
        P.pop_scope()
        P.finalize(block)
    return nc



_CACHE = {}


def _program():
    if "prog" not in _CACHE:
        psel, maskc = build_l1_attn_consts()
        nslot = psel.shape[1]
        _CACHE["consts"] = (psel, maskc, nslot)
        _CACHE["prog"] = build_fused_program(nslot)
    return _CACHE["prog"], _CACHE["consts"]


def kernel(**ins):
    from concourse.bass_utils import run_bass_kernel_spmd
    prog, (psel, maskc, nslot) = _program()
    ncores = 8

    ident = np.eye(128, dtype=np.float32)
    ones = np.ones((1, 128), np.float32)
    onehot = np.zeros((16, 16, 128), np.float32)
    for n in range(16):
        onehot[n, n, :] = 1.0

    w = {
        "winT": np.ascontiguousarray(ins["L0_attn_W_in"].T),
        "bin": ins["L0_attn_b_in"],
        "wconv": np.ascontiguousarray(ins["L0_attn_W_conv"].reshape(2048, 4)),
        "bconv": ins["L0_attn_b_conv"],
        "wxT": np.ascontiguousarray(ins["L0_attn_W_x"].T),
        "bx": ins["L0_attn_b_x"],
        "woutT": np.ascontiguousarray(ins["L0_attn_W_out"].T),
        "bout": ins["L0_attn_b_out"],
        "alog": ins["L0_attn_A_log"],
        "dvec": ins["L0_attn_D"],
        "wrT": np.ascontiguousarray(ins["L0_ffn_W_r"].T),
        "wkT": np.ascontiguousarray(ins["L0_ffn_W_k"].T),
        "wvT": np.ascontiguousarray(ins["L0_ffn_W_v"].T),
        "mur": ins["L0_ffn_mu_r"], "muk": ins["L0_ffn_mu_k"],
        "watn": ins["L1_attn_W_attn"], "batn": ins["L1_attn_b_attn"],
        "wproj": ins["L1_attn_W_proj"], "bproj": ins["L1_attn_b_proj"],
        "wgcat": np.ascontiguousarray(
            np.concatenate([ins["L1_ffn_Wr"], ins["L1_ffn_Wg"]], axis=1)),
        "brg": np.concatenate([ins["L1_ffn_br"], ins["L1_ffn_bg"]]),
        "w1": np.ascontiguousarray(
            np.concatenate([ins["L1_ffn_W1"], ins["L1_ffn_D1"][None]], axis=0)),
        "b1": np.ascontiguousarray(
            np.concatenate([ins["L1_ffn_b1"], ins["L1_ffn_d1b"][None]], axis=0)),
        "w2": np.ascontiguousarray(
            np.concatenate([ins["L1_ffn_W2"], ins["L1_ffn_D2"][None]], axis=0)),
        "aug2": np.ascontiguousarray(
            np.concatenate([ins["L1_ffn_b2"], ins["L1_ffn_d2b"][None]], axis=0)),
        "ident": ident, "ones": ones, "onehot": onehot,
        "psel": np.ascontiguousarray(psel.transpose(2, 0, 1, 3)),
        "maskc": maskc,
    }
    x = ins["L0_attn_x"]
    in_maps = [dict(w, xin=np.ascontiguousarray(
        x[2 * c:2 * c + 2].reshape(NTOK, DM))) for c in range(ncores)]
    res = run_bass_kernel_spmd(prog, in_maps, core_ids=list(range(ncores)))
    out = np.stack([r["out"].reshape(2, SEQ, DM) for r in res.results])
    return out.reshape(16, SEQ, DM).astype(np.float32)


# version 08 (final)
# speedup vs baseline: 1.0493x; 1.0261x over previous
import sys
sys.path.insert(0, "/opt/trn_rl_repo")
import numpy as np
from contextlib import ExitStack
import concourse.bass as bass
import concourse.mybir as mybir

F32 = mybir.dt.float32
BF16 = mybir.dt.bfloat16
AF = mybir.ActivationFunctionType
ALU = mybir.AluOpType
AX = mybir.AxisListType

ENGINES = ("sync", "gpsimd", "scalar", "vector", "tensor")

NTOK = 4096
SEQ = 2048
DM = 1024


class Sem:
    __slots__ = ("name", "val", "h")

    def __init__(self, name):
        self.name = name
        self.val = 0
        self.h = None


class Prog:
    def __init__(self, nc, ctx):
        self.nc = nc
        self.ctx = ctx
        self.ops = {e: [] for e in ENGINES}
        self.sems = []
        self.waited = {e: {} for e in ENGINES}
        self.esem = {}
        for e in ("vector", "scalar"):
            self.esem[e] = self.sem(f"eo_{e}")

    @property
    def sv(self):
        return self.esem["vector"]

    @property
    def sa(self):
        return self.esem["scalar"]

    def sem(self, name):
        s = Sem(name)
        s.h = self.ctx.enter_context(self.nc.semaphore(name))
        self.sems.append(s)
        return s

    def sb(self, name, shape, dtype):
        st = self._scopes[-1] if getattr(self, "_scopes", None) else self.ctx
        return st.enter_context(self.nc.sbuf_tensor(name, shape, dtype))

    def ps(self, name, shape, dtype=F32):
        st = self._scopes[-1] if getattr(self, "_scopes", None) else self.ctx
        return st.enter_context(self.nc.psum_tensor(name, shape, dtype))

    def push_scope(self):
        if not hasattr(self, "_scopes"):
            self._scopes = []
        self._scopes.append(ExitStack())

    def pop_scope(self):
        self._scopes.pop().close()

    def join(self, *sems):
        return [(s, s.val) for s in sems]

    def soft_barrier(self):
        snaps = [(s, s.val) for s in self.sems if s.val > 0]
        for e in ENGINES:
            for s, v in snaps:
                if self.waited[e].get(s.name, -1) >= v:
                    continue
                self.waited[e][s.name] = v

                def run(engobj, _s=s, _v=v):
                    engobj.wait_ge(_s.h, _v)
                self.ops[e].append(run)

    def emit(self, eng, fn, waits=(), inc=None, n=1):
        if eng in self.esem:
            es = self.esem[eng]
            waits = list(waits) + [(es, es.val)]
            if inc is None:
                inc = es
            elif inc is not es:
                raise ValueError("vector/scalar ops must inc their engine sem")
        wl = []
        for w in waits:
            if isinstance(w, Sem):
                w = (w, w.val)
            s, v = w
            if v <= 0:
                continue
            if self.waited[eng].get(s.name, -1) >= v:
                continue
            self.waited[eng][s.name] = v
            wl.append((s, v))
        if inc is not None:
            inc.val += n
        ninc = n

        def run(engobj, _fn=fn, _wl=wl, _inc=inc, _n=ninc):
            for s, v in _wl:
                engobj.wait_ge(s.h, v)
            instr = _fn(engobj)
            if _inc is not None:
                instr.then_inc(_inc.h, _n)
            return instr

        self.ops[eng].append(run)

    def dma(self, queue, out, in_, waits=(), inc=None, slow=False):
        self.emit(queue, lambda e, o=out, i=in_, s=slow: e.dma_start(
            out=o, in_=i, allow_slow_non_contiguous=s),
            waits=waits, inc=inc, n=16)

    def barrier(self, name):
        s = self.sem(name)
        for e in ENGINES:
            self.emit(e, lambda g: g.engine_nop() if hasattr(g, "engine_nop") else g.nop(),
                      waits=(), inc=s, n=1)
        target = s.val
        for e in ENGINES:
            def run(engobj, _s=s, _v=target):
                engobj.wait_ge(_s.h, _v)
            self.ops[e].append(run)
            self.waited[e][s.name] = target

    def finalize(self, block):
        for e in ENGINES:
            ops = self.ops[e]
            if not ops:
                continue
            section = getattr(block, e if e != "sync" else "sync")

            def body(engobj, _ops=ops):
                for fn in _ops:
                    fn(engobj)
            section(body)



def emit_l0_ffn(P, rin_d, rout_d, wr_d, wk_d, wv_d, mur_d, muk_d, ident_d,
                name="f0"):
    nc = P.nc
    CH = 512
    NCH = NTOK // CH
    SUB = CH // 128

    sb, ps = P.sb, P.ps
    rin = [sb(f"{name}_rin{i}", [128, SUB, DM], F32) for i in range(2)]
    x2p = sb(f"{name}_x2p", [128, SUB, DM], BF16)
    xf = sb(f"{name}_xf", [128, 8, CH], BF16)
    xp = sb(f"{name}_xp", [128, 8, CH], BF16)
    carry = [sb(f"{name}_car{i}", [128, 8], BF16) for i in range(2)]
    xr = sb(f"{name}_xr", [128, 8, CH], BF16)
    xk = sb(f"{name}_xk", [128, 8, CH], BF16)
    t1 = sb(f"{name}_t1", [128, CH], BF16)
    ka = sb(f"{name}_ka", [128, 8, CH], BF16)
    rt = sb(f"{name}_rt", [128, SUB, 2, 512], BF16)
    sq = sb(f"{name}_sq", [128, DM], F32)
    stat = sb(f"{name}_stat", [128, 8], F32)
    krelu = sb(f"{name}_krelu", [128, 512], F32)
    vtmp = sb(f"{name}_vtmp", [128, 512], F32)
    wr = sb(f"{name}_wr", [128, 8, DM], BF16)
    wk = sb(f"{name}_wk", [128, 8, DM], BF16)
    wv = sb(f"{name}_wv", [128, 8, DM], BF16)
    mur = sb(f"{name}_mur", [128, 8], F32)
    murc = sb(f"{name}_murc", [128, 8], F32)
    muk = sb(f"{name}_muk", [128, 8], F32)
    mukc = sb(f"{name}_mukc", [128, 8], F32)
    idt = sb(f"{name}_idt", [128, 128], BF16)

    pt = [ps(f"{name}_pt{i}", [128, 128], BF16) for i in range(2)]
    pk = [ps(f"{name}_pk{i}", [128, 512]) for i in range(2)]
    pr = [ps(f"{name}_pr{i}", [128, 512]) for i in range(2)]

    s_ldw = P.sem(f"{name}_ldw")
    s_ldx = [P.sem(f"{name}_ldx{i}") for i in range(2)]
    s_st = [P.sem(f"{name}_st{i}") for i in range(2)]
    s_pe = P.sem(f"{name}_pe")
    s_act = P.sa
    s_dve = P.sv

    P.dma("gpsimd", wr[:], wr_d.rearrange("(c p) o -> p c o", p=128), inc=s_ldw)
    P.dma("gpsimd", wk[:], wk_d.rearrange("(c p) o -> p c o", p=128), inc=s_ldw)
    P.dma("gpsimd", wv[:], wv_d.rearrange("(c p) o -> p c o", p=128), inc=s_ldw)
    P.dma("gpsimd", mur[:], mur_d.rearrange("(c p) -> p c", p=128), inc=s_ldw,
          slow=True)
    P.dma("gpsimd", muk[:], muk_d.rearrange("(c p) -> p c", p=128), inc=s_ldw,
          slow=True)
    P.dma("gpsimd", idt[:], ident_d, inc=s_ldw)
    P.emit("vector", lambda e: e.tensor_scalar(murc[:], mur[:], -1.0, 1.0,
                                               op0=ALU.mult, op1=ALU.add),
           waits=[s_ldw], inc=s_dve)
    P.emit("vector", lambda e: e.tensor_scalar(mukc[:], muk[:], -1.0, 1.0,
                                               op0=ALU.mult, op1=ALU.add),
           waits=[s_ldw], inc=s_dve)
    P.emit("vector", lambda e: e.memset(carry[0][:], 0.0), inc=s_dve)
    P.emit("vector", lambda e: e.memset(carry[1][:], 0.0), inc=s_dve)
    eps = P.sb(f"{name}_eps", [128, 1], F32)
    P.emit("vector", lambda e: e.memset(eps[:], 1e-5), inc=s_dve)

    for ci in range(NCH):
        par = ci % 2
        r_c = rin[par]
        rview = rin_d[ci * CH:(ci + 1) * CH, :].rearrange("(s p) d -> p s d", p=128)
        P.dma("sync", r_c[:], rview,
              waits=[(s_st[par], s_st[par].val)], inc=s_ldx[par])
        ld_done = s_ldx[par].val

        for s in range(SUB):
            row = r_c[:, s, :]
            P.emit("vector", lambda e, row=row: e.reduce_sum(
                stat[:, 0:1], row, axis=AX.X),
                waits=[(s_ldx[par], ld_done)], inc=s_dve)
            P.emit("scalar", lambda e, row=row: e.activation(
                sq[:], row, AF.Square, accum_out=stat[:, 1:2]),
                waits=[(s_ldx[par], ld_done), (s_dve, s_dve.val)], inc=s_act)
            P.emit("vector", lambda e: e.tensor_scalar(
                stat[:, 2:3], stat[:, 0:1], 1.0 / DM, None, op0=ALU.mult),
                waits=[s_dve], inc=s_dve)
            P.emit("vector", lambda e: e.tensor_mul(
                stat[:, 3:4], stat[:, 2:3], stat[:, 2:3]), inc=s_dve)
            P.emit("vector", lambda e: e.tensor_scalar(
                stat[:, 4:5], stat[:, 1:2], 1.0 / DM, None, op0=ALU.mult),
                waits=[s_act], inc=s_dve)
            P.emit("vector", lambda e: e.tensor_sub(
                stat[:, 4:5], stat[:, 4:5], stat[:, 3:4]), inc=s_dve)
            P.emit("scalar", lambda e: e.activation(
                stat[:, 5:6], stat[:, 4:5], AF.Sqrt, bias=eps[:, 0:1]),
                waits=[s_dve], inc=s_act)
            P.emit("vector", lambda e: e.reciprocal(
                stat[:, 6:7], stat[:, 5:6]), waits=[s_act], inc=s_dve)
            P.emit("vector", lambda e, row=row, s=s: e.tensor_scalar(
                x2p[:, s, :], row, stat[:, 2:3], stat[:, 6:7],
                op0=ALU.subtract, op1=ALU.mult),
                waits=[(s_pe, s_pe.val)], inc=s_dve)
            x2_done = s_dve.val
            for c in range(8):
                tp = pt[c % 2]
                P.emit("tensor", lambda e, s=s, c=c, tp=tp: e.transpose(
                    tp[:], x2p[:, s, c * 128:(c + 1) * 128], idt[:]),
                    waits=[(s_dve, x2_done), (s_ldw, 96), (s_act, s_act.val)],
                    inc=s_pe)
                P.emit("scalar", lambda e, s=s, c=c, tp=tp: e.copy(
                    xf[:, c, s * 128:(s + 1) * 128], tp[:]),
                    waits=[s_pe], inc=s_act)
        tr_done = s_act.val

        P.emit("vector", lambda e: e.tensor_copy(xp[:, :, 1:CH], xf[:, :, 0:CH - 1]),
               waits=[(s_act, tr_done)], inc=s_dve)
        P.emit("vector", lambda e, par=par: e.tensor_copy(
            xp[:, :, 0:1], carry[par][:].rearrange("p c -> p c ()")),
            inc=s_dve)
        P.emit("vector", lambda e, par=par: e.tensor_copy(
            carry[par][:].rearrange("p c -> p c ()"), xf[:, :, CH - 1:CH]),
            inc=s_dve)
        if ci == NCH // 2 - 1:
            P.emit("vector", lambda e, par=par: e.memset(carry[(par + 1) % 2][:], 0.0),
                   inc=s_dve)

        for c in range(8):
            P.emit("vector", lambda e, c=c: e.tensor_scalar(
                t1[:], xf[:, c, :], mur[:, c:c + 1], None, op0=ALU.mult),
                inc=s_dve)
            P.emit("vector", lambda e, c=c: e.scalar_tensor_tensor(
                xr[:, c, :], xp[:, c, :], murc[:, c:c + 1], t1[:],
                op0=ALU.mult, op1=ALU.add), inc=s_dve)
            P.emit("vector", lambda e, c=c: e.tensor_scalar(
                t1[:], xf[:, c, :], muk[:, c:c + 1], None, op0=ALU.mult),
                inc=s_dve)
            P.emit("vector", lambda e, c=c: e.scalar_tensor_tensor(
                xk[:, c, :], xp[:, c, :], mukc[:, c:c + 1], t1[:],
                op0=ALU.mult, op1=ALU.add), inc=s_dve)
        mix_done = s_dve.val

        for o in range(8):
            po = pk[o % 2]
            for c in range(8):
                P.emit("tensor", lambda e, o=o, c=c, po=po: e.matmul(
                    po[:], wk[:, c, o * 128:(o + 1) * 128], xk[:, c, :],
                    start=(c == 0), stop=(c == 7)),
                    waits=[(s_dve, s_dve.val)] if c == 0 else (),
                    inc=s_pe if c == 7 else None)
            P.emit("vector", lambda e, po=po: e.tensor_scalar(
                krelu[:], po[:], 0.0, None, op0=ALU.max),
                waits=[s_pe, (s_act, s_act.val)], inc=s_dve)
            P.emit("scalar", lambda e, o=o: e.activation(
                ka[:, o, :], krelu[:], AF.Square), waits=[s_dve], inc=s_act)
        for s in range(SUB):
            for oc in range(2):
                pp = pr[(s * 2 + oc) % 2]
                for c in range(8):
                    P.emit("tensor", lambda e, s=s, oc=oc, c=c, pp=pp: e.matmul(
                        pp[:], xr[:, c, s * 128:(s + 1) * 128],
                        wr[:, c, oc * 512:(oc + 1) * 512],
                        start=(c == 0), stop=(c == 7)),
                        waits=[(s_act, s_act.val)] if c == 0 else (),
                        inc=s_pe if c == 7 else None)
                P.emit("scalar", lambda e, s=s, oc=oc, pp=pp: e.activation(
                    rt[:, s, oc, :], pp[:], AF.Sigmoid),
                    waits=[s_pe, (s_dve, s_dve.val)], inc=s_act)
        ka_done = s_act.val

        for s in range(SUB):
            for oc in range(2):
                pp = pk[(s * 2 + oc) % 2]
                for c in range(8):
                    P.emit("tensor", lambda e, s=s, oc=oc, c=c, pp=pp: e.matmul(
                        pp[:], ka[:, c, s * 128:(s + 1) * 128],
                        wv[:, c, oc * 512:(oc + 1) * 512],
                        start=(c == 0), stop=(c == 7)),
                        waits=[(s_act, ka_done), (s_dve, s_dve.val)] if c == 0 else (),
                        inc=s_pe if c == 7 else None)
                P.emit("vector", lambda e, s=s, oc=oc, pp=pp: e.tensor_mul(
                    vtmp[:], pp[:], rt[:, s, oc, :]),
                    waits=[s_pe, (s_act, s_act.val)], inc=s_dve)
                P.emit("vector", lambda e, s=s, oc=oc, r_c=r_c: e.tensor_add(
                    r_c[:, s, oc * 512:(oc + 1) * 512],
                    r_c[:, s, oc * 512:(oc + 1) * 512], vtmp[:]),
                    inc=s_dve)
        add_done = s_dve.val

        oview = rout_d[ci * CH:(ci + 1) * CH, :].rearrange("(s p) d -> p s d", p=128)
        P.dma("gpsimd", oview, r_c[:], waits=[(s_dve, add_done)], inc=s_st[par])

    fin = P.sb(f"{name}_fin", [1, 8], F32)
    P.emit("gpsimd", lambda e: e.memset(fin[:], 0.0),
           waits=[(s_st[0], s_st[0].val), (s_st[1], s_st[1].val)], inc=None)



def emit_l0_attn(P, rin_d, rout_d, w_inT_d, b_in_d, wconv_d, bconv_d, w_xT_d,
                 b_x_d, w_outT_d, b_out_d, alog_d, dvec_d, ident_d, ones_d,
                 onehot_d, name="a0"):
    nc = P.nc
    IT = 1024
    NIT = NTOK // IT
    TS = 512
    DI = 2048
    NDC = DI // 128
    NN = 16
    sb, ps = P.sb, P.ps
    s_act, s_dve = P.sa, P.sv

    ones1 = sb(f"{name}_ones", [1, 128], BF16)
    idt = sb(f"{name}_idt", [128, 128], BF16)
    oh_t = sb(f"{name}_oh", [16, NN, 128], BF16)
    at_t = sb(f"{name}_A", [128, NDC, NN], F32)
    wct = sb(f"{name}_wc", [128, NDC, 4], F32)
    bct = sb(f"{name}_bc", [128, NDC], F32)
    dvt = sb(f"{name}_dv", [128, NDC], F32)
    bint = sb(f"{name}_bin", [128, 32], F32)
    bxdt = sb(f"{name}_bxd", [128, NDC], F32)
    bxB = sb(f"{name}_bxB", [16, 1], F32)
    bxC = sb(f"{name}_bxC", [16, 1], F32)
    boutr = sb(f"{name}_bout", [1, DM], BF16)
    wxBC = sb(f"{name}_wxbc", [128, 16, 32], BF16)
    eps = sb(f"{name}_eps", [128, 1], F32)
    hcar = sb(f"{name}_hcar", [128, NDC * NN], F32)
    xh = sb(f"{name}_xh", [128, NDC, 4], F32)

    s_cst = P.sem(f"{name}_cst")
    s_ldx = [P.sem(f"{name}_ldx{i}") for i in range(2)]
    s_rst = [P.sem(f"{name}_rst{i}") for i in range(2)]
    s_win = [P.sem(f"{name}_win{i}") for i in range(2)]
    s_wx = [P.sem(f"{name}_wx{i}") for i in range(2)]
    s_wr = [P.sem(f"{name}_wr{i}") for i in range(2)]
    s_wo = P.sem(f"{name}_wo")
    s_wst = [P.sem(f"{name}_wst{i}") for i in range(2)]
    s_pe = P.sem(f"{name}_pe")
    s_pool = P.sem(f"{name}_pool")

    P.dma("gpsimd", ones1[:], ones_d, inc=s_cst)
    P.dma("gpsimd", idt[:], ident_d, inc=s_cst)
    P.dma("gpsimd", oh_t[:], onehot_d, inc=s_cst)
    P.dma("gpsimd", wct[:], wconv_d.rearrange("(c p) k -> p c k", p=128),
          inc=s_cst, slow=True)
    P.dma("gpsimd", bct[:], bconv_d.rearrange("(c p) -> p c", p=128),
          inc=s_cst, slow=True)
    P.dma("gpsimd", dvt[:], dvec_d.rearrange("(c p) -> p c", p=128),
          inc=s_cst, slow=True)
    P.dma("gpsimd", bint[:], b_in_d.rearrange("(c p) -> p c", p=128),
          inc=s_cst, slow=True)
    P.dma("gpsimd", bxdt[:], b_x_d[0:DI].rearrange("(c p) -> p c", p=128),
          inc=s_cst, slow=True)
    P.dma("gpsimd", bxB[:], b_x_d[DI:DI + 16].rearrange("(p) -> p ()"),
          inc=s_cst, slow=True)
    P.dma("gpsimd", bxC[:], b_x_d[DI + 16:DI + 32].rearrange("(p) -> p ()"),
          inc=s_cst, slow=True)
    P.dma("gpsimd", boutr[:], b_out_d.rearrange("(o d) -> o d", o=1), inc=s_cst)
    P.dma("gpsimd", wxBC[:], w_xT_d[:, DI:DI + 32].rearrange(
        "(c p) b -> p c b", p=128), inc=s_cst)
    P.push_scope()
    alt = sb(f"{name}_alt", [128, NDC, NN], F32)
    P.dma("gpsimd", alt[:], alog_d.rearrange("(c p) n -> p c n", p=128),
          inc=s_cst)
    cst_all = s_cst.val
    P.emit("scalar", lambda e: e.activation(at_t[:], alt[:], AF.Exp),
           waits=[(s_cst, cst_all)])
    P.emit("vector", lambda e: e.tensor_scalar(
        at_t[:], at_t[:], -1.0, None, op0=ALU.mult), waits=[s_act])
    a_scal_done = P.sa.val
    a_vec_done = P.sv.val
    P.pop_scope()
    P.emit("vector", lambda e: e.memset(eps[:], 1e-5))

    rin_t = [sb(f"{name}_rin{i}", [128, DM], F32) for i in range(2)]
    x1p = [sb(f"{name}_x1p{i}", [128, DM], BF16) for i in range(2)]
    xfm = sb(f"{name}_xfm", [128, 8, IT], BF16)
    stat = sb(f"{name}_stat", [128, 8], F32)
    win_t = [sb(f"{name}_win{i}", [128, 8, 128], BF16) for i in range(2)]
    wx_t = [sb(f"{name}_wxt{i}", [128, 16, 128], BF16) for i in range(2)]
    wr_t = [sb(f"{name}_wrt{i}", [128, 8, 128], BF16) for i in range(2)]
    wo_t = sb(f"{name}_wot", [128, 16, 512], BF16)
    xi_pre = sb(f"{name}_xipre", [128, 4 + IT], BF16)
    cacc = sb(f"{name}_cacc", [128, IT], F32)
    sq = cacc
    xic = sb(f"{name}_xic", [128, NDC, IT], BF16)
    sres_t = sb(f"{name}_srest", [128, TS], BF16)
    B_sb = sb(f"{name}_Bsb", [16, TS], BF16)
    C_sb = sb(f"{name}_Csb", [16, TS], BF16)
    cbc = sb(f"{name}_cbc", [128, NN, TS], BF16)
    bbc = sb(f"{name}_bbc", [128, NN, TS], BF16)
    d_t = [cacc[:, 0:TS], cacc[:, TS:2 * TS]]
    u_t = sb(f"{name}_ut", [128, TS], BF16)
    sp_t = sb(f"{name}_spt", [128, TS], F32)
    dA_t = [sb(f"{name}_dA{i}", [128, TS], F32) for i in range(2)]
    dbxa = sb(f"{name}_dbxa", [128, NN, TS], BF16)
    h_t = sb(f"{name}_h", [128, NN, TS], BF16)
    y_t = sp_t
    y2_t = sb(f"{name}_y2", [128, TS], F32)
    y3 = sb(f"{name}_y3", [128, NDC, TS], BF16)
    wt2 = [sb(f"{name}_wt2{i}", [128, 512], F32) for i in range(2)]

    pIN = [ps(f"{name}_pIN{i}", [128, 512]) for i in range(2)]
    pDX = [ps(f"{name}_pDX{i}", [128, 512]) for i in range(2)]
    pBB = [ps(f"{name}_pBB{i}", [128, TS]) for i in range(2)]
    pt = [ps(f"{name}_pt{i}", [128, 128], BF16) for i in range(2)]

    rin_free = [[(s_cst, cst_all), (s_act, a_scal_done), (s_dve, a_vec_done)]
                for _ in range(2)]
    for it in range(NIT):
        seq_start = (it % 2 == 0)
        t0 = it * IT
        if seq_start:
            P.emit("vector", lambda e: e.memset(hcar[:], 0.0))
            P.emit("vector", lambda e: e.memset(xh[:], 0.0))

        for s in range(8):
            par = s % 2
            r_c = rin_t[par]
            rview = rin_d[t0 + s * 128:t0 + (s + 1) * 128, :]
            P.dma("sync", r_c[:], rview,
                  waits=[(s_rst[par], s_rst[par].val)] + rin_free[par],
                  inc=s_ldx[par])
            ld_done = s_ldx[par].val
            oview = rout_d[t0 + s * 128:t0 + (s + 1) * 128, :]
            P.dma("gpsimd", oview, r_c[:], waits=[(s_ldx[par], ld_done)],
                  inc=s_rst[par])
            P.emit("vector", lambda e, r_c=r_c: e.reduce_sum(
                stat[:, 0:1], r_c[:], axis=AX.X), waits=[(s_ldx[par], ld_done)])
            P.emit("scalar", lambda e, r_c=r_c: e.activation(
                sq[:], r_c[:], AF.Square, accum_out=stat[:, 1:2]),
                waits=[(s_ldx[par], ld_done), (s_dve, s_dve.val)])
            P.emit("vector", lambda e: e.tensor_scalar(
                stat[:, 2:3], stat[:, 0:1], 1.0 / DM, None, op0=ALU.mult))
            P.emit("vector", lambda e: e.tensor_mul(
                stat[:, 3:4], stat[:, 2:3], stat[:, 2:3]))
            P.emit("vector", lambda e: e.tensor_scalar(
                stat[:, 4:5], stat[:, 1:2], 1.0 / DM, None, op0=ALU.mult),
                waits=[s_act])
            P.emit("vector", lambda e: e.tensor_sub(
                stat[:, 4:5], stat[:, 4:5], stat[:, 3:4]))
            P.emit("scalar", lambda e: e.activation(
                stat[:, 5:6], stat[:, 4:5], AF.Sqrt, bias=eps[:, 0:1]),
                waits=[s_dve])
            P.emit("vector", lambda e: e.reciprocal(
                stat[:, 6:7], stat[:, 5:6]), waits=[s_act])
            P.emit("vector", lambda e, r_c=r_c, par=par: e.tensor_scalar(
                x1p[par][:], r_c[:], stat[:, 2:3], stat[:, 6:7],
                op0=ALU.subtract, op1=ALU.mult),
                waits=[(s_pe, s_pe.val)])
            x1_done = s_dve.val
            rin_free[par] = [(s_dve, x1_done), (s_act, s_act.val)]
            for c in range(8):
                tp = pt[c % 2]
                P.emit("tensor", lambda e, par=par, c=c, tp=tp: e.transpose(
                    tp[:], x1p[par][:, c * 128:(c + 1) * 128], idt[:]),
                    waits=[(s_dve, x1_done), (s_cst, cst_all),
                           (s_act, s_act.val)],
                    inc=s_pe)
                P.emit("scalar", lambda e, s=s, c=c, tp=tp: e.copy(
                    xfm[:, c, s * 128:(s + 1) * 128], tp[:]),
                    waits=[s_pe])
        tr_done = s_act.val

        win_used = [0, 0]
        for dc in range(16):
            wpar = dc % 2
            P.dma("gpsimd", win_t[wpar][:],
                  w_inT_d[:, dc * 128:(dc + 1) * 128].rearrange(
                      "(c p) o -> p c o", p=128),
                  waits=[(s_pe, win_used[wpar])], inc=s_win[wpar])
            win_ok = s_win[wpar].val
            for th in range(2):
                pp = pIN[th]
                for c in range(8):
                    P.emit("tensor", lambda e, th=th, c=c, pp=pp, wpar=wpar:
                           e.matmul(pp[:], win_t[wpar][:, c, :],
                                    xfm[:, c, th * 512:(th + 1) * 512],
                                    start=(c == 0), stop=(c == 7)),
                           waits=[(s_win[wpar], win_ok), (s_act, s_act.val),
                                  (s_dve, s_dve.val)]
                           if c == 0 else (),
                           inc=s_pe if c == 7 else None)
            win_used[wpar] = s_pe.val
            P.emit("vector", lambda e, dc=dc: e.tensor_copy(
                xi_pre[:, 1:4], xh[:, dc, 1:4]), waits=[(s_act, s_act.val)])
            for th in range(2):
                P.emit("scalar", lambda e, th=th, dc=dc: e.activation(
                    xi_pre[:, 4 + th * 512:4 + (th + 1) * 512],
                    pIN[th][:], AF.Identity, bias=bint[:, dc:dc + 1]),
                    waits=[s_pe, (s_dve, s_dve.val)])
            P.emit("vector", lambda e, dc=dc: e.tensor_scalar(
                cacc[:], xi_pre[:, 1:1 + IT], wct[:, dc, 0:1], None,
                op0=ALU.mult), waits=[s_act])
            for k in range(1, 4):
                P.emit("vector", lambda e, dc=dc, k=k: e.scalar_tensor_tensor(
                    cacc[:], xi_pre[:, 1 + k:1 + k + IT],
                    wct[:, dc, k:k + 1], cacc[:],
                    op0=ALU.mult, op1=ALU.add))
            P.emit("scalar", lambda e, dc=dc: e.activation(
                xic[:, dc, :], cacc[:], AF.Silu, bias=bct[:, dc:dc + 1]),
                waits=[s_dve])
            P.emit("vector", lambda e, dc=dc: e.tensor_copy(
                xh[:, dc, 1:4], xi_pre[:, 1 + IT:4 + IT]))

        pdx_free = [[], []]
        pbb_free = [[], []]
        for su in range(2):
            ts0 = su * TS
            for c in range(16):
                P.emit("tensor", lambda e, c=c: e.matmul(
                    pBB[0][0:16, :], wxBC[:, c, 0:16], xic[:, c, ts0:ts0 + TS],
                    start=(c == 0), stop=(c == 15)),
                    waits=[(s_act, s_act.val)] + pbb_free[0]
                    if c == 0 else (),
                    inc=s_pe if c == 15 else None)
            for c in range(16):
                P.emit("tensor", lambda e, c=c: e.matmul(
                    pBB[1][0:16, :], wxBC[:, c, 16:32], xic[:, c, ts0:ts0 + TS],
                    start=(c == 0), stop=(c == 15)),
                    waits=pbb_free[1] if c == 0 else (),
                    inc=s_pe if c == 15 else None)
            P.emit("vector", lambda e: e.tensor_scalar(
                B_sb[:], pBB[0][0:16, :], bxB[:, 0:1], None, op0=ALU.add),
                waits=[s_pe])
            pbb_free[0] = [(s_dve, s_dve.val)]
            P.emit("vector", lambda e: e.tensor_scalar(
                C_sb[:], pBB[1][0:16, :], bxC[:, 0:1], None, op0=ALU.add))
            pbb_free[1] = [(s_dve, s_dve.val)]
            bc_done = s_dve.val
            for n in range(NN):
                pp = pBB[n % 2]
                P.emit("tensor", lambda e, n=n, pp=pp: e.matmul(
                    pp[:], oh_t[:, n, :], C_sb[:], start=True, stop=True),
                    waits=[(s_dve, bc_done)] + pbb_free[n % 2], inc=s_pe)
                P.emit("scalar", lambda e, n=n, pp=pp: e.copy(
                    cbc[:, n, :], pp[:]),
                    waits=[s_pe, (s_dve, s_dve.val), (s_pool, s_pool.val)])
                pbb_free[n % 2] = [(s_act, s_act.val)]
            for n in range(NN):
                pp = pBB[n % 2]
                P.emit("tensor", lambda e, n=n, pp=pp: e.matmul(
                    pp[:], oh_t[:, n, :], B_sb[:], start=True, stop=True),
                    waits=pbb_free[n % 2], inc=s_pe)
                P.emit("scalar", lambda e, n=n, pp=pp: e.copy(
                    bbc[:, n, :], pp[:]), waits=[s_pe, (s_dve, s_dve.val)])
                pbb_free[n % 2] = [(s_act, s_act.val)]
            cbc_done = s_act.val

            wx_used = [0, 0]
            wr_used = [0, 0]
            wx_ok = [0, 0]
            wr_ok = [0, 0]
            zc_prev = 0
            P.dma("gpsimd", wx_t[0][:],
                  w_xT_d[:, 0:128].rearrange("(c p) o -> p c o", p=128),
                  waits=[(s_pe, wx_used[0])], inc=s_wx[0])
            wx_ok[0] = s_wx[0].val
            P.dma("gpsimd", wr_t[0][:],
                  w_inT_d[:, 16 * 128:17 * 128].rearrange(
                      "(c p) o -> p c o", p=128),
                  waits=[(s_pe, wr_used[0])], inc=s_wr[0])
            wr_ok[0] = s_wr[0].val
            for dc in range(NDC):
                wpar = dc % 2
                pp = pDX[wpar]
                for c in range(16):
                    P.emit("tensor", lambda e, c=c, pp=pp, wpar=wpar:
                           e.matmul(pp[:], wx_t[wpar][:, c, :],
                                    xic[:, c, ts0:ts0 + TS],
                                    start=(c == 0), stop=(c == 15)),
                           waits=[(s_wx[wpar], wx_ok[wpar])] + pdx_free[wpar]
                           if c == 0 else (),
                           inc=s_pe if c == 15 else None)
                wx_used[wpar] = s_pe.val
                dt_c = d_t[wpar]
                P.emit("vector", lambda e, dc=dc, pp=pp: e.tensor_scalar(
                    sp_t[:], pp[:], bxdt[:, dc:dc + 1], None,
                    op0=ALU.add), waits=[s_pe, (s_act, s_act.val)])
                pdx_free[wpar] = [(s_dve, s_dve.val)]
                P.emit("vector", lambda e: e.scalar_tensor_tensor(
                    y2_t[:], sp_t[:], -1.0, sp_t[:],
                    op0=ALU.mult, op1=ALU.max))
                P.emit("scalar", lambda e: e.activation(
                    y2_t[:], y2_t[:], AF.Exp, scale=-1.0),
                    waits=[s_dve])
                P.emit("scalar", lambda e: e.activation(
                    y2_t[:], y2_t[:], AF.Ln, bias=1.0))
                P.emit("vector", lambda e, dt_c=dt_c: e.scalar_tensor_tensor(
                    dt_c, sp_t[:], 0.0, y2_t[:],
                    op0=ALU.max, op1=ALU.add), waits=[s_act])
                P.emit("vector", lambda e, dc=dc, dt_c=dt_c:
                       e.tensor_mul(u_t[:], dt_c, xic[:, dc, ts0:ts0 + TS]))
                P.emit("vector", lambda e: e.tensor_mul(
                    dbxa[:], u_t[:, None, :].broadcast_to([128, NN, TS]),
                    bbc[:]), waits=[(s_act, cbc_done)])
                dbx_done = s_dve.val
                for n in range(NN):
                    npar = n % 2
                    if n == 3 and dc + 1 < NDC:
                        np_ = (dc + 1) % 2
                        P.dma("gpsimd", wx_t[np_][:],
                              w_xT_d[:, (dc + 1) * 128:(dc + 2) * 128].rearrange(
                                  "(c p) o -> p c o", p=128),
                              waits=[(s_pe, wx_used[np_])], inc=s_wx[np_])
                        wx_ok[np_] = s_wx[np_].val
                        P.dma("gpsimd", wr_t[np_][:],
                              w_inT_d[:, (17 + dc) * 128:(18 + dc) * 128].rearrange(
                                  "(c p) o -> p c o", p=128),
                              waits=[(s_pe, wr_used[np_])], inc=s_wr[np_])
                        wr_ok[np_] = s_wr[np_].val
                    P.emit("scalar", lambda e, n=n, dc=dc, dt_c=dt_c, npar=npar:
                           e.activation(dA_t[npar][:], dt_c, AF.Exp,
                                        scale=at_t[:, dc, n:n + 1]),
                           waits=[(s_dve, max(dbx_done, s_dve.val - 1))])
                    P.emit("vector", lambda e, n=n, npar=npar, dc=dc:
                           e.tensor_tensor_scan(
                               h_t[:, n, :], dA_t[npar][:], dbxa[:, n, :],
                               hcar[:, dc * NN + n:dc * NN + n + 1],
                               op0=ALU.mult, op1=ALU.add),
                           waits=[s_act, (s_pool, zc_prev)])
                    P.emit("gpsimd", lambda e, n=n: e.tensor_mul(
                        dbxa[:, n, :], h_t[:, n, :], cbc[:, n, :]),
                        waits=[(s_dve, s_dve.val), (s_act, cbc_done),
                               (s_pool, s_pool.val)], inc=s_pool)
                zc_prev = s_pool.val
                P.emit("vector", lambda e, dc=dc: e.tensor_copy(
                    hcar[:, dc * NN:(dc + 1) * NN],
                    h_t[:, :, TS - 1]))
                P.emit("vector", lambda e: e.tensor_reduce(
                    y_t[:, :, None], dbxa[:].rearrange("p n t -> p t n"),
                    axis=AX.X, op=ALU.add),
                    waits=[(s_pool, s_pool.val)])
                P.emit("vector", lambda e, dc=dc, ts0=ts0: e.scalar_tensor_tensor(
                    y2_t[:], xic[:, dc, ts0:ts0 + TS], dvt[:, dc:dc + 1],
                    y_t[:], op0=ALU.mult, op1=ALU.add))
                pr2 = pIN[wpar]
                for c in range(8):
                    P.emit("tensor", lambda e, c=c, pr2=pr2, wpar=wpar, ts0=ts0:
                           e.matmul(pr2[:], wr_t[wpar][:, c, :],
                                    xfm[:, c, ts0:ts0 + TS],
                                    start=(c == 0), stop=(c == 7)),
                           waits=[(s_wr[wpar], wr_ok[wpar]), (s_act, s_act.val),
                                  (s_dve, s_dve.val)]
                           if c == 0 else (),
                           inc=s_pe if c == 7 else None)
                wr_used[wpar] = s_pe.val
                P.emit("scalar", lambda e, dc=dc, pr2=pr2: e.activation(
                    sres_t[:], pr2[:], AF.Silu, bias=bint[:, 16 + dc:17 + dc]),
                    waits=[s_pe, (s_dve, s_dve.val)])
                P.emit("vector", lambda e, dc=dc: e.tensor_mul(
                    y3[:, dc, :], y2_t[:], sres_t[:]),
                    waits=[(s_pe, s_pe.val), (s_act, s_act.val)])
            y3_done = s_dve.val

            for ocol in range(2):
                P.dma("gpsimd", wo_t[:],
                      w_outT_d[:, ocol * 512:(ocol + 1) * 512].rearrange(
                          "(c p) o -> p c o", p=128),
                      waits=[(s_pe, s_pe.val)], inc=s_wo)
                wo_ok = s_wo.val
                for tsu in range(4):
                    pp = pIN[tsu % 2]
                    for c in range(16):
                        P.emit("tensor",
                               lambda e, tsu=tsu, c=c, pp=pp:
                               e.matmul(pp[:], y3[:, c, tsu * 128:(tsu + 1) * 128],
                                        wo_t[:, c, :],
                                        start=(c == 0), stop=False),
                               waits=[(s_dve, y3_done), (s_wo, wo_ok),
                                      (s_act, s_act.val),
                                      (s_dve, s_dve.val)]
                               if c == 0 else ())
                    P.emit("tensor", lambda e, ocol=ocol, pp=pp: e.matmul(
                        pp[:], ones1[:], boutr[0:1, ocol * 512:(ocol + 1) * 512],
                        start=False, stop=True),
                        waits=[(s_cst, cst_all)], inc=s_pe)
                    wp = tsu % 2
                    P.emit("vector", lambda e, pp=pp, wp=wp: e.tensor_copy(
                        wt2[wp][:], pp[:]),
                        waits=[s_pe, (s_wst[wp], s_wst[wp].val)])
                    r0 = t0 + su * TS + tsu * 128
                    P.emit("gpsimd", lambda e, wp=wp, r0=r0, ocol=ocol:
                           e.dma_start(
                               out=rout_d[r0:r0 + 128,
                                          ocol * 512:(ocol + 1) * 512],
                               in_=wt2[wp][:], accum_op=ALU.add),
                           waits=[(s_dve, s_dve.val),
                                  (s_rst[0], s_rst[0].val),
                                  (s_rst[1], s_rst[1].val)],
                           inc=s_wst[wp], n=16)

    fin = sb(f"{name}_fin", [1, 8], F32)
    P.emit("gpsimd", lambda e: e.memset(fin[:], 0.0),
           waits=[(s_wst[0], s_wst[0].val), (s_wst[1], s_wst[1].val),
                  (s_rst[0], s_rst[0].val), (s_rst[1], s_rst[1].val)])


def build_l0_attn_program(dbg_stop=None):
    nc = bass.Bass("TRN2", target_bir_lowering=False)
    rin_d = nc.dram_tensor("rin", [NTOK, DM], F32, kind="ExternalInput").ap()
    w_inT_d = nc.dram_tensor("winT", [DM, 4096], F32, kind="ExternalInput").ap()
    b_in_d = nc.dram_tensor("bin", [4096], F32, kind="ExternalInput").ap()
    wconv_d = nc.dram_tensor("wconv", [2048, 4], F32, kind="ExternalInput").ap()
    bconv_d = nc.dram_tensor("bconv", [2048], F32, kind="ExternalInput").ap()
    w_xT_d = nc.dram_tensor("wxT", [2048, 2080], F32, kind="ExternalInput").ap()
    b_x_d = nc.dram_tensor("bx", [2080], F32, kind="ExternalInput").ap()
    w_outT_d = nc.dram_tensor("woutT", [2048, DM], F32, kind="ExternalInput").ap()
    b_out_d = nc.dram_tensor("bout", [DM], F32, kind="ExternalInput").ap()
    alog_d = nc.dram_tensor("alog", [2048, 16], F32, kind="ExternalInput").ap()
    dvec_d = nc.dram_tensor("dvec", [2048], F32, kind="ExternalInput").ap()
    ident_d = nc.dram_tensor("ident", [128, 128], F32, kind="ExternalInput").ap()
    ones_d = nc.dram_tensor("ones", [1, 128], F32, kind="ExternalInput").ap()
    onehot_d = nc.dram_tensor("onehot", [16, 16, 128], F32,
                              kind="ExternalInput").ap()
    rout_d = nc.dram_tensor("rout", [NTOK, DM], F32, kind="ExternalOutput").ap()

    with ExitStack() as ctx:
        P = Prog(nc, ctx)
        block = ctx.enter_context(nc.Block())
        emit_l0_attn(P, rin_d, rout_d, w_inT_d, b_in_d, wconv_d, bconv_d,
                     w_xT_d, b_x_d, w_outT_d, b_out_d, alog_d, dvec_d,
                     ident_d, ones_d, onehot_d)
        P.finalize(block)
    return nc



def poolsel_chunks(sb):
    out = {}

    def add(chunk, row_in_chunk, col, w, span):
        m = out.setdefault(chunk, np.zeros((128, 40), np.float32))
        m[row_in_chunk:row_in_chunk + span, col] = w
    for r in range(16):
        pl = 16 * sb - 8 + r
        if pl < 0:
            continue
        add(pl * 8 // 128, (pl * 8) % 128, r, 1 / 8, 8)
    for r in range(8):
        pl = 8 * (sb - 1) + r
        if pl < 0:
            continue
        add(pl * 16 // 128, (pl * 16) % 128, 16 + r, 1 / 16, 16)
    for r in range(8):
        pl = 8 * (sb // 2 - 1) + r
        if pl < 0:
            continue
        add(pl * 32 // 128, (pl * 32) % 128, 24 + r, 1 / 32, 32)
    for r in range(8):
        pl = 8 * (sb // 4 - 1) + r
        if pl < 0:
            continue
        c0 = pl * 64 // 128
        add(c0, (pl * 64) % 128, 32 + r, 1 / 64, 64)
    return out


def build_l1_attn_consts():
    slots = []
    for sb in range(16):
        slots.append(sorted(poolsel_chunks(sb).items()))
    nslot = max(len(s) for s in slots)
    psel = np.zeros((16, nslot, 128, 40), np.float32)
    for sb, items in enumerate(slots):
        for i, (cu, m) in enumerate(items):
            psel[sb, i] = m
    mask = np.full((128, 232), -1e30, np.float32)
    for q in range(128):
        mask[q, q + 1:q + 65] = 0.0
        if q < 64:
            mask[q, 192:200] = np.log(8.0)
        else:
            mask[q, 200:208] = np.log(8.0)
    mask[:, 208:216] = np.log(16.0)
    mask[:, 216:224] = np.log(32.0)
    mask[:, 224:232] = np.log(64.0)
    return psel, mask


def emit_l1_attn(P, rin_d, rout_d, watn_d, batn_d, wproj_d, bproj_d, ident_d,
                 ones_d, psel_d, mask_d, nslot, name="t1"):
    nc = P.nc
    sb_, ps = P.sb, P.ps
    s_act, s_dve = P.sa, P.sv
    DH = 64

    ones1 = sb_(f"{name}_ones", [1, 128], BF16)
    idt = sb_(f"{name}_idt", [128, 128], BF16)
    mask = sb_(f"{name}_mask", [128, 232], F32)
    psel = sb_(f"{name}_psel", [128, 16, nslot, 40], BF16)
    batn = sb_(f"{name}_batn", [128, 24], F32)
    batq = sb_(f"{name}_batq", [128, 8], F32)
    bprr = sb_(f"{name}_bpr", [1, DM], BF16)
    eps = sb_(f"{name}_eps", [128, 1], F32)

    qt = sb_(f"{name}_qt", [128, 8, SEQ], BF16)
    kt = sb_(f"{name}_kt", [128, 8, 64 + SEQ + 64], BF16)
    vtm = sb_(f"{name}_vtm", [128, 8, 16, 128], BF16)
    kp = sb_(f"{name}_kp", [128, 8, 512], BF16)
    otm = sb_(f"{name}_otm", [128, 16, DM], BF16)
    wat = [sb_(f"{name}_wat{i}", [128, 8, 128], BF16) for i in range(2)]
    wpj = sb_(f"{name}_wpj", [128, 8, DM], BF16)
    rin2 = [sb_(f"{name}_rin{i}", [128, DM], F32) for i in range(2)]
    x3p = [sb_(f"{name}_x3p{i}", [128, DM], BF16) for i in range(2)]
    x3fm = sb_(f"{name}_x3fm", [128, 8, 512], BF16)
    sq = sb_(f"{name}_sq", [128, DM], BF16)
    stat = sb_(f"{name}_stat", [128, 8], F32)
    vfm = sb_(f"{name}_vfm", [128, 512], BF16)
    prt = sb_(f"{name}_prt", [128, 256], F32)
    Sm = [sb_(f"{name}_Sm{i}", [128, 232], F32) for i in range(2)]
    Pp = [sb_(f"{name}_Pp{i}", [128, 232], BF16) for i in range(2)]
    pta = [sb_(f"{name}_pta{i}", [64, 128], BF16) for i in range(2)]
    ptb = [sb_(f"{name}_ptb{i}", [64, 128], BF16) for i in range(2)]
    ptc = [sb_(f"{name}_ptc{i}", [64, 128], BF16) for i in range(2)]
    pt3s = [sb_(f"{name}_pt3s{i}", [40, 128], BF16) for i in range(2)]
    vca = [sb_(f"{name}_vca{i}", [64, 64], BF16) for i in range(2)]
    vcb = [sb_(f"{name}_vcb{i}", [64, 64], BF16) for i in range(2)]
    vps = [sb_(f"{name}_vps{i}", [40, 64], BF16) for i in range(2)]
    ofm = sb_(f"{name}_ofm", [128, 8, 128], BF16)
    wt2 = [sb_(f"{name}_wt2{i}", [128, 512], F32) for i in range(2)]

    pS = [ps(f"{name}_pS{i}", [128, 512]) for i in range(2)]
    pO = [ps(f"{name}_pO{i}", [128, 512]) for i in range(2)]
    pPool = [ps(f"{name}_pPool{i}", [40, 64]) for i in range(2)]
    pT = [ps(f"{name}_pT{i}", [128, 512], BF16) for i in range(2)]

    s_cst = P.sem(f"{name}_cst")
    s_ldx = [P.sem(f"{name}_ldx{i}") for i in range(2)]
    s_rst = [P.sem(f"{name}_rst{i}") for i in range(2)]
    s_wa = [P.sem(f"{name}_wa{i}") for i in range(2)]
    s_wp = P.sem(f"{name}_wp")
    s_wst = [P.sem(f"{name}_wst{i}") for i in range(2)]
    s_pe = P.sem(f"{name}_pe")
    s_pool = P.sem(f"{name}_pool")

    P.dma("gpsimd", ones1[:], ones_d, inc=s_cst)
    P.dma("gpsimd", idt[:], ident_d, inc=s_cst)
    P.dma("gpsimd", mask[:], mask_d, inc=s_cst)
    P.dma("gpsimd", psel[:], psel_d, inc=s_cst)
    P.dma("gpsimd", batn[:], batn_d.rearrange("(c p) -> p c", p=128),
          inc=s_cst, slow=True)
    P.dma("gpsimd", bprr[:], bproj_d.rearrange("(o d) -> o d", o=1), inc=s_cst)
    cst_all = s_cst.val
    P.emit("vector", lambda e: e.memset(eps[:], 1e-5))
    P.emit("vector", lambda e: e.tensor_scalar(
        batq[:], batn[:, 0:8], 0.125, None, op0=ALU.mult),
        waits=[(s_cst, cst_all)])

    rin_free = [[], []]
    for sq_i in range(2):
        t0 = sq_i * SEQ
        P.emit("vector", lambda e: e.memset(kt[:, :, 0:64], 0.0),
               waits=[(s_pe, s_pe.val)])
        P.emit("vector", lambda e: e.memset(kt[:, :, 64 + SEQ:], 0.0))
        P.emit("vector", lambda e: e.memset(kp[:, :, 0:8], 0.0))
        P.emit("vector", lambda e: e.memset(kp[:, :, 264:272], 0.0))
        P.emit("vector", lambda e: e.memset(kp[:, :, 400:408], 0.0))
        P.emit("vector", lambda e: e.memset(kp[:, :, 472:480], 0.0))
        zero_done = s_dve.val

        for ch in range(4):
            c0 = t0 + ch * 512
            for s in range(4):
                par = s % 2
                r_c = rin2[par]
                P.dma("sync", r_c[:], rin_d[c0 + s * 128:c0 + (s + 1) * 128, :],
                      waits=[(s_rst[par], s_rst[par].val)] + rin_free[par],
                      inc=s_ldx[par])
                ld_done = s_ldx[par].val
                P.dma("gpsimd", rout_d[c0 + s * 128:c0 + (s + 1) * 128, :],
                      r_c[:], waits=[(s_ldx[par], ld_done)], inc=s_rst[par])
                P.emit("vector", lambda e, r_c=r_c: e.reduce_sum(
                    stat[:, 0:1], r_c[:], axis=AX.X),
                    waits=[(s_ldx[par], ld_done)])
                P.emit("scalar", lambda e, r_c=r_c: e.activation(
                    sq[:], r_c[:], AF.Square, accum_out=stat[:, 1:2]),
                    waits=[(s_ldx[par], ld_done), (s_dve, s_dve.val)])
                P.emit("vector", lambda e: e.tensor_scalar(
                    stat[:, 2:3], stat[:, 0:1], 1.0 / DM, None, op0=ALU.mult))
                P.emit("vector", lambda e: e.tensor_mul(
                    stat[:, 3:4], stat[:, 2:3], stat[:, 2:3]))
                P.emit("vector", lambda e: e.tensor_scalar(
                    stat[:, 4:5], stat[:, 1:2], 1.0 / DM, None, op0=ALU.mult),
                    waits=[s_act])
                P.emit("vector", lambda e: e.tensor_sub(
                    stat[:, 4:5], stat[:, 4:5], stat[:, 3:4]))
                P.emit("scalar", lambda e: e.activation(
                    stat[:, 5:6], stat[:, 4:5], AF.Sqrt, bias=eps[:, 0:1]),
                    waits=[s_dve])
                P.emit("vector", lambda e: e.reciprocal(
                    stat[:, 6:7], stat[:, 5:6]), waits=[s_act])
                P.emit("vector", lambda e, r_c=r_c, par=par: e.tensor_scalar(
                    x3p[par][:], r_c[:], stat[:, 2:3], stat[:, 6:7],
                    op0=ALU.subtract, op1=ALU.mult),
                    waits=[(s_pe, s_pe.val)])
                x3_done = s_dve.val
                rin_free[par] = [(s_dve, x3_done), (s_act, s_act.val)]
                for c in range(8):
                    tp_ = pT[c % 2]
                    P.emit("tensor", lambda e, par=par, c=c, tp_=tp_: e.transpose(
                        tp_[:, 0:128], x3p[par][:, c * 128:(c + 1) * 128], idt[:]),
                        waits=[(s_dve, x3_done), (s_cst, cst_all),
                               (s_act, s_act.val)],
                        inc=s_pe)
                    P.emit("scalar", lambda e, s=s, c=c, tp_=tp_: e.copy(
                        x3fm[:, c, s * 128:(s + 1) * 128], tp_[:, 0:128]),
                        waits=[s_pe])
            x3_ok = s_act.val
            wa_used = [0, 0]
            for o in range(24):
                wpar = o % 2
                P.dma("gpsimd", wat[wpar][:],
                      watn_d[:, o * 128:(o + 1) * 128].rearrange(
                          "(c p) o -> p c o", p=128),
                      waits=[(s_pe, wa_used[wpar])], inc=s_wa[wpar])
                wa_ok = s_wa[wpar].val
                pp = pS[o % 2]
                for c in range(8):
                    P.emit("tensor", lambda e, o=o, c=c, pp=pp, wpar=wpar:
                           e.matmul(pp[:], wat[wpar][:, c, :], x3fm[:, c, :],
                                    start=(c == 0), stop=(c == 7)),
                           waits=[(s_wa[wpar], wa_ok), (s_act, x3_ok),
                                  (s_act, s_act.val), (s_dve, s_dve.val)]
                           if c == 0 else (),
                           inc=s_pe if c == 7 else None)
                wa_used[wpar] = s_pe.val
                if o < 8:
                    P.emit("scalar", lambda e, o=o, ch=ch, pp=pp: e.activation(
                        qt[:, o, ch * 512:(ch + 1) * 512], pp[:], AF.Identity,
                        bias=batq[:, o:o + 1], scale=0.125),
                        waits=[s_pe, (s_dve, s_dve.val)])
                elif o < 16:
                    P.emit("scalar", lambda e, o=o, ch=ch, pp=pp: e.activation(
                        kt[:, o - 8, 64 + ch * 512:64 + (ch + 1) * 512], pp[:],
                        AF.Identity, bias=batn[:, o:o + 1]),
                        waits=[s_pe, (s_dve, zero_done), (s_dve, s_dve.val)])
                else:
                    hp = o - 16
                    P.emit("scalar", lambda e, o=o, pp=pp: e.activation(
                        vfm[:], pp[:], AF.Identity, bias=batn[:, o:o + 1]),
                        waits=[s_pe, (s_dve, s_dve.val)])
                    for b in range(4):
                        su = ch * 4 + b
                        tp_ = pT[b % 2]
                        P.emit("tensor", lambda e, b=b, tp_=tp_: e.transpose(
                            tp_[:, 0:128], vfm[:, b * 128:(b + 1) * 128], idt[:]),
                            waits=[(s_act, s_act.val), (s_dve, s_dve.val)],
                            inc=s_pe)
                        P.emit("scalar", lambda e, hp=hp, su=su, tp_=tp_: e.copy(
                            vtm[:, hp, su, :], tp_[:, 0:128]), waits=[s_pe])
        pk_done0 = s_dve.val
        for hp in range(8):
            kv = kt[:, hp, 64:64 + SEQ]
            P.emit("vector", lambda e, kv=kv: e.tensor_reduce(
                prt[:, 0:256, None], kv.rearrange("p (a b) -> p a b", b=8),
                axis=AX.X, op=ALU.add), waits=[(s_act, s_act.val)])
            P.emit("vector", lambda e, hp=hp: e.tensor_scalar(
                kp[:, hp, 8:264], prt[:, 0:256], 1 / 8.0, None,
                op0=ALU.mult))
            P.emit("vector", lambda e, hp=hp: e.tensor_reduce(
                prt[:, 0:128, None],
                kp[:, hp, 8:264].rearrange("p (a b) -> p a b", b=2),
                axis=AX.X, op=ALU.add))
            P.emit("vector", lambda e, hp=hp: e.tensor_scalar(
                kp[:, hp, 272:400], prt[:, 0:128], 0.5, None,
                op0=ALU.mult))
            P.emit("vector", lambda e, hp=hp: e.tensor_reduce(
                prt[:, 0:64, None],
                kp[:, hp, 272:400].rearrange("p (a b) -> p a b", b=2),
                axis=AX.X, op=ALU.add))
            P.emit("vector", lambda e, hp=hp: e.tensor_scalar(
                kp[:, hp, 408:472], prt[:, 0:64], 0.5, None,
                op0=ALU.mult))
            P.emit("vector", lambda e, hp=hp: e.tensor_reduce(
                prt[:, 0:32, None],
                kp[:, hp, 408:472].rearrange("p (a b) -> p a b", b=2),
                axis=AX.X, op=ALU.add))
            P.emit("vector", lambda e, hp=hp: e.tensor_scalar(
                kp[:, hp, 480:512], prt[:, 0:32], 0.5, None,
                op0=ALU.mult))
        kp_done = s_dve.val

        pS_free = [[], []]
        pO_free = [[], []]
        pPool_free = [[], []]
        pT_free = [[], []]
        vc_free = [[], []]
        sm_free = [[], []]
        for hp in range(8):
            for hd in range(2):
                hb = hd * 64
                for sb in range(16):
                    ip = sb % 2
                    pp = pS[ip]
                    q_l = qt[hb:hb + 64, hp, sb * 128:(sb + 1) * 128]
                    P.emit("tensor", lambda e, pp=pp, q_l=q_l, hp=hp, hb=hb,
                           sb=sb: e.matmul(
                               pp[:, 0:192], q_l,
                               kt[hb:hb + 64, hp, sb * 128:sb * 128 + 192],
                               start=True, stop=True),
                           waits=[(s_dve, kp_done)] + pS_free[ip])
                    P.emit("tensor", lambda e, pp=pp, q_l=q_l, hp=hp, hb=hb,
                           sb=sb: e.matmul(
                               pp[:, 192:208], q_l,
                               kp[hb:hb + 64, hp, 16 * sb:16 * sb + 16],
                               start=True, stop=True))
                    P.emit("tensor", lambda e, pp=pp, q_l=q_l, hp=hp, hb=hb,
                           sb=sb: e.matmul(
                               pp[:, 208:216], q_l,
                               kp[hb:hb + 64, hp, 264 + 8 * sb:272 + 8 * sb],
                               start=True, stop=True))
                    P.emit("tensor", lambda e, pp=pp, q_l=q_l, hp=hp, hb=hb,
                           sb=sb: e.matmul(
                               pp[:, 216:224], q_l,
                               kp[hb:hb + 64, hp,
                                  400 + 8 * (sb // 2):408 + 8 * (sb // 2)],
                               start=True, stop=True))
                    P.emit("tensor", lambda e, pp=pp, q_l=q_l, hp=hp, hb=hb,
                           sb=sb: e.matmul(
                               pp[:, 224:232], q_l,
                               kp[hb:hb + 64, hp,
                                  472 + 8 * (sb // 4):480 + 8 * (sb // 4)],
                               start=True, stop=True), inc=s_pe)
                    s_done = s_pe.val
                    items = sorted(poolsel_chunks(sb).items())
                    ni = len(items)
                    ppl = pPool[ip]
                    for i, (cu, _m) in enumerate(items):
                        P.emit("tensor", lambda e, sb=sb, i=i, cu=cu, hp=hp,
                               hb=hb, ni=ni, ppl=ppl: e.matmul(
                                   ppl[:], psel[:, sb, i, :],
                                   vtm[:, hp, cu, hb:hb + 64],
                                   start=(i == 0), stop=(i == ni - 1)),
                               waits=pPool_free[ip] if i == 0 else (),
                               inc=s_pe if i == ni - 1 else None)
                    P.emit("scalar", lambda e, ppl=ppl, ip=ip: e.copy(
                        vps[ip][:], ppl[:]), waits=[s_pe] + vc_free[ip])
                    pPool_free[ip] = [(s_act, s_act.val)]
                    smt = Sm[ip]
                    P.emit("vector", lambda e, smt=smt, pp=pp: e.tensor_add(
                        smt[:], pp[:, 0:232], mask[:]),
                        waits=[(s_pe, s_done)] + sm_free[ip])
                    pS_free[ip] = [(s_dve, s_dve.val)]
                    P.emit("vector", lambda e, smt=smt: e.reduce_max(
                        stat[:, 0:1], smt[:], axis=AX.X))
                    P.emit("vector", lambda e: e.tensor_scalar(
                        stat[:, 1:2], stat[:, 0:1], -1.0, None, op0=ALU.mult),
                        waits=[(s_act, s_act.val)])
                    P.emit("scalar", lambda e, smt=smt: e.activation(
                        smt[:], smt[:], AF.Exp, bias=stat[:, 1:2]),
                        waits=[s_dve])
                    P.emit("vector", lambda e, smt=smt: e.reduce_sum(
                        stat[:, 2:3], smt[:], axis=AX.X), waits=[s_act])
                    P.emit("vector", lambda e: e.reciprocal(
                        stat[:, 3:4], stat[:, 2:3]))
                    ppt = Pp[ip]
                    P.emit("vector", lambda e, smt=smt, ppt=ppt: e.tensor_scalar(
                        ppt[:], smt[:], stat[:, 3:4], None, op0=ALU.mult),
                        waits=pT_free[ip])
                    sm_free[ip] = [(s_dve, s_dve.val)]
                    p_done = s_dve.val
                    tp_ = pT[ip]
                    P.emit("tensor", lambda e, ppt=ppt, tp_=tp_: e.transpose(
                        tp_[0:64, 0:128], ppt[:, 0:64], idt[:]),
                        waits=[(s_dve, p_done)] + pT_free[ip], inc=s_pe)
                    P.emit("tensor", lambda e, ppt=ppt, tp_=tp_: e.transpose(
                        tp_[0:64, 128:256], ppt[:, 64:128], idt[:]), inc=s_pe)
                    P.emit("tensor", lambda e, ppt=ppt, tp_=tp_: e.transpose(
                        tp_[0:64, 256:384], ppt[:, 128:192], idt[:]), inc=s_pe)
                    P.emit("tensor", lambda e, ppt=ppt, tp_=tp_: e.transpose(
                        tp_[0:40, 384:512], ppt[:, 192:232], idt[:]), inc=s_pe)
                    tr_pe = s_pe.val
                    P.emit("scalar", lambda e, ip=ip, tp_=tp_: e.copy(
                        pta[ip][:], tp_[0:64, 0:128]),
                        waits=[(s_pe, tr_pe)])
                    P.emit("scalar", lambda e, ip=ip, tp_=tp_: e.copy(
                        ptb[ip][:], tp_[0:64, 128:256]))
                    P.emit("scalar", lambda e, ip=ip, tp_=tp_: e.copy(
                        ptc[ip][:], tp_[0:64, 256:384]))
                    P.emit("scalar", lambda e, ip=ip, tp_=tp_: e.copy(
                        pt3s[ip][:], tp_[0:40, 384:512]))
                    tr_ok = s_act.val
                    if sb > 0:
                        P.emit("vector", lambda e, hp=hp, hb=hb, sb=sb, ip=ip:
                               e.tensor_copy(vca[ip][:],
                                             vtm[64:128, hp, sb - 1,
                                                 hb:hb + 64]),
                               waits=pO_free[ip])
                    P.emit("vector", lambda e, hp=hp, hb=hb, sb=sb, ip=ip:
                           e.tensor_copy(vcb[ip][:], vtm[64:128, hp, sb,
                                                         hb:hb + 64]),
                           waits=pO_free[ip])
                    vc_ok = s_dve.val
                    vc_free[ip] = [(s_dve, vc_ok)]
                    po = pO[ip]
                    first = True
                    if sb > 0:
                        P.emit("tensor", lambda e, po=po, ip=ip:
                               e.matmul(po[:, 0:64], pta[ip][:], vca[ip][:],
                                        start=True, stop=False),
                               waits=[(s_act, tr_ok), (s_dve, vc_ok)] +
                               pO_free[ip])
                        first = False
                    P.emit("tensor", lambda e, po=po, hp=hp, hb=hb, sb=sb,
                           ip=ip, first=first: e.matmul(
                               po[:, 0:64], ptb[ip][:],
                               vtm[0:64, hp, sb, hb:hb + 64],
                               start=first, stop=False),
                           waits=[(s_act, tr_ok), (s_dve, vc_ok)] +
                           pO_free[ip] if first else ())
                    P.emit("tensor", lambda e, po=po, ip=ip:
                           e.matmul(po[:, 0:64], ptc[ip][:], vcb[ip][:],
                                    start=False, stop=False))
                    P.emit("tensor", lambda e, po=po, ip=ip: e.matmul(
                        po[:, 0:64], pt3s[ip][:], vps[ip][:],
                        start=False, stop=True),
                        waits=[(s_act, s_act.val)], inc=s_pe)
                    o_pe = s_pe.val
                    P.emit("scalar", lambda e, po=po, sb=sb, hp=hp, hd=hd:
                           e.copy(otm[:, sb, (hp * 2 + hd) * 64:
                                      (hp * 2 + hd + 1) * 64], po[:, 0:64]),
                           waits=[(s_pe, o_pe)])
                    pO_free[ip] = [(s_act, s_act.val)]
                    pT_free[ip] = [(s_act, tr_ok)]
        at_done = s_act.val

        P.dma("gpsimd", wpj[:], wproj_d.rearrange("(c p) o -> p c o", p=128),
              waits=[(s_pe, s_pe.val)], inc=s_wp)
        wp_ok = s_wp.val
        for tsu in range(16):
            for c in range(8):
                tp_ = pT[c % 2]
                P.emit("tensor", lambda e, tsu=tsu, c=c, tp_=tp_: e.transpose(
                    tp_[:, 0:128], otm[:, tsu, c * 128:(c + 1) * 128], idt[:]),
                    waits=[(s_act, at_done), (s_act, s_act.val)], inc=s_pe)
                P.emit("scalar", lambda e, c=c, tp_=tp_: e.copy(
                    ofm[:, c, :], tp_[:, 0:128]), waits=[s_pe])
            ofm_ok = s_act.val
            for ocol in range(2):
                pp = pS[ocol]
                for c in range(8):
                    P.emit("tensor", lambda e, c=c, ocol=ocol, pp=pp: e.matmul(
                        pp[:], ofm[:, c, :],
                        wpj[:, c, ocol * 512:(ocol + 1) * 512],
                        start=(c == 0), stop=False),
                        waits=[(s_act, ofm_ok), (s_wp, wp_ok),
                               (s_dve, s_dve.val)] if c == 0 else ())
                P.emit("tensor", lambda e, ocol=ocol, pp=pp: e.matmul(
                    pp[:], ones1[:], bprr[0:1, ocol * 512:(ocol + 1) * 512],
                    start=False, stop=True),
                    waits=[(s_cst, cst_all)], inc=s_pe)
                wp2 = ocol
                P.emit("vector", lambda e, pp=pp, wp2=wp2: e.tensor_copy(
                    wt2[wp2][:], pp[:]),
                    waits=[s_pe, (s_wst[wp2], s_wst[wp2].val)])
                r0 = t0 + tsu * 128
                P.emit("gpsimd", lambda e, wp2=wp2, r0=r0, ocol=ocol:
                       e.dma_start(
                           out=rout_d[r0:r0 + 128,
                                      ocol * 512:(ocol + 1) * 512],
                           in_=wt2[wp2][:], accum_op=ALU.add),
                       waits=[(s_dve, s_dve.val),
                              (s_rst[0], s_rst[0].val),
                              (s_rst[1], s_rst[1].val)],
                       inc=s_wst[wp2], n=16)

    fin = sb_(f"{name}_fin", [1, 8], F32)
    P.emit("gpsimd", lambda e: e.memset(fin[:], 0.0),
           waits=[(s_wst[0], s_wst[0].val), (s_wst[1], s_wst[1].val),
                  (s_rst[0], s_rst[0].val), (s_rst[1], s_rst[1].val)])


def build_l1_attn_program(nslot):
    nc = bass.Bass("TRN2", target_bir_lowering=False)
    rin_d = nc.dram_tensor("rin", [NTOK, DM], F32, kind="ExternalInput").ap()
    watn_d = nc.dram_tensor("watn", [DM, 3072], F32, kind="ExternalInput").ap()
    batn_d = nc.dram_tensor("batn", [3072], F32, kind="ExternalInput").ap()
    wproj_d = nc.dram_tensor("wproj", [DM, DM], F32, kind="ExternalInput").ap()
    bproj_d = nc.dram_tensor("bproj", [DM], F32, kind="ExternalInput").ap()
    ident_d = nc.dram_tensor("ident", [128, 128], F32, kind="ExternalInput").ap()
    ones_d = nc.dram_tensor("ones", [1, 128], F32, kind="ExternalInput").ap()
    psel_d = nc.dram_tensor("psel", [128, 16, nslot, 40], F32,
                            kind="ExternalInput").ap()
    mask_d = nc.dram_tensor("maskc", [128, 232], F32, kind="ExternalInput").ap()
    rout_d = nc.dram_tensor("rout", [NTOK, DM], F32, kind="ExternalOutput").ap()

    with ExitStack() as ctx:
        P = Prog(nc, ctx)
        block = ctx.enter_context(nc.Block())
        emit_l1_attn(P, rin_d, rout_d, watn_d, batn_d, wproj_d, bproj_d,
                     ident_d, ones_d, psel_d, mask_d, nslot)
        P.finalize(block)
    return nc



def emit_l1_ffn(P, rin_d, rout_d, x4fm_d, wgcat_d, brg_d, w1_d, b1_d, w2_d,
                aug2_d, ident_d, ones_d, name="m1"):
    nc = P.nc
    CH = 512
    NCH = NTOK // CH
    SUB = CH // 128
    NE = 9
    sb, ps = P.sb, P.ps
    s_act, s_dve = P.sa, P.sv

    cwp = sb(f"{name}_cwp", [128, 32, NE], F32)
    ones1 = sb(f"{name}_ones", [1, 128], BF16)

    s_ldw = P.sem(f"{name}_ldw")
    s_ldx = [P.sem(f"{name}_ldx{i}") for i in range(2)]
    s_rst = [P.sem(f"{name}_rst{i}") for i in range(2)]
    s_x4st = [P.sem(f"{name}_x4st{i}") for i in range(2)]
    s_pe = P.sem(f"{name}_pe")

    P.dma("gpsimd", ones1[:], ones_d, inc=s_ldw)

    P.push_scope()
    rin2 = [sb(f"{name}_rin{i}", [128, SUB, DM], F32) for i in range(2)]
    x2p = sb(f"{name}_x2p", [128, SUB, DM], BF16)
    xf = sb(f"{name}_xf", [128, 8, CH], BF16)
    sq = sb(f"{name}_sq", [128, DM], F32)
    stat = sb(f"{name}_stat", [128, 8], F32)
    stat2 = sb(f"{name}_stat2", [128, 16], F32)
    tmp10 = sb(f"{name}_tmp10", [128, 10], F32)
    gpt = sb(f"{name}_gpt", [128, 8], F32)
    t8a = sb(f"{name}_t8a", [128, 8], F32)
    t8b = sb(f"{name}_t8b", [128, 8], F32)
    wgt = sb(f"{name}_wgt", [128, 8, 10], BF16)
    brgt = sb(f"{name}_brgt", [1, 10], BF16)
    idt = sb(f"{name}_idt", [128, 128], BF16)
    eps = sb(f"{name}_eps", [128, 1], F32)
    pt = [ps(f"{name}_pt{i}", [128, 128], BF16) for i in range(2)]
    prt = ps(f"{name}_prt", [128, 16])

    P.dma("gpsimd", wgt[:], wgcat_d.rearrange("(c p) g -> p c g", p=128), inc=s_ldw)
    P.dma("gpsimd", brgt[:], brg_d.rearrange("(o g) -> o g", o=1), inc=s_ldw)
    P.dma("gpsimd", idt[:], ident_d, inc=s_ldw)
    ldw_all = s_ldw.val
    P.emit("vector", lambda e: e.memset(eps[:], 1e-5))

    rin_free = [[], []]
    for ci in range(NCH):
        par = ci % 2
        r_c = rin2[par]
        rview = rin_d[ci * CH:(ci + 1) * CH, :].rearrange("(s p) d -> p s d", p=128)
        P.dma("sync", r_c[:], rview,
              waits=[(s_rst[par], s_rst[par].val)] + rin_free[par],
              inc=s_ldx[par])
        ld_done = s_ldx[par].val
        oview = rout_d[ci * CH:(ci + 1) * CH, :].rearrange("(s p) d -> p s d", p=128)
        P.dma("gpsimd", oview, r_c[:], waits=[(s_ldx[par], ld_done)],
              inc=s_rst[par])
        for s in range(SUB):
            row = r_c[:, s, :]
            P.emit("vector", lambda e, row=row: e.reduce_sum(
                stat[:, 0:1], row, axis=AX.X), waits=[(s_ldx[par], ld_done)])
            P.emit("scalar", lambda e, row=row: e.activation(
                sq[:], row, AF.Square, accum_out=stat[:, 1:2]),
                waits=[(s_ldx[par], ld_done), (s_dve, s_dve.val)])
            P.emit("vector", lambda e: e.tensor_scalar(
                stat[:, 2:3], stat[:, 0:1], 1.0 / DM, None, op0=ALU.mult))
            P.emit("vector", lambda e: e.tensor_mul(
                stat[:, 3:4], stat[:, 2:3], stat[:, 2:3]))
            P.emit("vector", lambda e: e.tensor_scalar(
                stat[:, 4:5], stat[:, 1:2], 1.0 / DM, None, op0=ALU.mult),
                waits=[s_act])
            P.emit("vector", lambda e: e.tensor_sub(
                stat[:, 4:5], stat[:, 4:5], stat[:, 3:4]))
            P.emit("scalar", lambda e: e.activation(
                stat[:, 5:6], stat[:, 4:5], AF.Sqrt, bias=eps[:, 0:1]),
                waits=[s_dve])
            P.emit("vector", lambda e: e.reciprocal(
                stat[:, 6:7], stat[:, 5:6]), waits=[s_act])
            P.emit("vector", lambda e, row=row, s=s: e.tensor_scalar(
                x2p[:, s, :], row, stat[:, 2:3], stat[:, 6:7],
                op0=ALU.subtract, op1=ALU.mult),
                waits=[(s_pe, s_pe.val)])
            x2_done = s_dve.val
            for c in range(8):
                tp = pt[c % 2]
                P.emit("tensor", lambda e, s=s, c=c, tp=tp: e.transpose(
                    tp[:], x2p[:, s, c * 128:(c + 1) * 128], idt[:]),
                    waits=[(s_dve, x2_done), (s_ldw, ldw_all),
                           (s_act, s_act.val)],
                    inc=s_pe)
                P.emit("scalar", lambda e, s=s, c=c, tp=tp: e.copy(
                    xf[:, c, s * 128:(s + 1) * 128], tp[:]),
                    waits=[s_pe, (s_x4st[0], s_x4st[0].val),
                           (s_x4st[1], s_x4st[1].val)])
        tr_done = s_act.val
        rin_free[par] = [(s_dve, s_dve.val), (s_act, s_act.val)]
        for s in range(SUB):
            for c in range(8):
                P.emit("tensor", lambda e, s=s, c=c: e.matmul(
                    prt[:, 0:10], xf[:, c, s * 128:(s + 1) * 128],
                    wgt[:, c, :], start=(c == 0), stop=False),
                    waits=[(s_act, s_act.val), (s_dve, s_dve.val)] if c == 0 else ())
            P.emit("tensor", lambda e: e.matmul(
                prt[:, 0:10], ones1[:], brgt[:], start=False, stop=True),
                inc=s_pe)
            g = ci * SUB + s
            P.emit("vector", lambda e: e.reduce_max(
                stat2[:, 0:1], prt[:, 0:2], axis=AX.X), waits=[s_pe])
            P.emit("vector", lambda e: e.tensor_scalar(
                stat2[:, 1:2], stat2[:, 0:1], -1.0, None, op0=ALU.mult))
            P.emit("scalar", lambda e: e.activation(
                tmp10[:, 0:2], prt[:, 0:2], AF.Exp, bias=stat2[:, 1:2]),
                waits=[s_dve, s_pe])
            P.emit("vector", lambda e: e.reduce_sum(
                stat2[:, 2:3], tmp10[:, 0:2], axis=AX.X), waits=[s_act])
            P.emit("vector", lambda e: e.reciprocal(stat2[:, 3:4], stat2[:, 2:3]))
            P.emit("vector", lambda e: e.reduce_max(
                stat2[:, 4:5], prt[:, 2:10], axis=AX.X))
            P.emit("vector", lambda e: e.tensor_scalar(
                stat2[:, 5:6], stat2[:, 4:5], -1.0, None, op0=ALU.mult))
            P.emit("scalar", lambda e: e.activation(
                tmp10[:, 2:10], prt[:, 2:10], AF.Exp, bias=stat2[:, 5:6]),
                waits=[s_dve])
            P.emit("vector", lambda e: e.reduce_sum(
                stat2[:, 6:7], tmp10[:, 2:10], axis=AX.X), waits=[s_act])
            P.emit("vector", lambda e: e.reciprocal(stat2[:, 7:8], stat2[:, 6:7]))
            P.emit("vector", lambda e: e.tensor_scalar(
                gpt[:], tmp10[:, 2:10], stat2[:, 7:8], None, op0=ALU.mult))
            P.emit("vector", lambda e: e.reduce_max(
                stat2[:, 8:9], gpt[:], axis=AX.X))
            P.emit("vector", lambda e: e.tensor_scalar(
                t8a[:], gpt[:], stat2[:, 8:9], -1e30,
                op0=ALU.is_equal, op1=ALU.mult))
            P.emit("vector", lambda e: e.tensor_add(t8a[:], t8a[:], gpt[:]))
            P.emit("vector", lambda e: e.reduce_max(
                stat2[:, 9:10], t8a[:], axis=AX.X))
            P.emit("vector", lambda e: e.tensor_add(
                stat2[:, 10:11], stat2[:, 8:9], stat2[:, 9:10]))
            P.emit("vector", lambda e: e.reciprocal(
                stat2[:, 11:12], stat2[:, 10:11]))
            P.emit("vector", lambda e: e.tensor_scalar(
                t8a[:], gpt[:], stat2[:, 8:9], None, op0=ALU.is_equal))
            P.emit("vector", lambda e: e.tensor_scalar(
                t8b[:], gpt[:], stat2[:, 9:10], None, op0=ALU.is_equal))
            P.emit("vector", lambda e: e.tensor_add(t8a[:], t8a[:], t8b[:]))
            P.emit("vector", lambda e: e.tensor_mul(t8a[:], t8a[:], gpt[:]))
            P.emit("vector", lambda e: e.tensor_scalar(
                t8a[:], t8a[:], stat2[:, 11:12], None, op0=ALU.mult))
            P.emit("vector", lambda e: e.tensor_scalar(
                t8a[:], t8a[:], stat2[:, 3:4], None, op0=ALU.mult))
            P.emit("vector", lambda e, g=g: e.tensor_scalar(
                cwp[:, g, 0:8], t8a[:], tmp10[:, 0:1], None, op0=ALU.mult))
            P.emit("vector", lambda e, g=g: e.tensor_scalar(
                cwp[:, g, 8:9], tmp10[:, 1:2], stat2[:, 3:4], None,
                op0=ALU.mult))
        xview = x4fm_d[:, ci, :, :]
        P.dma("gpsimd", xview, xf[:],
              waits=[(s_act, tr_done), (s_pe, s_pe.val)], inc=s_x4st[par])
    P.pop_scope()

    w1t = sb(f"{name}_w1t", [128, 8, 4096], BF16)
    w2t = sb(f"{name}_w2t", [128, 32, DM], BF16)
    b1t = [sb(f"{name}_b1t{i}", [128, 32], F32) for i in range(2)]
    aug2t = [sb(f"{name}_aug{i}", [1, DM], BF16) for i in range(2)]
    xb = [sb(f"{name}_xb{i}", [128, 8, CH], BF16) for i in range(2)]
    h_blk = sb(f"{name}_hblk", [128, 32, CH], BF16)
    wtile = [sb(f"{name}_wt{i}", [128, 512], F32) for i in range(2)]
    psA = [ps(f"{name}_psA{i}", [128, 512]) for i in range(2)]
    psB = [ps(f"{name}_psB{i}", [128, 512]) for i in range(2)]
    s_w1q = [P.sem(f"{name}_w1q{i}") for i in range(4)]
    s_w2 = P.sem(f"{name}_w2")
    s_bld = P.sem(f"{name}_bld")
    s_xb = [P.sem(f"{name}_xb{i}") for i in range(2)]
    s_wst = [P.sem(f"{name}_wst{i}") for i in range(2)]

    boundary = P.join(P.sv, P.sa, s_pe, s_x4st[0], s_x4st[1],
                      s_rst[0], s_rst[1], s_ldx[0], s_ldx[1])
    peA_end = 0
    peB_end = 0
    xbA_done = [0, 0]
    peA_q = [0, 0, 0, 0]
    for e_i in range(NE):
        for q in range(4):
            P.dma("gpsimd", w1t[:, :, q * 1024:(q + 1) * 1024],
                  w1_d[e_i].rearrange("(c p) f -> p c f", p=128)[
                      :, :, q * 1024:(q + 1) * 1024],
                  waits=boundary + [(s_pe, peA_q[q])], inc=s_w1q[q])
        P.dma("gpsimd", w2t[:],
              w2_d[e_i].rearrange("(f p) d -> p f d", p=128),
              waits=boundary + [(s_pe, peB_end)], inc=s_w2)
        P.dma("gpsimd", b1t[e_i % 2][:], b1_d[e_i],
              waits=boundary + [(s_act, s_act.val)], inc=s_bld)
        P.dma("gpsimd", aug2t[e_i % 2][:],
              aug2_d[e_i:e_i + 1, :],
              waits=boundary + [(s_pe, peB_end)], inc=s_bld)
        w1_ok = [s_w1q[q].val for q in range(4)]
        w2_ok = s_w2.val
        bld_ok = s_bld.val
        for tb in range(NCH):
            tp = tb % 2
            xview = x4fm_d[:, tb, :, :]
            P.dma("sync", xb[tp][:], xview,
                  waits=boundary + [(s_pe, xbA_done[tp])], inc=s_xb[tp])
            xb_ok = s_xb[tp].val
            for f in range(32):
                ph = psA[f % 2]
                for c in range(8):
                    P.emit("tensor", lambda e, f=f, c=c, ph=ph, tp=tp: e.matmul(
                        ph[:], w1t[:, c, f * 128:(f + 1) * 128],
                        xb[tp][:, c, :], start=(c == 0), stop=(c == 7)),
                        waits=[(s_xb[tp], xb_ok), (s_w1q[f // 8], w1_ok[f // 8]),
                               (s_act, s_act.val)] if c == 0 else (),
                        inc=s_pe if c == 7 else None)
                P.emit("scalar", lambda e, f=f, ph=ph, e_i=e_i: e.activation(
                    h_blk[:, f, :], ph[:], AF.Relu,
                    bias=b1t[e_i % 2][:, f:f + 1]),
                    waits=[s_pe, (s_bld, bld_ok)])
                if tb == NCH - 1 and f % 8 == 7:
                    peA_q[f // 8] = s_pe.val
            xbA_done[tp] = s_pe.val
            hB_ok = s_act.val
            for ts_i in range(SUB):
                for dc in range(2):
                    po = psB[(ts_i * 2 + dc) % 2]
                    for f in range(32):
                        P.emit("tensor",
                               lambda e, f=f, ts_i=ts_i, dc=dc, po=po: e.matmul(
                                   po[:], h_blk[:, f, ts_i * 128:(ts_i + 1) * 128],
                                   w2t[:, f, dc * 512:(dc + 1) * 512],
                                   start=(f == 0), stop=False),
                               waits=[(s_act, hB_ok), (s_w2, w2_ok),
                                      (s_dve, s_dve.val)] if f == 0 else ())
                    P.emit("tensor", lambda e, dc=dc, po=po, e_i=e_i: e.matmul(
                        po[:], ones1[:], aug2t[e_i % 2][0:1, dc * 512:(dc + 1) * 512],
                        start=False, stop=True),
                        waits=[(s_bld, bld_ok)], inc=s_pe)
                    wp = (ts_i * 2 + dc) % 2
                    g = tb * SUB + ts_i
                    P.emit("vector", lambda e, po=po, wp=wp, g=g, e_i=e_i:
                           e.tensor_scalar(wtile[wp][:], po[:],
                                           cwp[:, g, e_i:e_i + 1], None,
                                           op0=ALU.mult),
                           waits=[s_pe, (s_wst[wp], s_wst[wp].val)])
                    r0 = tb * CH + ts_i * 128
                    P.emit("gpsimd", lambda e, wp=wp, r0=r0, dc=dc: e.dma_start(
                        out=rout_d[r0:r0 + 128, dc * 512:(dc + 1) * 512],
                        in_=wtile[wp][:], accum_op=ALU.add),
                        waits=[(s_dve, s_dve.val)], inc=s_wst[wp], n=16)
            peB_end = s_pe.val

    fin = sb(f"{name}_fin", [1, 8], F32)
    P.emit("gpsimd", lambda e: e.memset(fin[:], 0.0),
           waits=[(s_wst[0], s_wst[0].val), (s_wst[1], s_wst[1].val),
                  (s_rst[0], s_rst[0].val), (s_rst[1], s_rst[1].val)])


def build_l1_ffn_program():
    nc = bass.Bass("TRN2", target_bir_lowering=False)
    rin_d = nc.dram_tensor("rin", [NTOK, DM], F32, kind="ExternalInput").ap()
    wgcat_d = nc.dram_tensor("wgcat", [DM, 10], F32, kind="ExternalInput").ap()
    brg_d = nc.dram_tensor("brg", [10], F32, kind="ExternalInput").ap()
    w1_d = nc.dram_tensor("w1", [9, DM, 4096], F32, kind="ExternalInput").ap()
    b1_d = nc.dram_tensor("b1", [9, 128, 32], F32, kind="ExternalInput").ap()
    w2_d = nc.dram_tensor("w2", [9, 4096, DM], F32, kind="ExternalInput").ap()
    aug2_d = nc.dram_tensor("aug2", [9, DM], F32, kind="ExternalInput").ap()
    ident_d = nc.dram_tensor("ident", [128, 128], F32, kind="ExternalInput").ap()
    ones_d = nc.dram_tensor("ones", [1, 128], F32, kind="ExternalInput").ap()
    rout_d = nc.dram_tensor("rout", [NTOK, DM], F32, kind="ExternalOutput").ap()
    x4fm_d = nc.dram_tensor("x4fm", [128, 8, 8, 512], BF16, kind="Internal").ap()

    with ExitStack() as ctx:
        P = Prog(nc, ctx)
        block = ctx.enter_context(nc.Block())
        emit_l1_ffn(P, rin_d, rout_d, x4fm_d, wgcat_d, brg_d, w1_d, b1_d,
                    w2_d, aug2_d, ident_d, ones_d)
        P.finalize(block)
    return nc


def build_l0_ffn_program():
    nc = bass.Bass("TRN2", target_bir_lowering=False)
    rin_d = nc.dram_tensor("rin", [NTOK, DM], F32, kind="ExternalInput").ap()
    wr_d = nc.dram_tensor("wrT", [DM, DM], F32, kind="ExternalInput").ap()
    wk_d = nc.dram_tensor("wkT", [DM, DM], F32, kind="ExternalInput").ap()
    wv_d = nc.dram_tensor("wvT", [DM, DM], F32, kind="ExternalInput").ap()
    mur_d = nc.dram_tensor("mur", [DM], F32, kind="ExternalInput").ap()
    muk_d = nc.dram_tensor("muk", [DM], F32, kind="ExternalInput").ap()
    ident_d = nc.dram_tensor("ident", [128, 128], F32, kind="ExternalInput").ap()
    rout_d = nc.dram_tensor("rout", [NTOK, DM], F32, kind="ExternalOutput").ap()

    with ExitStack() as ctx:
        P = Prog(nc, ctx)
        block = ctx.enter_context(nc.Block())
        emit_l0_ffn(P, rin_d, rout_d, wr_d, wk_d, wv_d, mur_d, muk_d, ident_d)
        P.finalize(block)
    return nc


def build_fused_program(nslot):
    nc = bass.Bass("TRN2", target_bir_lowering=False)
    F = F32
    xin = nc.dram_tensor("xin", [NTOK, DM], F, kind="ExternalInput").ap()
    out = nc.dram_tensor("out", [NTOK, DM], F, kind="ExternalOutput").ap()
    r1 = nc.dram_tensor("r1", [NTOK, DM], F, kind="Internal").ap()
    r2 = nc.dram_tensor("r2", [NTOK, DM], F, kind="Internal").ap()
    r3 = nc.dram_tensor("r3", [NTOK, DM], F, kind="Internal").ap()
    x4fm_d = nc.dram_tensor("x4fm", [128, 8, 8, 512], BF16, kind="Internal").ap()

    winT = nc.dram_tensor("winT", [DM, 4096], F, kind="ExternalInput").ap()
    bin_ = nc.dram_tensor("bin", [4096], F, kind="ExternalInput").ap()
    wconv = nc.dram_tensor("wconv", [2048, 4], F, kind="ExternalInput").ap()
    bconv = nc.dram_tensor("bconv", [2048], F, kind="ExternalInput").ap()
    wxT = nc.dram_tensor("wxT", [2048, 2080], F, kind="ExternalInput").ap()
    bx = nc.dram_tensor("bx", [2080], F, kind="ExternalInput").ap()
    woutT = nc.dram_tensor("woutT", [2048, DM], F, kind="ExternalInput").ap()
    bout = nc.dram_tensor("bout", [DM], F, kind="ExternalInput").ap()
    alog = nc.dram_tensor("alog", [2048, 16], F, kind="ExternalInput").ap()
    dvec = nc.dram_tensor("dvec", [2048], F, kind="ExternalInput").ap()
    wrT = nc.dram_tensor("wrT", [DM, DM], F, kind="ExternalInput").ap()
    wkT = nc.dram_tensor("wkT", [DM, DM], F, kind="ExternalInput").ap()
    wvT = nc.dram_tensor("wvT", [DM, DM], F, kind="ExternalInput").ap()
    mur = nc.dram_tensor("mur", [DM], F, kind="ExternalInput").ap()
    muk = nc.dram_tensor("muk", [DM], F, kind="ExternalInput").ap()
    watn = nc.dram_tensor("watn", [DM, 3072], F, kind="ExternalInput").ap()
    batn = nc.dram_tensor("batn", [3072], F, kind="ExternalInput").ap()
    wproj = nc.dram_tensor("wproj", [DM, DM], F, kind="ExternalInput").ap()
    bproj = nc.dram_tensor("bproj", [DM], F, kind="ExternalInput").ap()
    wgcat = nc.dram_tensor("wgcat", [DM, 10], F, kind="ExternalInput").ap()
    brg = nc.dram_tensor("brg", [10], F, kind="ExternalInput").ap()
    w1 = nc.dram_tensor("w1", [9, DM, 4096], F, kind="ExternalInput").ap()
    b1 = nc.dram_tensor("b1", [9, 4096], F, kind="ExternalInput").ap()
    w2 = nc.dram_tensor("w2", [9, 4096, DM], F, kind="ExternalInput").ap()
    aug2 = nc.dram_tensor("aug2", [9, DM], F, kind="ExternalInput").ap()
    ident = nc.dram_tensor("ident", [128, 128], F, kind="ExternalInput").ap()
    ones = nc.dram_tensor("ones", [1, 128], F, kind="ExternalInput").ap()
    onehot = nc.dram_tensor("onehot", [16, 16, 128], F, kind="ExternalInput").ap()
    psel = nc.dram_tensor("psel", [128, 16, nslot, 40], F,
                          kind="ExternalInput").ap()
    maskc = nc.dram_tensor("maskc", [128, 232], F, kind="ExternalInput").ap()

    with ExitStack() as ctx:
        P = Prog(nc, ctx)
        block = ctx.enter_context(nc.Block())
        P.push_scope()
        emit_l0_attn(P, xin, r1, winT, bin_, wconv, bconv, wxT, bx, woutT,
                     bout, alog, dvec, ident, ones, onehot)
        P.pop_scope()
        P.soft_barrier()
        P.push_scope()
        emit_l0_ffn(P, r1, r2, wrT, wkT, wvT, mur, muk, ident)
        P.pop_scope()
        P.soft_barrier()
        P.push_scope()
        emit_l1_attn(P, r2, r3, watn, batn, wproj, bproj, ident, ones,
                     psel, maskc, nslot)
        P.pop_scope()
        P.soft_barrier()
        P.push_scope()
        emit_l1_ffn(P, r3, out, x4fm_d, wgcat, brg, w1, b1, w2, aug2,
                    ident, ones)
        P.pop_scope()
        P.finalize(block)
    return nc



_CACHE = {}


def _program():
    if "prog" not in _CACHE:
        psel, maskc = build_l1_attn_consts()
        nslot = psel.shape[1]
        _CACHE["consts"] = (psel, maskc, nslot)
        _CACHE["prog"] = build_fused_program(nslot)
    return _CACHE["prog"], _CACHE["consts"]


def kernel(**ins):
    from concourse.bass_utils import run_bass_kernel_spmd
    prog, (psel, maskc, nslot) = _program()
    ncores = 8

    ident = np.eye(128, dtype=np.float32)
    ones = np.ones((1, 128), np.float32)
    onehot = np.zeros((16, 16, 128), np.float32)
    for n in range(16):
        onehot[n, n, :] = 1.0

    w = {
        "winT": np.ascontiguousarray(ins["L0_attn_W_in"].T),
        "bin": ins["L0_attn_b_in"],
        "wconv": np.ascontiguousarray(ins["L0_attn_W_conv"].reshape(2048, 4)),
        "bconv": ins["L0_attn_b_conv"],
        "wxT": np.ascontiguousarray(ins["L0_attn_W_x"].T),
        "bx": ins["L0_attn_b_x"],
        "woutT": np.ascontiguousarray(ins["L0_attn_W_out"].T),
        "bout": ins["L0_attn_b_out"],
        "alog": ins["L0_attn_A_log"],
        "dvec": ins["L0_attn_D"],
        "wrT": np.ascontiguousarray(ins["L0_ffn_W_r"].T),
        "wkT": np.ascontiguousarray(ins["L0_ffn_W_k"].T),
        "wvT": np.ascontiguousarray(ins["L0_ffn_W_v"].T),
        "mur": ins["L0_ffn_mu_r"], "muk": ins["L0_ffn_mu_k"],
        "watn": ins["L1_attn_W_attn"], "batn": ins["L1_attn_b_attn"],
        "wproj": ins["L1_attn_W_proj"], "bproj": ins["L1_attn_b_proj"],
        "wgcat": np.ascontiguousarray(
            np.concatenate([ins["L1_ffn_Wr"], ins["L1_ffn_Wg"]], axis=1)),
        "brg": np.concatenate([ins["L1_ffn_br"], ins["L1_ffn_bg"]]),
        "w1": np.ascontiguousarray(
            np.concatenate([ins["L1_ffn_W1"], ins["L1_ffn_D1"][None]], axis=0)),
        "b1": np.ascontiguousarray(
            np.concatenate([ins["L1_ffn_b1"], ins["L1_ffn_d1b"][None]],
                           axis=0).reshape(9, 32, 128).transpose(0, 2, 1)),
        "w2": np.ascontiguousarray(
            np.concatenate([ins["L1_ffn_W2"], ins["L1_ffn_D2"][None]], axis=0)),
        "aug2": np.ascontiguousarray(
            np.concatenate([ins["L1_ffn_b2"], ins["L1_ffn_d2b"][None]], axis=0)),
        "ident": ident, "ones": ones, "onehot": onehot,
        "psel": np.ascontiguousarray(psel.transpose(2, 0, 1, 3)),
        "maskc": maskc,
    }
    x = ins["L0_attn_x"]
    in_maps = [dict(w, xin=np.ascontiguousarray(
        x[2 * c:2 * c + 2].reshape(NTOK, DM))) for c in range(ncores)]
    res = run_bass_kernel_spmd(prog, in_maps, core_ids=list(range(ncores)))
    out = np.stack([r["out"].reshape(2, SEQ, DM) for r in res.results])
    return out.reshape(16, SEQ, DM).astype(np.float32)
